# version 10
# baseline (speedup 1.0000x reference)
import math
from contextlib import ExitStack

import numpy as np

import concourse.bass as bass
import concourse.mybir as mybir
from concourse.bass_utils import run_bass_kernel_spmd

F32 = mybir.dt.float32
BF16 = mybir.dt.bfloat16
AF = mybir.ActivationFunctionType
ALU = mybir.AluOpType
AX = mybir.AxisListType

S = 2048
D = 1024
H = 16
NE = 8
NTC = 16
NQT = 4
NDC = 8
EPS_LN = 1e-5
EPS_RMS = float(np.finfo(np.float32).eps)
ALPHA = 1.702
LIMIT = 7.0


class Sem:
    def __init__(self, stack, nc, name):
        self.h = stack.enter_context(nc.semaphore(name))
        self.n = 0
        self.name = name

    def inc(self, instr, k=1):
        instr.then_inc(self.h, k)
        self.n += k
        return self.n

    def dma(self, instr):
        return self.inc(instr, 16)


class Waiter:
    def __init__(self):
        self.seen = {}

    def wait(self, eng, sem, at=None):
        v = sem.n if at is None else at
        if v <= 0:
            return
        key = (id(eng), sem.name)
        if self.seen.get(key, -1) >= v:
            return
        eng.wait_ge(sem.h, v)
        self.seen[key] = v


def build_nc(stage=3):
    nc = bass.Bass("TRN2", target_bir_lowering=False)

    xin = nc.dram_tensor("x", [S, D], F32, kind="ExternalInput").ap()
    wqeT = nc.dram_tensor("wqeT", [D, 512], F32, kind="ExternalInput").ap()
    wqoT = nc.dram_tensor("wqoT", [D, 512], F32, kind="ExternalInput").ap()
    wkeT = nc.dram_tensor("wkeT", [D, 512], F32, kind="ExternalInput").ap()
    wkoT = nc.dram_tensor("wkoT", [D, 512], F32, kind="ExternalInput").ap()
    wvT = nc.dram_tensor("wvT", [D, D], F32, kind="ExternalInput").ap()
    woT = nc.dram_tensor("woT", [D, D], F32, kind="ExternalInput").ap()
    wob = nc.dram_tensor("wob", [1, D], F32, kind="ExternalInput").ap()
    cosT = nc.dram_tensor("cosT", [512, S], F32, kind="ExternalInput").ap()
    sinT = nc.dram_tensor("sinT", [512, S], F32, kind="ExternalInput").ap()
    nwr = nc.dram_tensor("nwr", [1, D], F32, kind="ExternalInput").ap()
    gwT = nc.dram_tensor("gwT", [D, NE], F32, kind="ExternalInput").ap()
    gbr = nc.dram_tensor("gbr", [1, NE], F32, kind="ExternalInput").ap()
    w1eT = nc.dram_tensor("w1eT", [NE, D, D], F32, kind="ExternalInput").ap()
    w1oT = nc.dram_tensor("w1oT", [NE, D, D], F32, kind="ExternalInput").ap()
    w2T = nc.dram_tensor("w2T", [NE, D, D], F32, kind="ExternalInput").ap()
    b1e3 = nc.dram_tensor("b1e3", [8, 128, NE], F32, kind="ExternalInput").ap()
    b1o3 = nc.dram_tensor("b1o3", [8, 128, NE], F32, kind="ExternalInput").ap()
    b2n = nc.dram_tensor("b2n", [NE, D], F32, kind="ExternalInput").ap()
    idf = nc.dram_tensor("idf", [128, 128], F32, kind="ExternalInput").ap()
    onesAB = nc.dram_tensor("onesAB", [128, 128], F32, kind="ExternalInput").ap()
    onesr = nc.dram_tensor("onesr", [1, 512], F32, kind="ExternalInput").ap()
    ohdiv = nc.dram_tensor("ohdiv", [NE, NE, 128], F32, kind="ExternalInput").ap()

    out = nc.dram_tensor("out", [S, D], F32, kind="ExternalOutput").ap()

    attin_d = nc.dram_tensor("attin_d", [NDC, 128, S], BF16, kind="Internal").ap()
    h_d = nc.dram_tensor("h_d", [NDC, 128, S], BF16, kind="Internal").ap()
    kro_d = nc.dram_tensor("kro_d", [NDC, 128, S], BF16, kind="Internal").ap()
    qro_d = nc.dram_tensor("qro_d", [NDC, 128, S], BF16, kind="Internal").ap()
    v_d = nc.dram_tensor("v_d", [H, NTC, 128, 65], BF16, kind="Internal").ap()

    T = nc.tensor
    V = nc.vector
    Sc = nc.scalar
    G = nc.gpsimd
    Sy = nc.sync
    W = Waiter()

    stk = ExitStack()
    with stk:
        LD = Sem(stk, nc, "LD")
        WGC = Sem(stk, nc, "WGC")
        PE = Sem(stk, nc, "PE")
        DV = Sem(stk, nc, "DV")
        AC = Sem(stk, nc, "AC")
        PL = Sem(stk, nc, "PL")
        ST = Sem(stk, nc, "ST")
        SH = [Sem(stk, nc, f"SH{i}") for i in range(2)]
        SA = [Sem(stk, nc, f"SA{i}") for i in range(2)]
        SKV = [Sem(stk, nc, f"SKV{i}") for i in range(2)]
        SW1 = Sem(stk, nc, "SW1")
        SW2 = [Sem(stk, nc, f"SW2{i}") for i in range(2)]
        STB = Sem(stk, nc, "STB")
        SATW = Sem(stk, nc, "SATW")
        SKST = [Sem(stk, nc, f"SKST{i}") for i in range(4)]
        SVS = [Sem(stk, nc, f"SVS{i}") for i in range(2)]
        SHD = Sem(stk, nc, "SHD")
        SAD = Sem(stk, nc, "SAD")
        ALL_SEMS = ([LD, WGC, PE, DV, AC, PL, SW1, STB, SATW, SHD, SAD]
                    + SH + SA + SKV + SW2 + SKST + SVS)

        def sdma(sem, q, out_ap, in_ap):
            W.wait(q, sem)
            sem.dma(q.dma_start(out_ap, in_ap))
            return sem.n

        x_sb = stk.enter_context(nc.sbuf_tensor("x_sb", [128, NTC * D], F32))
        xv = x_sb[:, :].rearrange("p (tc d) -> p tc d", tc=NTC)
        g_fm = stk.enter_context(nc.sbuf_tensor("g_fm", [NE, S], F32))
        nw_bc = stk.enter_context(nc.sbuf_tensor("nw_bc", [128, D], F32))
        b2_sb = stk.enter_context(nc.sbuf_tensor("b2_sb", [NE, D], F32))
        b1e_sb = stk.enter_context(nc.sbuf_tensor("b1e_sb", [128, 64], F32))
        b1o_sb = stk.enter_context(nc.sbuf_tensor("b1o_sb", [128, 64], F32))
        gb_sb = stk.enter_context(nc.sbuf_tensor("gb_sb", [1, NE], F32))
        wob_sb = stk.enter_context(nc.sbuf_tensor("wob_sb", [1, D], F32))
        ones_sb = stk.enter_context(nc.sbuf_tensor("ones_sb", [1, 128], F32))
        oAB_sb = stk.enter_context(nc.sbuf_tensor("oAB_sb", [128, 128], BF16))
        idf_sb = stk.enter_context(nc.sbuf_tensor("idf_sb", [128, 128], F32))
        idb_sb = stk.enter_context(nc.sbuf_tensor("idb_sb", [128, 128], BF16))
        oh_sb = stk.enter_context(nc.sbuf_tensor("oh_sb", [NE, NE * 128], F32))
        gwT_sb = stk.enter_context(nc.sbuf_tensor("gwT_sb", [128, NDC * NE], BF16))
        knmax = stk.enter_context(nc.sbuf_tensor("knmax", [1, 8], F32))
        arena = stk.enter_context(nc.sbuf_tensor("arena", [128, 24576], BF16))
        ar = arena[:, :]
        w1e_b = ar[:, 0:8192]
        w1o_b = ar[:, 8192:16384]
        w2_b = [ar[:, 16384:20480], ar[:, 20480:24576]]
        wqe_b = ar[:, 0:4096]
        wqo_b = ar[:, 4096:8192]
        wv_b = ar[:, 8192:16384]
        wo_t = ar[:, 16384:24576]
        Ebuf_v = ar[:, 8192:12288]
        kbuf_v = [ar[:, 12288:14336], ar[:, 14336:16384]]

        LD.dma(Sy.dma_start(x_sb[:, :].rearrange("p (tc d) -> p tc d", tc=NTC),
                            xin.rearrange("(tc p) d -> p tc d", p=128)))
        LD.dma(Sy.dma_start(b2_sb[:, :], b2n))
        LD.dma(Sy.dma_start(b1e_sb[:, :].rearrange("p (i e) -> p i e", e=NE),
                            b1e3.rearrange("i p e -> p i e")))
        LD.dma(Sy.dma_start(b1o_sb[:, :].rearrange("p (i e) -> p i e", e=NE),
                            b1o3.rearrange("i p e -> p i e")))
        LD.dma(Sy.dma_start(gb_sb[:, :], gbr))
        LD.dma(Sy.dma_start(wob_sb[:, :], wob))
        LD.dma(Sy.dma_start(ones_sb[:, :], onesr[0:1, 0:128]))
        LD.dma(Sy.dma_start(idf_sb[:, :], idf))
        LD.dma(Sy.dma_start(oh_sb[:, :].rearrange("p (e c) -> p e c", e=NE),
                            ohdiv.rearrange("e p c -> p e c")))
        ones_bf = stk.enter_context(nc.sbuf_tensor("ones_bf", [1, 64], BF16))
        WGC.dma(G.dma_start(ones_bf[:, :], onesr[0:1, 0:64]))
        WGC.dma(G.dma_start(oAB_sb[:, :], onesAB))
        WGC.dma(G.dma_start(idb_sb[:, :], idf))
        WGC.dma(G.dma_start(gwT_sb[:, :].rearrange("p (dc e) -> p dc e", e=NE),
                            gwT.rearrange("(dc p) e -> p dc e", p=128)))
        ld0, wg0 = LD.n, WGC.n

        with nc.psum_tensor("p_nw", [128, 512], F32) as p_nw, \
                nc.sbuf_tensor("nwr_sb", [1, D], F32) as nwr_sb:
            LD.dma(Sy.dma_start(nwr_sb[:, :], nwr))
            ld0 = LD.n
            for hlf in range(2):
                W.wait(T, LD, ld0)
                W.wait(T, AC)
                PE.inc(T.matmul(p_nw[:, :], ones_sb[0:1, 0:128],
                                nwr_sb[0:1, hlf * 512:(hlf + 1) * 512],
                                start=True, stop=True))
                W.wait(Sc, PE)
                AC.inc(Sc.copy(nw_bc[:, hlf * 512:(hlf + 1) * 512], p_nw[:, :]))

        def phase_barrier():
            for eng in (T, V, Sc, G, Sy):
                for sem in ALL_SEMS:
                    W.wait(eng, sem)

        def load_w3(sem, dst, src, guard=None):
            if guard is not None:
                W.wait(G, *guard)
            n = src.shape[-1]
            return sdma(sem, G, dst[:, :].rearrange("p (dc n) -> p dc n", n=n),
                        src.rearrange("(dc p) n -> p dc n", p=128))

        def stats_norm(kind, tag, ph):
            raw = ph.enter_context(nc.sbuf_tensor(f"raw{tag}", [128, NTC * 12], F32))
            agg = ph.enter_context(nc.sbuf_tensor(f"agg{tag}", [128, NTC * 4], F32))
            mt = ph.enter_context(nc.sbuf_tensor(f"m{tag}", [128, NTC], F32))
            at = ph.enter_context(nc.sbuf_tensor(f"a{tag}", [128, NTC], F32))
            r2 = ph.enter_context(nc.sbuf_tensor(f"r2{tag}", [128, NTC], F32))
            tmp = ph.enter_context(nc.sbuf_tensor(f"tmq{tag}", [128, NTC * 4], F32))
            W.wait(V, LD, ld0)
            W.wait(V, DV)
            W.wait(V, PL)
            for tc in range(NTC):
                V.bn_stats(raw[:, tc * 12:tc * 12 + 6], xv[:, tc, 0:512])
                ins = V.bn_stats(raw[:, tc * 12 + 6:tc * 12 + 12], xv[:, tc, 512:1024])
                if tc == NTC - 1:
                    DV.inc(ins)
            W.wait(V, DV)
            for tc in range(NTC):
                V.bn_aggr(agg[:, tc * 4:tc * 4 + 2], raw[:, tc * 12:tc * 12 + 6])
                ins = V.bn_aggr(agg[:, tc * 4 + 2:tc * 4 + 4],
                                raw[:, tc * 12 + 6:tc * 12 + 12])
                if tc == NTC - 1:
                    DV.inc(ins)
            W.wait(V, DV)
            av = agg[:, :].rearrange("p (tc k) -> p tc k", k=4)
            m1, v1, m2, v2 = av[:, :, 0], av[:, :, 1], av[:, :, 2], av[:, :, 3]
            tv = tmp[:, :].rearrange("p (k tc) -> p k tc", k=4)
            t0, t1, t2, t3 = tv[:, 0, :], tv[:, 1, :], tv[:, 2, :], tv[:, 3, :]
            V.tensor_tensor(t0, m1, m2, ALU.add)
            V.tensor_tensor(t1, m1, m2, ALU.subtract)
            DV.inc(V.tensor_tensor(t2, v1, v2, ALU.add))
            W.wait(V, DV)
            V.tensor_scalar(mt[:, :], t0, 0.5, None, ALU.mult)
            DV.inc(V.tensor_tensor(t3, t1, t1, ALU.mult))
            W.wait(V, DV)
            DV.inc(V.scalar_tensor_tensor(t0, t3, 0.5, t2, ALU.mult, ALU.add))
            W.wait(V, DV)
            DV.inc(V.tensor_scalar(t1, t0, 0.5, EPS_LN, ALU.mult, ALU.add))
            W.wait(Sc, DV)
            AC.inc(Sc.activation(t2, t1, AF.Ln))
            W.wait(Sc, AC)
            AC.inc(Sc.activation(at[:, :], t2, AF.Exp, scale=-0.5))
            if kind == "ffn":
                W.wait(V, AC)
                V.tensor_scalar(t0, t1, -EPS_LN, None, ALU.add)
                DV.inc(V.tensor_tensor(t2, at[:, :], at[:, :], ALU.mult))
                W.wait(V, DV)
                DV.inc(V.tensor_tensor(t3, t0, t2, ALU.mult))
                W.wait(V, DV)
                DV.inc(V.tensor_scalar(t3, t3, EPS_RMS, None, ALU.add))
                W.wait(Sc, DV)
                AC.inc(Sc.activation(t0, t3, AF.Ln))
                W.wait(Sc, AC)
                AC.inc(Sc.activation(r2[:, :], t0, AF.Exp, scale=-0.5))
            return mt, at, r2

        def ffn(inv):
            tag = f"f{inv}"
            load_w3(SW1, w1e_b, w1eT[0], guard=(PE, PE.n))
            w1mark0 = load_w3(SW1, w1o_b, w1oT[0])
            W.wait(G, SW2[0])
            mk = sdma(SW2[0], G,
                      w2_b[0][:, :].rearrange("p (i n) -> p i n", n=512),
                      w2T[0, :, 0:512].rearrange("(i p) n -> p i n", p=128))
            w2mark0 = (mk, 0)
            with ExitStack() as ph:
                phase_barrier()
                mt, at, r2 = stats_norm("ffn", tag, ph)
                u_sb = [ph.enter_context(nc.sbuf_tensor(f"u{tag}{i}", [128, D], F32))
                        for i in range(2)]
                h1_sb = [ph.enter_context(nc.sbuf_tensor(f"h1{tag}{i}", [128, D], F32))
                         for i in range(2)]
                htm = [ph.enter_context(nc.sbuf_tensor(f"htm{tag}{i}", [128, D], BF16))
                       for i in range(2)]
                gbuf = [ph.enter_context(nc.sbuf_tensor(f"gb2{tag}{i}", [128, D], BF16))
                        for i in range(2)]
                hstage = [ph.enter_context(nc.sbuf_tensor(f"hs{tag}{i}", [128, NDC * 128], BF16))
                          for i in range(2)]
                lg_sb = ph.enter_context(nc.sbuf_tensor(f"lg{tag}", [128, 8], F32))
                top8 = ph.enter_context(nc.sbuf_tensor(f"top{tag}", [128, 8], F32))
                gtmp = ph.enter_context(nc.sbuf_tensor(f"gq{tag}", [128, 24], F32))
                g_tm = ph.enter_context(nc.sbuf_tensor(f"gtm{tag}", [128, 8], F32))
                p_tr = [ph.enter_context(nc.psum_tensor(f"ptr{tag}{i}", [128, 128], BF16))
                        for i in range(2)]
                p_lg = ph.enter_context(nc.psum_tensor(f"plg{tag}", [128, 8], F32))
                p_gt = ph.enter_context(nc.psum_tensor(f"pgt{tag}", [8, 128], F32))

                W.wait(Sc, AC)
                sc_hd = [0, 0]
                dv_tr = [0, 0]
                shd_mark = {}
                pl_h = {}

                def emit_norm(tc):
                    i = tc % 2
                    W.wait(V, AC)
                    W.wait(V, DV, dv_tr[0])
                    W.wait(V, PL)
                    DV.inc(V.tensor_scalar(u_sb[i][:, :], xv[:, tc, :], mt[:, tc:tc + 1],
                                           at[:, tc:tc + 1], ALU.subtract, ALU.mult))
                    W.wait(V, DV)
                    DV.inc(V.tensor_scalar(h1_sb[i][:, :], u_sb[i][:, :],
                                           r2[:, tc:tc + 1], None, ALU.mult))
                    dv_h1 = DV.n
                    W.wait(G, DV, dv_h1)
                    PL.inc(G.tensor_tensor(htm[i][:, :], h1_sb[i][:, :], nw_bc[:, :],
                                           ALU.mult))
                    pl_h[tc] = PL.n
                    DV.inc(V.scalar_tensor_tensor(xv[:, tc, :], u_sb[i][:, :], 0.5,
                                                  xv[:, tc, :], ALU.mult, ALU.add))

                emit_norm(0)
                for tc in range(NTC):
                    if tc + 1 < NTC:
                        emit_norm(tc + 1)
                    i = tc % 2
                    hs = hstage[i]
                    W.wait(T, PL, pl_h[tc])
                    W.wait(T, WGC, wg0)
                    for dc in range(NDC):
                        pt = p_tr[dc % 2]
                        W.wait(T, DV, dv_tr[dc % 2])
                        PE.inc(T.transpose(pt[:, :], htm[i][:, dc * 128:(dc + 1) * 128],
                                           idb_sb[:, :]))
                        W.wait(V, PE)
                        W.wait(V, SHD, sc_hd[i])
                        DV.inc(V.tensor_copy(hs[:, dc * 128:(dc + 1) * 128], pt[:, :]))
                        dv_tr[dc % 2] = DV.n
                    dv_h = DV.n
                    W.wait(Sy, DV, dv_h)
                    sc_hd[i] = sdma(
                        SHD, Sy,
                        h_d[:, :, tc * 128:(tc + 1) * 128].rearrange("dc p f -> p dc f"),
                        hs[:, :].rearrange("p (dc f) -> p dc f", f=128))
                    shd_mark[tc] = sc_hd[i]

                gdma = {}
                g_guard = [0, 0]

                def fetch_g(tc, bi):
                    W.wait(Sy, PE, g_guard[bi])
                    W.wait(Sy, SHD, shd_mark[tc])
                    gdma[tc] = (sdma(SH[bi], Sy,
                                     gbuf[bi][:, :].rearrange("p (dc f) -> p dc f", f=128),
                                     h_d[:, :, tc * 128:(tc + 1) * 128].rearrange("dc p f -> p dc f")),
                                bi)

                fetch_g(0, 0)
                for tc in range(NTC):
                    if tc + 1 < NTC:
                        fetch_g(tc + 1, (tc + 1) % 2)
                    scn_g, gi = gdma[tc]
                    W.wait(T, SH[gi], scn_g)
                    W.wait(T, AC)
                    for dc in range(NDC):
                        T.matmul(p_lg[:, :], gbuf[gi][:, dc * 128:(dc + 1) * 128],
                                 gwT_sb[:, dc * NE:(dc + 1) * NE],
                                 start=(dc == 0), stop=False)
                    PE.inc(T.matmul(p_lg[:, :], ones_sb[0:1, 0:128], gb_sb[0:1, :],
                                    start=False, stop=True))
                    g_guard[gi] = PE.n
                    W.wait(Sc, PE)
                    W.wait(Sc, DV)
                    AC.inc(Sc.copy(lg_sb[:, :], p_lg[:, :]))
                    W.wait(V, AC)
                    DV.inc(V.max(top8[:, :], lg_sb[:, :]))
                    W.wait(V, DV)
                    DV.inc(V.tensor_tensor(gtmp[:, 0:1], top8[:, 0:1], top8[:, 1:2],
                                           ALU.subtract))
                    W.wait(Sc, DV)
                    AC.inc(Sc.activation(gtmp[:, 1:2], gtmp[:, 0:1], AF.Sigmoid))
                    W.wait(V, AC)
                    V.tensor_scalar(gtmp[:, 2:3], gtmp[:, 1:2], 0.5, None, ALU.mult)
                    DV.inc(V.tensor_scalar(gtmp[:, 3:4], gtmp[:, 1:2], -0.5, 0.5,
                                           ALU.mult, ALU.add))
                    V.tensor_scalar(gtmp[:, 8:16], lg_sb[:, :], top8[:, 0:1], None,
                                    ALU.is_equal)
                    DV.inc(V.tensor_scalar(gtmp[:, 16:24], lg_sb[:, :], top8[:, 1:2],
                                           None, ALU.is_equal))
                    W.wait(V, DV)
                    V.tensor_scalar(gtmp[:, 8:16], gtmp[:, 8:16], gtmp[:, 2:3], None,
                                    ALU.mult)
                    DV.inc(V.tensor_scalar(gtmp[:, 16:24], gtmp[:, 16:24], gtmp[:, 3:4],
                                           None, ALU.mult))
                    W.wait(V, DV)
                    DV.inc(V.tensor_tensor(g_tm[:, :], gtmp[:, 8:16], gtmp[:, 16:24],
                                           ALU.add))
                    W.wait(T, DV)
                    PE.inc(T.transpose(p_gt[:, :], g_tm[:, :], idf_sb[:, :]))
                    W.wait(Sc, PE)
                    AC.inc(Sc.copy(g_fm[:, tc * 128:(tc + 1) * 128], p_gt[:, :]))

            with ExitStack() as ph:
                phase_barrier()
                hbuf = [ph.enter_context(nc.sbuf_tensor(f"hb{tag}{i}", [128, NDC * 512], BF16))
                        for i in range(2)]
                acg = ph.enter_context(nc.sbuf_tensor(f"acg{tag}", [128, NDC * S], BF16))
                acgv = acg[:, :].rearrange("p (i t) -> p i t", i=NDC)
                v0 = [ph.enter_context(nc.sbuf_tensor(f"v0{tag}{i}", [128, 512], F32))
                      for i in range(2)]
                sl = [ph.enter_context(nc.sbuf_tensor(f"sl{tag}{i}", [128, 512], F32))
                      for i in range(2)]
                l1 = [ph.enter_context(nc.sbuf_tensor(f"l1{tag}{i}", [128, 512], F32))
                      for i in range(2)]
                l2s = ph.enter_context(nc.sbuf_tensor(f"l2{tag}", [128, 512], F32))
                l2 = [l2s, l2s]
                pp = [ph.enter_context(nc.sbuf_tensor(f"pq{tag}{i}", [128, 512], F32))
                      for i in range(2)]
                gbc = ph.enter_context(nc.sbuf_tensor(f"gbc{tag}", [128, 512], F32))
                p_t1e = [ph.enter_context(nc.psum_tensor(f"pt1e{tag}{i}", [128, 512], F32))
                         for i in range(2)]
                p_t1o = [ph.enter_context(nc.psum_tensor(f"pt1o{tag}{i}", [128, 512], F32))
                         for i in range(2)]
                p_gb = ph.enter_context(nc.psum_tensor(f"pgb{tag}", [128, 512], F32))
                p_m2 = [ph.enter_context(nc.psum_tensor(f"pm2{tag}{i}", [128, 512], F32))
                        for i in range(2)]

                w1mark = {}
                w2mark = {}
                w1_guard = [(PE, PE.n)]
                w2_guard = [(PE, PE.n), (PE, PE.n)]
                h_guard = [(PE, PE.n), (PE, PE.n)]

                def fetch_w1(e):
                    load_w3(SW1, w1e_b, w1eT[e], guard=w1_guard[0])
                    w1mark[e] = load_w3(SW1, w1o_b, w1oT[e])

                def fetch_w2(e, hlf, bi):
                    W.wait(G, *w2_guard[bi])
                    mk = sdma(SW2[bi], G,
                              w2_b[bi][:, :].rearrange("p (i n) -> p i n", n=512),
                              w2T[e, :, hlf * 512:(hlf + 1) * 512].rearrange(
                                  "(i p) n -> p i n", p=128))
                    w2mark[(e, hlf)] = (mk, bi)

                hdma = {}

                def fetch_h(e, tt, bi):
                    W.wait(Sy, *h_guard[bi])
                    W.wait(Sy, SHD)
                    mk = sdma(SH[bi], Sy,
                              hbuf[bi][:, :].rearrange("p (dc f) -> p dc f", f=512),
                              h_d[:, :, tt * 512:(tt + 1) * 512].rearrange("dc p f -> p dc f"))
                    hdma[(e, tt)] = (mk, bi)

                w1mark[0] = w1mark0
                w2mark[(0, 0)] = w2mark0
                fetch_h(0, 0, 0)
                dv_t1e = [0, 0]
                dv_t1o = [0, 0]
                ac_sl = [0, 0]
                dv_pp = [0, 0]
                pl_l2 = [0, 0]
                pl_acg = [0, 0]
                dv_gb_used = 0
                ac_gfm = AC.n
                for e in range(NE):
                    for tt in range(NQT):
                        if tt + 1 < NQT:
                            fetch_h(e, tt + 1, (tt + 1) % 2)
                        scn, hb = hdma[(e, tt)]
                        W.wait(T, AC, ac_gfm)
                        W.wait(T, AC)
                        PE.inc(T.matmul(p_gb[:, :], oh_sb[:, e * 128:(e + 1) * 128],
                                        g_fm[:, tt * 512:(tt + 1) * 512],
                                        start=True, stop=True))
                        pe_gb = PE.n
                        W.wait(Sc, PE, pe_gb)
                        W.wait(Sc, PL)
                        AC.inc(Sc.copy(gbc[:, :], p_gb[:, :]))
                        ac_gbc = AC.n
                        for ic in range(NDC):
                            bk = ic % 2
                            W.wait(T, SH[hb], scn)
                            W.wait(T, SW1, w1mark[e])
                            W.wait(T, DV, dv_t1e[bk])
                            for dc in range(NDC):
                                ins = T.matmul(p_t1e[bk][:, :],
                                               w1e_b[:, dc * D + ic * 128: dc * D + (ic + 1) * 128],
                                               hbuf[hb][:, dc * 512:(dc + 1) * 512],
                                               start=(dc == 0), stop=(dc == NDC - 1))
                            pe_e = PE.inc(ins)
                            W.wait(T, DV, dv_t1o[bk])
                            for dc in range(NDC):
                                ins = T.matmul(p_t1o[bk][:, :],
                                               w1o_b[:, dc * D + ic * 128: dc * D + (ic + 1) * 128],
                                               hbuf[hb][:, dc * 512:(dc + 1) * 512],
                                               start=(dc == 0), stop=(dc == NDC - 1))
                            pe_o = PE.inc(ins)
                            b1ec = b1e_sb[:, ic * 8 + e: ic * 8 + e + 1]
                            b1oc = b1o_sb[:, ic * 8 + e: ic * 8 + e + 1]
                            W.wait(V, PE, pe_e)
                            W.wait(V, AC, ac_sl[bk])
                            DV.inc(V.tensor_scalar(v0[bk][:, :], p_t1e[bk][:, :],
                                                   b1ec, LIMIT, ALU.add, ALU.min))
                            dv_v0 = dv_t1e[bk] = DV.n
                            W.wait(V, PE, pe_o)
                            DV.inc(V.tensor_scalar(l1[bk][:, :], p_t1o[bk][:, :],
                                                   b1oc, LIMIT, ALU.add, ALU.min))
                            dv_l1 = dv_t1o[bk] = DV.n
                            W.wait(Sc, DV, dv_v0)
                            W.wait(Sc, DV, dv_pp[bk])
                            AC.inc(Sc.activation(sl[bk][:, :], v0[bk][:, :],
                                                 AF.Silu, scale=ALPHA))
                            ac_sl[bk] = AC.n
                            W.wait(V, DV, dv_l1)
                            l2w = V.tensor_scalar(l2[bk][:, :], l1[bk][:, :],
                                                  -LIMIT, None, ALU.max)
                            pl_l2[bk] = DV.inc(l2w)
                            W.wait(V, DV, pl_l2[bk])
                            W.wait(V, AC, ac_sl[bk])
                            W.wait(V, PL, pl_acg[bk])
                            DV.inc(V.scalar_tensor_tensor(pp[bk][:, :], l2[bk][:, :],
                                                          1.0, sl[bk][:, :],
                                                          ALU.add, ALU.mult))
                            dv_pp[bk] = DV.n
                            W.wait(G, DV, dv_pp[bk])
                            W.wait(G, AC, ac_gbc)
                            PL.inc(G.tensor_tensor(acgv[:, ic, tt * 512:(tt + 1) * 512],
                                                   pp[bk][:, :], gbc[:, :], ALU.mult))
                            pl_acg[bk] = PL.n
                        h_guard[hb] = (PE, PE.n)
                    w1_guard[0] = (PE, PE.n)
                    fetch_w2(e, 1, 1 - w2mark[(e, 0)][1])
                    for hlf in range(2):
                        wgn, wb = w2mark[(e, hlf)]
                        if hlf == 1 and e + 1 < NE:
                            fetch_w1(e + 1)
                            fetch_w2(e + 1, 0, 1 - wb)
                            fetch_h(e + 1, 0, 0)
                        dv_m2 = [DV.n, DV.n]
                        W.wait(T, PL)
                        for tc in range(NTC):
                            pm = p_m2[tc % 2]
                            W.wait(T, SW2[wb], wgn)
                            W.wait(T, DV, dv_m2[tc % 2])
                            for ic in range(NDC):
                                ins = T.matmul(pm[:, :],
                                               acgv[:, ic, tc * 128:(tc + 1) * 128],
                                               w2_b[wb][:, ic * 512:(ic + 1) * 512],
                                               start=(ic == 0),
                                               stop=(ic == NDC - 1 and e != 0))
                            if e == 0:
                                ins = T.matmul(pm[:, :], g_fm[:, tc * 128:(tc + 1) * 128],
                                               b2_sb[:, hlf * 512:(hlf + 1) * 512],
                                               start=False, stop=True)
                            PE.inc(ins)
                            W.wait(V, PE)
                            DV.inc(V.tensor_tensor(xv[:, tc, hlf * 512:(hlf + 1) * 512],
                                                   pm[:, :],
                                                   xv[:, tc, hlf * 512:(hlf + 1) * 512],
                                                   ALU.add))
                            dv_m2[tc % 2] = DV.n
                        w2_guard[wb] = (PE, PE.n)

        def attention():
            with ExitStack() as ph:
                phase_barrier()
                mt, at, _ = stats_norm("attn", "a0", ph)
                utm = [ph.enter_context(nc.sbuf_tensor(f"utm{i}", [128, D], BF16))
                       for i in range(2)]
                astage = [ph.enter_context(nc.sbuf_tensor(f"astage{i}", [128, NDC * 128], BF16))
                          for i in range(2)]
                p_tr = [ph.enter_context(nc.psum_tensor(f"patr{i}", [128, 128], BF16))
                        for i in range(2)]
                W.wait(Sc, AC)
                sc_ad = [0, 0]
                dv_tr = [0, 0]
                dv_u = {}

                def emit_u(tc):
                    W.wait(V, AC)
                    W.wait(V, DV, dv_tr[0])
                    DV.inc(V.tensor_scalar(utm[tc % 2][:, :], xv[:, tc, :],
                                           mt[:, tc:tc + 1], at[:, tc:tc + 1],
                                           ALU.subtract, ALU.mult))
                    dv_u[tc] = DV.n

                emit_u(0)
                for tc in range(NTC):
                    if tc + 1 < NTC:
                        emit_u(tc + 1)
                    ast = astage[tc % 2]
                    W.wait(T, DV, dv_u[tc])
                    W.wait(T, WGC, wg0)
                    for dc in range(NDC):
                        pt = p_tr[dc % 2]
                        W.wait(T, DV, dv_tr[dc % 2])
                        PE.inc(T.transpose(pt[:, :], utm[tc % 2][:, dc * 128:(dc + 1) * 128],
                                           idb_sb[:, :]))
                        W.wait(V, PE)
                        W.wait(V, SAD, sc_ad[tc % 2])
                        DV.inc(V.tensor_copy(ast[:, dc * 128:(dc + 1) * 128], pt[:, :]))
                        dv_tr[dc % 2] = DV.n
                    W.wait(Sy, DV)
                    sc_ad[tc % 2] = sdma(
                        SAD, Sy,
                        attin_d[:, :, tc * 128:(tc + 1) * 128].rearrange("dc p f -> p dc f"),
                        ast[:, :].rearrange("p (dc f) -> p dc f", f=128))

            wqe2 = ar[:, 16384:20480]
            wqo2 = ar[:, 20480:24576]
            load_w3(SATW, wqe_b, wkeT, guard=(PE, PE.n))
            load_w3(SATW, wqo_b, wkoT)
            load_w3(SATW, wv_b, wvT)
            load_w3(SATW, wqe2, wqeT)
            wk_mark = load_w3(SATW, wqo2, wqoT)

            with ExitStack() as ph:
                phase_barrier()
                abuf = [ph.enter_context(nc.sbuf_tensor(f"aab{i}", [128, NDC * 512], BF16))
                        for i in range(2)]
                cosb = ph.enter_context(nc.sbuf_tensor("cosbA", [128, 4 * 512], BF16))
                sinb = ph.enter_context(nc.sbuf_tensor("sinbA", [128, 4 * 512], BF16))
                t1 = ph.enter_context(nc.sbuf_tensor("rt1A", [128, 512], F32))
                t2 = ph.enter_context(nc.sbuf_tensor("rt2A", [128, 512], F32))
                t3 = ph.enter_context(nc.sbuf_tensor("rt3A", [128, 512], F32))
                t4 = ph.enter_context(nc.sbuf_tensor("rt4A", [128, 512], F32))
                kst = [ph.enter_context(nc.sbuf_tensor(f"kst{i}", [128, 512], BF16))
                       for i in range(4)]
                sq2 = [ph.enter_context(nc.sbuf_tensor(f"sqA{i}", [128, 512], BF16))
                       for i in range(2)]
                knacc = ph.enter_context(nc.sbuf_tensor("knacc", [128, 512], F32))
                qnacc = ph.enter_context(nc.sbuf_tensor("qnaccA", [128, 512], F32))
                kn1 = ph.enter_context(nc.sbuf_tensor("kn1", [128, 1], F32))
                vsb = [ph.enter_context(nc.sbuf_tensor(f"vsb{i}", [128, 1040], BF16))
                       for i in range(2)]
                p_ke = [ph.enter_context(nc.psum_tensor(f"pke{i}", [128, 512], F32))
                        for i in range(2)]
                p_ko = [ph.enter_context(nc.psum_tensor(f"pko{i}", [128, 512], F32))
                        for i in range(2)]
                p_kn = [ph.enter_context(nc.psum_tensor(f"pkn{i}", [128, 512], F32))
                        for i in range(2)]
                p_v = [ph.enter_context(nc.psum_tensor(f"pv2{i}", [128, 512], F32))
                       for i in range(2)]
                p_knt = p_kn[0][0:1, 0:128]

                DV.inc(V.memset(knacc[:, :], 0.0))
                DV.inc(V.memset(qnacc[:, :], 0.0))
                for i in range(2):
                    DV.inc(V.memset(
                        vsb[i][:, :].rearrange("p (h f) -> p h f", f=65)[:, :, 64:65], 1.0))

                adma = {}
                a_guard = [(PE, 0), (PE, 0)]

                def fetch_attin(st, bi):
                    W.wait(Sy, *a_guard[bi])
                    W.wait(Sy, SAD)
                    adma[st] = (sdma(SA[bi], Sy,
                                     abuf[bi][:, :].rearrange("p (dc f) -> p dc f", f=512),
                                     attin_d[:, :, st * 512:(st + 1) * 512].rearrange("dc p f -> p dc f")),
                                bi)

                tmark = [0]

                def fetch_tables(col, dv_guard):
                    W.wait(G, DV, dv_guard)
                    W.wait(G, STB)
                    STB.dma(G.dma_start(
                        cosb[:, :].rearrange("p (j f) -> p j f", f=512),
                        cosT[:, col * 512:(col + 1) * 512].rearrange("(j p) f -> p j f", p=128)))
                    STB.dma(G.dma_start(
                        sinb[:, :].rearrange("p (j f) -> p j f", f=512),
                        sinT[:, col * 512:(col + 1) * 512].rearrange("(j p) f -> p j f", p=128)))
                    tmark[0] = STB.n

                fetch_attin(0, 0)
                fetch_tables(0, 0)
                kst_guard = [0, 0, 0, 0]
                sc_vdma = [0, 0]
                dv_keko = [0, 0]
                dv_kn = [0, 0]
                dv_v = [0, 0]
                pl_sq = [0, 0]
                for st in range(4):
                    scn, ab = adma[st]
                    if st + 1 < 4:
                        fetch_attin(st + 1, (st + 1) % 2)
                    dv_rope_st = 0
                    tasks = [(side, jc) for side in range(2) for jc in range(4)]
                    pe_proj = {}

                    def emit_proj(i):
                        side, jc = tasks[i]
                        web = wqe_b if side == 0 else wqe2
                        wob2 = wqo_b if side == 0 else wqo2
                        kbk = i % 2
                        W.wait(T, SA[ab], scn)
                        W.wait(T, SATW, wk_mark)
                        W.wait(T, DV, dv_keko[kbk])
                        for dc in range(NDC):
                            T.matmul(p_ke[kbk][:, :],
                                     web[:, dc * 512 + jc * 128: dc * 512 + (jc + 1) * 128],
                                     abuf[ab][:, dc * 512:(dc + 1) * 512],
                                     start=(dc == 0), stop=(dc == NDC - 1))
                        for dc in range(NDC):
                            ins = T.matmul(p_ko[kbk][:, :],
                                           wob2[:, dc * 512 + jc * 128: dc * 512 + (jc + 1) * 128],
                                           abuf[ab][:, dc * 512:(dc + 1) * 512],
                                           start=(dc == 0), stop=(dc == NDC - 1))
                        pe_proj[i] = PE.inc(ins)

                    emit_proj(0)
                    for i, (side, jc) in enumerate(tasks):
                        if i + 1 < len(tasks):
                            emit_proj(i + 1)
                        dest = kro_d if side == 0 else qro_d
                        nacc = knacc if side == 0 else qnacc
                        kbk = i % 2
                        W.wait(V, PE, pe_proj[i])
                        W.wait(V, STB, tmark[0])
                        W.wait(V, SKST[2 * kbk], kst_guard[2 * kbk])
                        W.wait(V, SKST[2 * kbk + 1], kst_guard[2 * kbk + 1])
                        co = cosb[:, jc * 512:(jc + 1) * 512]
                        si = sinb[:, jc * 512:(jc + 1) * 512]
                        klo = kst[2 * kbk]
                        khi = kst[2 * kbk + 1]
                        V.tensor_tensor(t1[:, :], p_ke[kbk][:, :], co, ALU.mult)
                        V.tensor_tensor(t2[:, :], p_ko[kbk][:, :], si, ALU.mult)
                        V.tensor_tensor(t3[:, :], p_ke[kbk][:, :], si, ALU.mult)
                        DV.inc(V.tensor_tensor(t4[:, :], p_ko[kbk][:, :], co, ALU.mult))
                        W.wait(V, DV)
                        DV.inc(V.tensor_tensor(klo[:, :], t1[:, :], t2[:, :], ALU.subtract))
                        DV.inc(V.tensor_tensor(khi[:, :], t3[:, :], t4[:, :], ALU.add))
                        dv_rope_st = DV.n
                        dv_keko[kbk] = DV.n
                        W.wait(Sy, DV, dv_rope_st)
                        kst_guard[2 * kbk] = sdma(
                            SKST[2 * kbk], Sy,
                            dest[jc, :, st * 512:(st + 1) * 512], klo[:, :])
                        kst_guard[2 * kbk + 1] = sdma(
                            SKST[2 * kbk + 1], Sy,
                            dest[4 + jc, :, st * 512:(st + 1) * 512], khi[:, :])
                        for ki, kt in ((0, klo), (1, khi)):
                            W.wait(G, DV, dv_rope_st)
                            W.wait(G, PE, pl_sq[ki])
                            PL.inc(G.tensor_tensor(sq2[ki][:, :], kt[:, :], kt[:, :],
                                                   ALU.mult))
                            pl_sq2 = PL.n
                            W.wait(T, PL, pl_sq2)
                            W.wait(T, WGC, wg0)
                            W.wait(T, DV, dv_kn[ki])
                            PE.inc(T.matmul(p_kn[ki][:, :], oAB_sb[:, :], sq2[ki][:, :],
                                            start=True, stop=True))
                            pl_sq[ki] = PE.n
                            W.wait(V, PE)
                            W.wait(V, DV, max(dv_kn))
                            DV.inc(V.tensor_tensor(nacc[:, :], p_kn[ki][:, :],
                                                   nacc[:, :], ALU.max))
                            dv_kn[ki] = DV.n
                    W.wait(V, DV)
                    DV.inc(V.reduce_max(kn1[:, :], qnacc[:, :], axis=AX.X))
                    W.wait(T, DV)
                    W.wait(T, DV, dv_kn[0])
                    PE.inc(T.transpose(p_knt, kn1[:, :], idf_sb[:, :]))
                    W.wait(V, PE)
                    DV.inc(V.reduce_max(knmax[:, 4 + st:5 + st], p_knt, axis=AX.X))
                    dv_kn[0] = DV.n
                    DV.inc(V.memset(qnacc[:, :], 0.0))
                    for sub in range(4):
                        stk128 = st * 4 + sub
                        vs = vsb[sub % 2]
                        for hlf in range(2):
                            vbk = hlf
                            W.wait(T, SA[ab], scn)
                            W.wait(T, DV, dv_v[vbk])
                            for dc in range(NDC):
                                ins = T.matmul(p_v[vbk][:, :],
                                               abuf[ab][:, dc * 512 + sub * 128: dc * 512 + (sub + 1) * 128],
                                               wv_b[:, dc * D + hlf * 512: dc * D + (hlf + 1) * 512],
                                               start=(dc == 0), stop=(dc == NDC - 1))
                            PE.inc(ins)
                            W.wait(V, PE)
                            W.wait(V, SVS[sub % 2], sc_vdma[sub % 2])
                            dst = vs[:, :].rearrange("p (h f) -> p h f", f=65)[
                                :, hlf * 8:(hlf + 1) * 8, 0:64]
                            src = p_v[vbk][:, :].rearrange("p (h f) -> p h f", f=64)
                            DV.inc(V.tensor_copy(dst, src))
                            dv_v[vbk] = DV.n
                        W.wait(Sy, DV)
                        sc_vdma[sub % 2] = sdma(
                            SVS[sub % 2], Sy,
                            v_d[:, stk128, :, :].rearrange("h p f -> p h f"),
                            vs[:, :].rearrange("p (h f) -> p h f", f=65))
                    if st + 1 < 4:
                        fetch_tables(st + 1, dv_rope_st)
                    a_guard[ab] = (PE, PE.n)
                W.wait(V, DV)
                DV.inc(V.reduce_max(kn1[:, :], knacc[:, :], axis=AX.X))
                W.wait(T, DV)
                W.wait(T, DV, dv_kn[0])
                PE.inc(T.transpose(p_knt, kn1[:, :], idf_sb[:, :]))
                W.wait(V, PE)
                DV.inc(V.reduce_max(knmax[:, 0:1], p_knt, axis=AX.X))

            wq_mark = load_w3(SATW, wo_t, woT, guard=(PE, PE.n))
            qbuf = [ar[:, 0:4096], ar[:, 4096:8192]]

            with ExitStack() as ph:
                phase_barrier()
                csc = ph.enter_context(nc.sbuf_tensor("csc", [1, 4], F32))
                bias_sb = ph.enter_context(nc.sbuf_tensor("bias_sb", [128, 1], F32))
                vbuf = [ph.enter_context(nc.sbuf_tensor(f"vbq{i}", [128, 2 * 16 * 65], BF16))
                        for i in range(2)]
                kbuf = kbuf_v
                Ebuf = Ebuf_v
                rz = [ph.enter_context(nc.sbuf_tensor(f"rz{i}", [1, 512], BF16))
                      for i in range(2)]
                zb = [ph.enter_context(nc.sbuf_tensor(f"zb{i}", [64, 512], F32))
                      for i in range(2)]
                O_sb = ph.enter_context(nc.sbuf_tensor("O_sb", [128, NDC * 512], BF16))
                p_scw = [ph.enter_context(nc.psum_tensor(f"pscw{i}", [128, 1024], F32))
                         for i in range(2)]
                p_pv = [ph.enter_context(nc.psum_tensor(f"ppv{i}", [65, 512], F32))
                        for i in range(2)]
                p_z = ph.enter_context(nc.psum_tensor("pz", [64, 512], F32))
                p_wo = ph.enter_context(nc.psum_tensor("pwo", [128, 512], F32))
                p_b = p_wo[:, 0:1]
                dv_pv = {}
                ac_zb = {}
                dv_zb_used = {}

                def heads_of(c):
                    hA = 2 * c if c < 4 else 8 + 2 * (c - 4)
                    return hA, hA + 1

                kdma = {}
                k_guard = [(PE, 0), (PE, 0)]

                def fetch_pair(qt, c, bi):
                    W.wait(Sy, *k_guard[bi])
                    for s4 in SKST:
                        W.wait(Sy, s4)
                    for s2 in SVS:
                        W.wait(Sy, s2)
                    W.wait(Sy, SKV[bi])
                    hA, hB = heads_of(c)
                    SKV[bi].dma(Sy.dma_start(kbuf[bi][:, :], kro_d[c, :, :]))
                    SKV[bi].dma(Sy.dma_start(
                        vbuf[bi][:, :].rearrange("p (hh k f) -> p hh k f", hh=2, f=65),
                        v_d[hA:hA + 2, :, :, :].rearrange("hh k p f -> p hh k f")))
                    kdma[(qt, c)] = (SKV[bi].n, bi)

                qdma = {}
                q_guard = [(PE, 0), (PE, 0)]

                def fetch_q(qt, bi):
                    W.wait(Sy, *q_guard[bi])
                    for s4 in SKST:
                        W.wait(Sy, s4)
                    qdma[qt] = (sdma(SA[bi], Sy,
                                     qbuf[bi][:, :].rearrange("p (c f) -> p c f", f=512),
                                     qro_d[:, :, qt * 512:(qt + 1) * 512].rearrange("c p f -> p c f")),
                                bi)

                fetch_q(0, 0)
                pe_wo_last = 0
                for qt in range(NQT):
                    scn_q, qb = qdma[qt]
                    if qt + 1 < NQT:
                        fetch_q(qt + 1, (qt + 1) % 2)
                    W.wait(V, AC)
                    DV.inc(V.tensor_tensor(csc[:, 1:2], knmax[:, 4 + qt:5 + qt],
                                           knmax[:, 0:1], ALU.add))
                    W.wait(V, DV)
                    DV.inc(V.tensor_scalar(csc[:, 2:3], csc[:, 1:2], -1.02 / 16.0, None,
                                           ALU.mult))
                    W.wait(T, DV)
                    W.wait(T, DV, dv_zb_used.get(0, 0))
                    PE.inc(T.matmul(p_b, ones_sb[0:1, 0:128], csc[0:1, 2:3],
                                    start=True, stop=True))
                    W.wait(Sc, PE)
                    AC.inc(Sc.copy(bias_sb[:, :], p_b))
                    W.wait(Sc, AC)

                    fetch_pair(qt, 0, 0)
                    prev_exp = {}
                    pending = [None]

                    def emit_normalize():
                        if pending[0] is None:
                            return
                        cN, pvm = pending[0]
                        pending[0] = None
                        for hh in range(2):
                            pvb = p_pv[hh]
                            W.wait(V, PE, pvm[(hh, 15)])
                            W.wait(V, AC, ac_zb.get(hh, 0))
                            with nc.allow_low_precision(
                                    reason="softmax normalizer 1/Z to bf16 — same "
                                           "class as the bf16 attention operands"):
                                DV.inc(V.reciprocal(rz[hh][:, :], pvb[64:65, :]))
                            dv_rz = DV.n
                            W.wait(T, DV, dv_rz)
                            W.wait(T, AC, ac_zb.get(1 - hh, 0))
                            PE.inc(T.matmul(p_z[:, :], ones_bf[0:1, 0:64],
                                            rz[hh][0:1, :], start=True, stop=True))
                            W.wait(Sc, PE)
                            W.wait(Sc, DV, dv_zb_used.get(hh, 0))
                            AC.inc(Sc.copy(zb[hh][:, :], p_z[:, :]))
                            ac_zb[hh] = AC.n
                            W.wait(V, AC, ac_zb[hh])
                            dst = O_sb[64 * hh:64 * hh + 64, cN * 512:(cN + 1) * 512]
                            DV.inc(V.tensor_tensor(dst, pvb[0:64, :], zb[hh][:, :],
                                                   ALU.mult))
                            dv_pv[hh] = DV.n
                            dv_zb_used[hh] = DV.n

                    for c in range(8):
                        if c + 1 < 8:
                            fetch_pair(qt, c + 1, (c + 1) % 2)
                        scn_k, kb = kdma[(qt, c)]
                        hA, hB = heads_of(c)
                        qhh = [qbuf[qb][0:64, c * 512:(c + 1) * 512],
                               qbuf[qb][64:128, c * 512:(c + 1) * 512]]
                        khh = [kbuf[kb][0:64, :], kbuf[kb][64:128, :]]
                        ac_exp = {}
                        pe_sc_mark = {}
                        pe_pv_mark = {}

                        def emit_sc(hh, kp):
                            bank = (2 * kp + hh) % 2
                            W.wait(T, SKV[kb], scn_k)
                            W.wait(T, SA[qb], scn_q)
                            if kp >= 1:
                                W.wait(T, AC, ac_exp.get((hh, kp - 1),
                                                         prev_exp.get((hh, 7), 0)))
                            else:
                                W.wait(T, AC, prev_exp.get((hh, 7), 0))
                            for half in range(2):
                                kc = 2 * kp + half
                                ins = T.matmul(p_scw[bank][:, half * 512:(half + 1) * 512],
                                               khh[hh][:, kc * 128:(kc + 1) * 128],
                                               qhh[hh], start=True, stop=True)
                            PE.inc(ins)
                            pe_sc_mark[(hh, kp)] = PE.n

                        def emit_exp(hh, kp):
                            slot = (2 * kp + hh) % 4
                            W.wait(Sc, PE, pe_sc_mark[(hh, kp)])
                            W.wait(Sc, PE, pe_pv_mark.get((hh, 2 * kp - 4),
                                                          prev_exp.get("pvdone", 0)))
                            AC.inc(Sc.activation(Ebuf[:, slot * 1024:(slot + 1) * 1024],
                                                 p_scw[(2 * kp + hh) % 2][:, :],
                                                 AF.Exp, bias=bias_sb[:, :], scale=0.125))
                            ac_exp[(hh, kp)] = AC.n

                        def emit_pv(hh, kc):
                            kp = kc // 2
                            slot = (2 * kp + hh) % 4
                            if kc == 0:
                                W.wait(T, DV, dv_pv.get(hh, 0))
                            W.wait(T, AC, ac_exp[(hh, kp)])
                            PE.inc(T.matmul(p_pv[hh][:, :],
                                            vbuf[kb][:, (hh * 16 + kc) * 65:
                                                     (hh * 16 + kc + 1) * 65],
                                            Ebuf[:, slot * 1024 + (kc % 2) * 512:
                                                 slot * 1024 + (kc % 2) * 512 + 512],
                                            start=(kc == 0), stop=(kc == 15)))
                            pe_pv_mark[(hh, kc)] = PE.n

                        for hh in range(2):
                            emit_sc(hh, 0)
                            emit_exp(hh, 0)
                        emit_normalize()
                        for kp in range(8):
                            if kp + 1 < 8:
                                for hh in range(2):
                                    emit_sc(hh, kp + 1)
                                    emit_exp(hh, kp + 1)
                            for half in range(2):
                                emit_pv(0, 2 * kp + half)
                                emit_pv(1, 2 * kp + half)
                        prev_exp = dict(ac_exp)
                        prev_exp["pvdone"] = PE.n
                        k_guard[kb] = (PE, PE.n)
                        pending[0] = (c, dict(pe_pv_mark))
                        if c == 7:
                            emit_normalize()
                    q_guard[qb] = (PE, PE.n)
                    dv_O = DV.n
                    W.wait(T, AC)
                    wo_banks = [p_wo, p_scw[0][:, 0:512]]
                    dv_wo = [DV.n, DV.n]
                    for ts in range(4):
                        tc = qt * 4 + ts
                        for hlf in range(2):
                            i2 = (ts * 2 + hlf) % 2
                            pmw = wo_banks[i2]
                            W.wait(T, DV, dv_O)
                            W.wait(T, DV, dv_wo[i2])
                            W.wait(T, SATW, wq_mark)
                            for dc in range(NDC):
                                T.matmul(pmw[:, :],
                                         O_sb[:, dc * 512 + ts * 128: dc * 512 + (ts + 1) * 128],
                                         wo_t[:, dc * D + hlf * 512: dc * D + (hlf + 1) * 512],
                                         start=(dc == 0), stop=False)
                            PE.inc(T.matmul(pmw[:, :], ones_sb[0:1, 0:128],
                                            wob_sb[0:1, hlf * 512:(hlf + 1) * 512],
                                            start=False, stop=True))
                            pe_wo_last = PE.n
                            W.wait(V, PE)
                            DV.inc(V.tensor_tensor(xv[:, tc, hlf * 512:(hlf + 1) * 512],
                                                   pmw[:, :],
                                                   xv[:, tc, hlf * 512:(hlf + 1) * 512],
                                                   ALU.add))
                            dv_wo[i2] = DV.n
                    dv_zb_used[0] = max(dv_zb_used.get(0, 0), dv_wo[0])

        ffn(0)
        if stage >= 2:
            attention()
        if stage >= 3:
            ffn(1)

        W.wait(Sy, DV)
        W.wait(Sy, AC)
        W.wait(Sy, PE)
        ST.dma(Sy.dma_start(out.rearrange("(tc p) d -> p tc d", p=128), xv))
        Sy.wait_ge(ST.h, ST.n)
        assert len(ALL_SEMS) + 2 < 106
    return nc


_CONST_CACHE = {}


def _consts():
    if _CONST_CACHE:
        return _CONST_CACHE
    pos = np.arange(S, dtype=np.float32)[:, None]
    freqs = np.exp(np.arange(0, D, 2, dtype=np.float32) *
                   np.float32(-math.log(10000.0) / D))
    ang = pos * freqs[None, :]
    _CONST_CACHE["cosT"] = np.ascontiguousarray(np.cos(ang).astype(np.float32).T)
    _CONST_CACHE["sinT"] = np.ascontiguousarray(np.sin(ang).astype(np.float32).T)
    _CONST_CACHE["idf"] = np.eye(128, dtype=np.float32)
    oab = np.zeros((128, 128), np.float32)
    oab[0:64, 0:64] = 1.0
    oab[64:128, 64:128] = 1.0
    _CONST_CACHE["onesAB"] = oab
    _CONST_CACHE["onesr"] = np.ones((1, 512), np.float32)
    oh = np.zeros((NE, NE, 128), np.float32)
    for e in range(NE):
        oh[e, e, :] = 1.0 / ALPHA
    _CONST_CACHE["ohdiv"] = oh
    return _CONST_CACHE


def _prep(inputs):
    c = _consts()
    wq = np.asarray(inputs["L0_attn_wq"], np.float32)
    wk = np.asarray(inputs["L0_attn_wk"], np.float32)
    wv = np.asarray(inputs["L0_attn_wv"], np.float32)
    wo = np.asarray(inputs["L0_attn_wo_w"], np.float32)
    m1w = np.asarray(inputs["L0_ffn_mlp1_w"], np.float32)
    m1b = np.asarray(inputs["L0_ffn_mlp1_b"], np.float32)
    m2w = np.asarray(inputs["L0_ffn_mlp2_w"], np.float32)
    shared = {
        "wqeT": np.ascontiguousarray(wq[0::2, :].T),
        "wqoT": np.ascontiguousarray(wq[1::2, :].T),
        "wkeT": np.ascontiguousarray(wk[0::2, :].T),
        "wkoT": np.ascontiguousarray(wk[1::2, :].T),
        "wvT": np.ascontiguousarray(wv.T),
        "woT": np.ascontiguousarray(wo.T),
        "wob": np.asarray(inputs["L0_attn_wo_b"], np.float32).reshape(1, D),
        "cosT": c["cosT"], "sinT": c["sinT"],
        "nwr": np.asarray(inputs["L0_ffn_norm_w"], np.float32).reshape(1, D),
        "gwT": np.ascontiguousarray(np.asarray(inputs["L0_ffn_gate_w"], np.float32).T),
        "gbr": np.asarray(inputs["L0_ffn_gate_b"], np.float32).reshape(1, NE),
        "w1eT": np.ascontiguousarray(m1w[:, 0::2, :].transpose(0, 2, 1)),
        "w1oT": np.ascontiguousarray(m1w[:, 1::2, :].transpose(0, 2, 1)),
        "w2T": np.ascontiguousarray(m2w.transpose(0, 2, 1)),
        "b1e3": np.ascontiguousarray(m1b[:, 0::2].reshape(NE, 8, 128).transpose(1, 2, 0)),
        "b1o3": np.ascontiguousarray(m1b[:, 1::2].reshape(NE, 8, 128).transpose(1, 2, 0)),
        "b2n": np.asarray(inputs["L0_ffn_mlp2_b"], np.float32),
        "idf": c["idf"], "onesAB": c["onesAB"], "onesr": c["onesr"],
        "ohdiv": c["ohdiv"],
    }
    x = np.asarray(inputs["L0_attn_x"], np.float32)
    maps = []
    for b in range(8):
        m = dict(shared)
        m["x"] = np.ascontiguousarray(x[b])
        maps.append(m)
    return maps


def kernel(**inputs):
    maps = _prep(inputs)
    nc = build_nc(stage=3)
    res = run_bass_kernel_spmd(nc, maps, core_ids=list(range(8)))
    return np.stack([np.asarray(r["out"], np.float32) for r in res.results], axis=0)


# version 11
# speedup vs baseline: 1.1951x; 1.1951x over previous
import math
from contextlib import ExitStack

import numpy as np

import concourse.bass as bass
import concourse.mybir as mybir
from concourse.bass_utils import run_bass_kernel_spmd

F32 = mybir.dt.float32
BF16 = mybir.dt.bfloat16
AF = mybir.ActivationFunctionType
ALU = mybir.AluOpType
AX = mybir.AxisListType

S = 2048
D = 1024
H = 16
NE = 8
NTC = 16
NQT = 4
NDC = 8
EPS_LN = 1e-5
EPS_RMS = float(np.finfo(np.float32).eps)
ALPHA = 1.702
LIMIT = 7.0


class Sem:
    def __init__(self, stack, nc, name):
        self.h = stack.enter_context(nc.semaphore(name))
        self.n = 0
        self.name = name

    def inc(self, instr, k=1):
        instr.then_inc(self.h, k)
        self.n += k
        return self.n

    def dma(self, instr):
        return self.inc(instr, 16)


class Waiter:
    def __init__(self):
        self.seen = {}

    def wait(self, eng, sem, at=None):
        v = sem.n if at is None else at
        if v <= 0:
            return
        key = (id(eng), sem.name)
        if self.seen.get(key, -1) >= v:
            return
        eng.wait_ge(sem.h, v)
        self.seen[key] = v


def build_nc(stage=3):
    nc = bass.Bass("TRN2", target_bir_lowering=False)

    xin = nc.dram_tensor("x", [S, D], F32, kind="ExternalInput").ap()
    wqeT = nc.dram_tensor("wqeT", [D, 512], F32, kind="ExternalInput").ap()
    wqoT = nc.dram_tensor("wqoT", [D, 512], F32, kind="ExternalInput").ap()
    wkeT = nc.dram_tensor("wkeT", [D, 512], F32, kind="ExternalInput").ap()
    wkoT = nc.dram_tensor("wkoT", [D, 512], F32, kind="ExternalInput").ap()
    wvT = nc.dram_tensor("wvT", [D, D], F32, kind="ExternalInput").ap()
    woT = nc.dram_tensor("woT", [D, D], F32, kind="ExternalInput").ap()
    wob = nc.dram_tensor("wob", [1, D], F32, kind="ExternalInput").ap()
    cosT = nc.dram_tensor("cosT", [512, S], F32, kind="ExternalInput").ap()
    sinT = nc.dram_tensor("sinT", [512, S], F32, kind="ExternalInput").ap()
    nwr = nc.dram_tensor("nwr", [1, D], F32, kind="ExternalInput").ap()
    gwT = nc.dram_tensor("gwT", [D, NE], F32, kind="ExternalInput").ap()
    gbr = nc.dram_tensor("gbr", [1, NE], F32, kind="ExternalInput").ap()
    w1eT = nc.dram_tensor("w1eT", [NE, D, D], F32, kind="ExternalInput").ap()
    w1oT = nc.dram_tensor("w1oT", [NE, D, D], F32, kind="ExternalInput").ap()
    w2T = nc.dram_tensor("w2T", [NE, D, D], F32, kind="ExternalInput").ap()
    b1e3 = nc.dram_tensor("b1e3", [8, 128, NE], F32, kind="ExternalInput").ap()
    b1o3 = nc.dram_tensor("b1o3", [8, 128, NE], F32, kind="ExternalInput").ap()
    b2n = nc.dram_tensor("b2n", [NE, D], F32, kind="ExternalInput").ap()
    idf = nc.dram_tensor("idf", [128, 128], F32, kind="ExternalInput").ap()
    onesAB = nc.dram_tensor("onesAB", [128, 128], F32, kind="ExternalInput").ap()
    onesr = nc.dram_tensor("onesr", [1, 512], F32, kind="ExternalInput").ap()
    ohdiv = nc.dram_tensor("ohdiv", [NE, NE, 128], F32, kind="ExternalInput").ap()

    out = nc.dram_tensor("out", [S, D], F32, kind="ExternalOutput").ap()

    attin_d = nc.dram_tensor("attin_d", [NDC, 128, S], BF16, kind="Internal").ap()
    h_d = nc.dram_tensor("h_d", [NDC, 128, S], BF16, kind="Internal").ap()
    kro_d = nc.dram_tensor("kro_d", [NDC, 128, S], BF16, kind="Internal").ap()
    qro_d = nc.dram_tensor("qro_d", [NDC, 128, S], BF16, kind="Internal").ap()
    v_d = nc.dram_tensor("v_d", [H, NTC, 128, 65], BF16, kind="Internal").ap()

    T = nc.tensor
    V = nc.vector
    Sc = nc.scalar
    G = nc.gpsimd
    Sy = nc.sync
    W = Waiter()

    stk = ExitStack()
    with stk:
        LD = Sem(stk, nc, "LD")
        WGC = Sem(stk, nc, "WGC")
        PE = Sem(stk, nc, "PE")
        DV = Sem(stk, nc, "DV")
        AC = Sem(stk, nc, "AC")
        PL = Sem(stk, nc, "PL")
        ST = Sem(stk, nc, "ST")
        SH = [Sem(stk, nc, f"SH{i}") for i in range(2)]
        SA = [Sem(stk, nc, f"SA{i}") for i in range(2)]
        SKV = [Sem(stk, nc, f"SKV{i}") for i in range(2)]
        SW1 = Sem(stk, nc, "SW1")
        SW2 = [Sem(stk, nc, f"SW2{i}") for i in range(2)]
        STB = Sem(stk, nc, "STB")
        SATW = Sem(stk, nc, "SATW")
        SKST = [Sem(stk, nc, f"SKST{i}") for i in range(4)]
        SVS = [Sem(stk, nc, f"SVS{i}") for i in range(2)]
        SHD = Sem(stk, nc, "SHD")
        SAD = Sem(stk, nc, "SAD")
        ALL_SEMS = ([LD, WGC, PE, DV, AC, PL, SW1, STB, SATW, SHD, SAD]
                    + SH + SA + SKV + SW2 + SKST + SVS)

        def sdma(sem, q, out_ap, in_ap):
            W.wait(q, sem)
            sem.dma(q.dma_start(out_ap, in_ap))
            return sem.n

        x_sb = stk.enter_context(nc.sbuf_tensor("x_sb", [128, NTC * D], F32))
        xv = x_sb[:, :].rearrange("p (tc d) -> p tc d", tc=NTC)
        g_fm = stk.enter_context(nc.sbuf_tensor("g_fm", [NE, S], F32))
        nw_bc = stk.enter_context(nc.sbuf_tensor("nw_bc", [128, D], F32))
        b2_sb = stk.enter_context(nc.sbuf_tensor("b2_sb", [NE, D], F32))
        b1e_sb = stk.enter_context(nc.sbuf_tensor("b1e_sb", [128, 64], F32))
        b1o_sb = stk.enter_context(nc.sbuf_tensor("b1o_sb", [128, 64], F32))
        gb_sb = stk.enter_context(nc.sbuf_tensor("gb_sb", [1, NE], F32))
        wob_sb = stk.enter_context(nc.sbuf_tensor("wob_sb", [1, D], F32))
        ones_sb = stk.enter_context(nc.sbuf_tensor("ones_sb", [1, 128], F32))
        oAB_sb = stk.enter_context(nc.sbuf_tensor("oAB_sb", [128, 128], BF16))
        idf_sb = stk.enter_context(nc.sbuf_tensor("idf_sb", [128, 128], F32))
        idb_sb = stk.enter_context(nc.sbuf_tensor("idb_sb", [128, 128], BF16))
        oh_sb = stk.enter_context(nc.sbuf_tensor("oh_sb", [NE, NE * 128], F32))
        gwT_sb = stk.enter_context(nc.sbuf_tensor("gwT_sb", [128, NDC * NE], BF16))
        knmax = stk.enter_context(nc.sbuf_tensor("knmax", [1, 8], F32))
        arena = stk.enter_context(nc.sbuf_tensor("arena", [128, 24576], BF16))
        ar = arena[:, :]
        w1e_b = ar[:, 0:8192]
        w1o_b = ar[:, 8192:16384]
        w2_b = [ar[:, 16384:20480], ar[:, 20480:24576]]
        wqe_b = ar[:, 0:4096]
        wqo_b = ar[:, 4096:8192]
        wv_b = ar[:, 8192:16384]
        wo_t = ar[:, 16384:24576]
        Ebuf_v = ar[:, 8192:12288]
        kbuf_v = [ar[:, 12288:14336], ar[:, 14336:16384]]

        LD.dma(Sy.dma_start(x_sb[:, :].rearrange("p (tc d) -> p tc d", tc=NTC),
                            xin.rearrange("(tc p) d -> p tc d", p=128)))
        LD.dma(Sy.dma_start(b2_sb[:, :], b2n))
        LD.dma(Sy.dma_start(b1e_sb[:, :].rearrange("p (i e) -> p i e", e=NE),
                            b1e3.rearrange("i p e -> p i e")))
        LD.dma(Sy.dma_start(b1o_sb[:, :].rearrange("p (i e) -> p i e", e=NE),
                            b1o3.rearrange("i p e -> p i e")))
        LD.dma(Sy.dma_start(gb_sb[:, :], gbr))
        LD.dma(Sy.dma_start(wob_sb[:, :], wob))
        LD.dma(Sy.dma_start(ones_sb[:, :], onesr[0:1, 0:128]))
        LD.dma(Sy.dma_start(idf_sb[:, :], idf))
        LD.dma(Sy.dma_start(oh_sb[:, :].rearrange("p (e c) -> p e c", e=NE),
                            ohdiv.rearrange("e p c -> p e c")))
        ones_bf = stk.enter_context(nc.sbuf_tensor("ones_bf", [1, 64], BF16))
        WGC.dma(G.dma_start(ones_bf[:, :], onesr[0:1, 0:64]))
        WGC.dma(G.dma_start(oAB_sb[:, :], onesAB))
        WGC.dma(G.dma_start(idb_sb[:, :], idf))
        WGC.dma(G.dma_start(gwT_sb[:, :].rearrange("p (dc e) -> p dc e", e=NE),
                            gwT.rearrange("(dc p) e -> p dc e", p=128)))
        ld0, wg0 = LD.n, WGC.n

        with nc.psum_tensor("p_nw", [128, 512], F32) as p_nw, \
                nc.sbuf_tensor("nwr_sb", [1, D], F32) as nwr_sb:
            LD.dma(Sy.dma_start(nwr_sb[:, :], nwr))
            ld0 = LD.n
            for hlf in range(2):
                W.wait(T, LD, ld0)
                W.wait(T, AC)
                PE.inc(T.matmul(p_nw[:, :], ones_sb[0:1, 0:128],
                                nwr_sb[0:1, hlf * 512:(hlf + 1) * 512],
                                start=True, stop=True))
                W.wait(Sc, PE)
                AC.inc(Sc.copy(nw_bc[:, hlf * 512:(hlf + 1) * 512], p_nw[:, :]))

        def phase_barrier():
            for eng in (T, V, Sc, G, Sy):
                for sem in ALL_SEMS:
                    W.wait(eng, sem)

        def load_w3(sem, dst, src, guard=None):
            if guard is not None:
                W.wait(G, *guard)
            n = src.shape[-1]
            return sdma(sem, G, dst[:, :].rearrange("p (dc n) -> p dc n", n=n),
                        src.rearrange("(dc p) n -> p dc n", p=128))

        def stats_norm(kind, tag, ph):
            raw = ph.enter_context(nc.sbuf_tensor(f"raw{tag}", [128, NTC * 12], F32))
            agg = ph.enter_context(nc.sbuf_tensor(f"agg{tag}", [128, NTC * 4], F32))
            mt = ph.enter_context(nc.sbuf_tensor(f"m{tag}", [128, NTC], F32))
            at = ph.enter_context(nc.sbuf_tensor(f"a{tag}", [128, NTC], F32))
            r2 = ph.enter_context(nc.sbuf_tensor(f"r2{tag}", [128, NTC], F32))
            tmp = ph.enter_context(nc.sbuf_tensor(f"tmq{tag}", [128, NTC * 4], F32))
            W.wait(V, LD, ld0)
            W.wait(V, DV)
            W.wait(V, PL)
            for tc in range(NTC):
                V.bn_stats(raw[:, tc * 12:tc * 12 + 6], xv[:, tc, 0:512])
                ins = V.bn_stats(raw[:, tc * 12 + 6:tc * 12 + 12], xv[:, tc, 512:1024])
                if tc == NTC - 1:
                    DV.inc(ins)
            W.wait(V, DV)
            for tc in range(NTC):
                V.bn_aggr(agg[:, tc * 4:tc * 4 + 2], raw[:, tc * 12:tc * 12 + 6])
                ins = V.bn_aggr(agg[:, tc * 4 + 2:tc * 4 + 4],
                                raw[:, tc * 12 + 6:tc * 12 + 12])
                if tc == NTC - 1:
                    DV.inc(ins)
            W.wait(V, DV)
            av = agg[:, :].rearrange("p (tc k) -> p tc k", k=4)
            m1, v1, m2, v2 = av[:, :, 0], av[:, :, 1], av[:, :, 2], av[:, :, 3]
            tv = tmp[:, :].rearrange("p (k tc) -> p k tc", k=4)
            t0, t1, t2, t3 = tv[:, 0, :], tv[:, 1, :], tv[:, 2, :], tv[:, 3, :]
            V.tensor_tensor(t0, m1, m2, ALU.add)
            V.tensor_tensor(t1, m1, m2, ALU.subtract)
            DV.inc(V.tensor_tensor(t2, v1, v2, ALU.add))
            W.wait(V, DV)
            V.tensor_scalar(mt[:, :], t0, 0.5, None, ALU.mult)
            DV.inc(V.tensor_tensor(t3, t1, t1, ALU.mult))
            W.wait(V, DV)
            DV.inc(V.scalar_tensor_tensor(t0, t3, 0.5, t2, ALU.mult, ALU.add))
            W.wait(V, DV)
            DV.inc(V.tensor_scalar(t1, t0, 0.5, EPS_LN, ALU.mult, ALU.add))
            W.wait(Sc, DV)
            AC.inc(Sc.activation(t2, t1, AF.Ln))
            W.wait(Sc, AC)
            AC.inc(Sc.activation(at[:, :], t2, AF.Exp, scale=-0.5))
            if kind == "ffn":
                W.wait(V, AC)
                V.tensor_scalar(t0, t1, -EPS_LN, None, ALU.add)
                DV.inc(V.tensor_tensor(t2, at[:, :], at[:, :], ALU.mult))
                W.wait(V, DV)
                DV.inc(V.tensor_tensor(t3, t0, t2, ALU.mult))
                W.wait(V, DV)
                DV.inc(V.tensor_scalar(t3, t3, EPS_RMS, None, ALU.add))
                W.wait(Sc, DV)
                AC.inc(Sc.activation(t0, t3, AF.Ln))
                W.wait(Sc, AC)
                AC.inc(Sc.activation(r2[:, :], t0, AF.Exp, scale=-0.5))
            return mt, at, r2

        def ffn(inv):
            tag = f"f{inv}"
            load_w3(SW1, w1e_b, w1eT[0], guard=(PE, PE.n))
            w1mark0 = load_w3(SW1, w1o_b, w1oT[0])
            W.wait(G, SW2[0])
            mk = sdma(SW2[0], G,
                      w2_b[0][:, :].rearrange("p (i n) -> p i n", n=512),
                      w2T[0, :, 0:512].rearrange("(i p) n -> p i n", p=128))
            w2mark0 = (mk, 0)
            with ExitStack() as ph:
                phase_barrier()
                mt, at, r2 = stats_norm("ffn", tag, ph)
                u_sb = [ph.enter_context(nc.sbuf_tensor(f"u{tag}{i}", [128, D], F32))
                        for i in range(2)]
                h1_sb = [ph.enter_context(nc.sbuf_tensor(f"h1{tag}{i}", [128, D], F32))
                         for i in range(2)]
                htm = [ph.enter_context(nc.sbuf_tensor(f"htm{tag}{i}", [128, D], BF16))
                       for i in range(2)]
                gbuf = [ph.enter_context(nc.sbuf_tensor(f"gb2{tag}{i}", [128, D], BF16))
                        for i in range(2)]
                hstage = [ph.enter_context(nc.sbuf_tensor(f"hs{tag}{i}", [128, NDC * 128], BF16))
                          for i in range(2)]
                lg_sb = ph.enter_context(nc.sbuf_tensor(f"lg{tag}", [128, 8], F32))
                top8 = ph.enter_context(nc.sbuf_tensor(f"top{tag}", [128, 8], F32))
                gtmp = ph.enter_context(nc.sbuf_tensor(f"gq{tag}", [128, 24], F32))
                g_tm = ph.enter_context(nc.sbuf_tensor(f"gtm{tag}", [128, 8], F32))
                p_tr = [ph.enter_context(nc.psum_tensor(f"ptr{tag}{i}", [128, 128], BF16))
                        for i in range(2)]
                p_lg = ph.enter_context(nc.psum_tensor(f"plg{tag}", [128, 8], F32))
                p_gt = ph.enter_context(nc.psum_tensor(f"pgt{tag}", [8, 128], F32))

                W.wait(Sc, AC)
                sc_hd = [0, 0]
                dv_tr = [0, 0]
                shd_mark = {}
                pl_h = {}

                def emit_norm(tc):
                    i = tc % 2
                    W.wait(V, AC)
                    W.wait(V, DV, dv_tr[0])
                    W.wait(V, PL)
                    DV.inc(V.tensor_scalar(u_sb[i][:, :], xv[:, tc, :], mt[:, tc:tc + 1],
                                           at[:, tc:tc + 1], ALU.subtract, ALU.mult))
                    W.wait(V, DV)
                    DV.inc(V.tensor_scalar(h1_sb[i][:, :], u_sb[i][:, :],
                                           r2[:, tc:tc + 1], None, ALU.mult))
                    dv_h1 = DV.n
                    W.wait(G, DV, dv_h1)
                    PL.inc(G.tensor_tensor(htm[i][:, :], h1_sb[i][:, :], nw_bc[:, :],
                                           ALU.mult))
                    pl_h[tc] = PL.n
                    DV.inc(V.scalar_tensor_tensor(xv[:, tc, :], u_sb[i][:, :], 0.5,
                                                  xv[:, tc, :], ALU.mult, ALU.add))

                emit_norm(0)
                for tc in range(NTC):
                    if tc + 1 < NTC:
                        emit_norm(tc + 1)
                    i = tc % 2
                    hs = hstage[i]
                    W.wait(T, PL, pl_h[tc])
                    W.wait(T, WGC, wg0)
                    for dc in range(NDC):
                        pt = p_tr[dc % 2]
                        if dc % 2 == 0:
                            W.wait(T, DV, dv_tr[0])
                        else:
                            W.wait(T, AC, dv_tr[1])
                        PE.inc(T.transpose(pt[:, :], htm[i][:, dc * 128:(dc + 1) * 128],
                                           idb_sb[:, :]))
                        if dc % 2 == 0:
                            W.wait(V, PE)
                            W.wait(V, SHD, sc_hd[i])
                            DV.inc(V.tensor_copy(hs[:, dc * 128:(dc + 1) * 128], pt[:, :]))
                            dv_tr[0] = DV.n
                        else:
                            W.wait(Sc, PE)
                            W.wait(Sc, SHD, sc_hd[i])
                            AC.inc(Sc.copy(hs[:, dc * 128:(dc + 1) * 128], pt[:, :]))
                            dv_tr[1] = AC.n
                    dv_h = DV.n
                    ac_h = AC.n
                    W.wait(Sy, DV, dv_h)
                    W.wait(Sy, AC, ac_h)
                    sc_hd[i] = sdma(
                        SHD, Sy,
                        h_d[:, :, tc * 128:(tc + 1) * 128].rearrange("dc p f -> p dc f"),
                        hs[:, :].rearrange("p (dc f) -> p dc f", f=128))
                    shd_mark[tc] = sc_hd[i]

                gdma = {}
                g_guard = [0, 0]

                def fetch_g(tc, bi):
                    W.wait(Sy, PE, g_guard[bi])
                    W.wait(Sy, SHD, shd_mark[tc])
                    gdma[tc] = (sdma(SH[bi], Sy,
                                     gbuf[bi][:, :].rearrange("p (dc f) -> p dc f", f=128),
                                     h_d[:, :, tc * 128:(tc + 1) * 128].rearrange("dc p f -> p dc f")),
                                bi)

                fetch_g(0, 0)
                for tc in range(NTC):
                    if tc + 1 < NTC:
                        fetch_g(tc + 1, (tc + 1) % 2)
                    scn_g, gi = gdma[tc]
                    W.wait(T, SH[gi], scn_g)
                    W.wait(T, AC)
                    W.wait(T, AC)
                    for dc in range(NDC):
                        T.matmul(p_lg[:, :], gbuf[gi][:, dc * 128:(dc + 1) * 128],
                                 gwT_sb[:, dc * NE:(dc + 1) * NE],
                                 start=(dc == 0), stop=False)
                    PE.inc(T.matmul(p_lg[:, :], ones_sb[0:1, 0:128], gb_sb[0:1, :],
                                    start=False, stop=True))
                    g_guard[gi] = PE.n
                    W.wait(Sc, PE)
                    W.wait(Sc, DV)
                    AC.inc(Sc.copy(lg_sb[:, :], p_lg[:, :]))
                    W.wait(V, AC)
                    DV.inc(V.max(top8[:, :], lg_sb[:, :]))
                    W.wait(V, DV)
                    DV.inc(V.tensor_tensor(gtmp[:, 0:1], top8[:, 0:1], top8[:, 1:2],
                                           ALU.subtract))
                    W.wait(Sc, DV)
                    AC.inc(Sc.activation(gtmp[:, 1:2], gtmp[:, 0:1], AF.Sigmoid))
                    W.wait(V, AC)
                    V.tensor_scalar(gtmp[:, 2:3], gtmp[:, 1:2], 0.5, None, ALU.mult)
                    DV.inc(V.tensor_scalar(gtmp[:, 3:4], gtmp[:, 1:2], -0.5, 0.5,
                                           ALU.mult, ALU.add))
                    V.tensor_scalar(gtmp[:, 8:16], lg_sb[:, :], top8[:, 0:1], None,
                                    ALU.is_equal)
                    DV.inc(V.tensor_scalar(gtmp[:, 16:24], lg_sb[:, :], top8[:, 1:2],
                                           None, ALU.is_equal))
                    W.wait(V, DV)
                    V.tensor_scalar(gtmp[:, 8:16], gtmp[:, 8:16], gtmp[:, 2:3], None,
                                    ALU.mult)
                    DV.inc(V.tensor_scalar(gtmp[:, 16:24], gtmp[:, 16:24], gtmp[:, 3:4],
                                           None, ALU.mult))
                    W.wait(V, DV)
                    DV.inc(V.tensor_tensor(g_tm[:, :], gtmp[:, 8:16], gtmp[:, 16:24],
                                           ALU.add))
                    W.wait(T, DV)
                    PE.inc(T.transpose(p_gt[:, :], g_tm[:, :], idf_sb[:, :]))
                    W.wait(Sc, PE)
                    AC.inc(Sc.copy(g_fm[:, tc * 128:(tc + 1) * 128], p_gt[:, :]))

            with ExitStack() as ph:
                phase_barrier()
                hbuf = [ph.enter_context(nc.sbuf_tensor(f"hb{tag}{i}", [128, NDC * 512], BF16))
                        for i in range(2)]
                acg = ph.enter_context(nc.sbuf_tensor(f"acg{tag}", [128, NDC * S], BF16))
                acgv = acg[:, :].rearrange("p (i t) -> p i t", i=NDC)
                v0 = [ph.enter_context(nc.sbuf_tensor(f"v0{tag}{i}", [128, 512], F32))
                      for i in range(2)]
                sl = [ph.enter_context(nc.sbuf_tensor(f"sl{tag}{i}", [128, 512], F32))
                      for i in range(2)]
                l1 = [ph.enter_context(nc.sbuf_tensor(f"l1{tag}{i}", [128, 512], F32))
                      for i in range(2)]
                l2s = ph.enter_context(nc.sbuf_tensor(f"l2{tag}", [128, 512], F32))
                l2 = [l2s, l2s]
                pp = [ph.enter_context(nc.sbuf_tensor(f"pq{tag}{i}", [128, 512], F32))
                      for i in range(2)]
                gbc = ph.enter_context(nc.sbuf_tensor(f"gbc{tag}", [128, 512], F32))
                p_t1e = [ph.enter_context(nc.psum_tensor(f"pt1e{tag}{i}", [128, 512], F32))
                         for i in range(2)]
                p_t1o = [ph.enter_context(nc.psum_tensor(f"pt1o{tag}{i}", [128, 512], F32))
                         for i in range(2)]
                p_gb = ph.enter_context(nc.psum_tensor(f"pgb{tag}", [128, 512], F32))
                p_m2 = [ph.enter_context(nc.psum_tensor(f"pm2{tag}{i}", [128, 512], F32))
                        for i in range(2)]

                w1mark = {}
                w2mark = {}
                w1_guard = [(PE, PE.n)]
                w2_guard = [(PE, PE.n), (PE, PE.n)]
                h_guard = [(PE, PE.n), (PE, PE.n)]

                def fetch_w1(e):
                    load_w3(SW1, w1e_b, w1eT[e], guard=w1_guard[0])
                    w1mark[e] = load_w3(SW1, w1o_b, w1oT[e])

                def fetch_w2(e, hlf, bi):
                    W.wait(G, *w2_guard[bi])
                    mk = sdma(SW2[bi], G,
                              w2_b[bi][:, :].rearrange("p (i n) -> p i n", n=512),
                              w2T[e, :, hlf * 512:(hlf + 1) * 512].rearrange(
                                  "(i p) n -> p i n", p=128))
                    w2mark[(e, hlf)] = (mk, bi)

                hdma = {}

                def fetch_h(e, tt, bi):
                    W.wait(Sy, *h_guard[bi])
                    W.wait(Sy, SHD)
                    mk = sdma(SH[bi], Sy,
                              hbuf[bi][:, :].rearrange("p (dc f) -> p dc f", f=512),
                              h_d[:, :, tt * 512:(tt + 1) * 512].rearrange("dc p f -> p dc f"))
                    hdma[(e, tt)] = (mk, bi)

                w1mark[0] = w1mark0
                w2mark[(0, 0)] = w2mark0
                fetch_h(0, 0, 0)
                dv_t1e = [0, 0]
                dv_t1o = [0, 0]
                ac_sl = [0, 0]
                dv_pp = [0, 0]
                pl_l2 = [0, 0]
                pl_acg = [0, 0]
                dv_gb_used = 0
                ac_gfm = AC.n
                for e in range(NE):
                    for tt in range(NQT):
                        if tt + 1 < NQT:
                            fetch_h(e, tt + 1, (tt + 1) % 2)
                        scn, hb = hdma[(e, tt)]
                        W.wait(T, AC, ac_gfm)
                        W.wait(T, AC)
                        PE.inc(T.matmul(p_gb[:, :], oh_sb[:, e * 128:(e + 1) * 128],
                                        g_fm[:, tt * 512:(tt + 1) * 512],
                                        start=True, stop=True))
                        pe_gb = PE.n
                        W.wait(Sc, PE, pe_gb)
                        W.wait(Sc, PL)
                        AC.inc(Sc.copy(gbc[:, :], p_gb[:, :]))
                        ac_gbc = AC.n
                        for ic in range(NDC):
                            bk = ic % 2
                            W.wait(T, SH[hb], scn)
                            W.wait(T, SW1, w1mark[e])
                            W.wait(T, DV, dv_t1e[bk])
                            for dc in range(NDC):
                                ins = T.matmul(p_t1e[bk][:, :],
                                               w1e_b[:, dc * D + ic * 128: dc * D + (ic + 1) * 128],
                                               hbuf[hb][:, dc * 512:(dc + 1) * 512],
                                               start=(dc == 0), stop=(dc == NDC - 1))
                            pe_e = PE.inc(ins)
                            W.wait(T, DV, dv_t1o[bk])
                            for dc in range(NDC):
                                ins = T.matmul(p_t1o[bk][:, :],
                                               w1o_b[:, dc * D + ic * 128: dc * D + (ic + 1) * 128],
                                               hbuf[hb][:, dc * 512:(dc + 1) * 512],
                                               start=(dc == 0), stop=(dc == NDC - 1))
                            pe_o = PE.inc(ins)
                            b1ec = b1e_sb[:, ic * 8 + e: ic * 8 + e + 1]
                            b1oc = b1o_sb[:, ic * 8 + e: ic * 8 + e + 1]
                            W.wait(V, PE, pe_e)
                            W.wait(V, AC, ac_sl[bk])
                            DV.inc(V.tensor_scalar(v0[bk][:, :], p_t1e[bk][:, :],
                                                   b1ec, LIMIT, ALU.add, ALU.min))
                            dv_v0 = dv_t1e[bk] = DV.n
                            W.wait(V, PE, pe_o)
                            DV.inc(V.tensor_scalar(l1[bk][:, :], p_t1o[bk][:, :],
                                                   b1oc, LIMIT, ALU.add, ALU.min))
                            dv_l1 = dv_t1o[bk] = DV.n
                            W.wait(Sc, DV, dv_v0)
                            W.wait(Sc, DV, dv_pp[bk])
                            AC.inc(Sc.activation(sl[bk][:, :], v0[bk][:, :],
                                                 AF.Silu, scale=ALPHA))
                            ac_sl[bk] = AC.n
                            W.wait(V, DV, dv_l1)
                            l2w = V.tensor_scalar(l2[bk][:, :], l1[bk][:, :],
                                                  -LIMIT, None, ALU.max)
                            pl_l2[bk] = DV.inc(l2w)
                            W.wait(V, DV, pl_l2[bk])
                            W.wait(V, AC, ac_sl[bk])
                            W.wait(V, PL, pl_acg[bk])
                            DV.inc(V.scalar_tensor_tensor(pp[bk][:, :], l2[bk][:, :],
                                                          1.0, sl[bk][:, :],
                                                          ALU.add, ALU.mult))
                            dv_pp[bk] = DV.n
                            W.wait(G, DV, dv_pp[bk])
                            W.wait(G, AC, ac_gbc)
                            PL.inc(G.tensor_tensor(acgv[:, ic, tt * 512:(tt + 1) * 512],
                                                   pp[bk][:, :], gbc[:, :], ALU.mult))
                            pl_acg[bk] = PL.n
                        h_guard[hb] = (PE, PE.n)
                    w1_guard[0] = (PE, PE.n)
                    fetch_w2(e, 1, 1 - w2mark[(e, 0)][1])
                    for hlf in range(2):
                        wgn, wb = w2mark[(e, hlf)]
                        if hlf == 1 and e + 1 < NE:
                            fetch_w1(e + 1)
                            fetch_w2(e + 1, 0, 1 - wb)
                            fetch_h(e + 1, 0, 0)
                        dv_m2 = [DV.n, DV.n]
                        W.wait(T, PL)
                        for tc in range(NTC):
                            pm = p_m2[tc % 2]
                            W.wait(T, SW2[wb], wgn)
                            W.wait(T, DV, dv_m2[tc % 2])
                            for ic in range(NDC):
                                ins = T.matmul(pm[:, :],
                                               acgv[:, ic, tc * 128:(tc + 1) * 128],
                                               w2_b[wb][:, ic * 512:(ic + 1) * 512],
                                               start=(ic == 0),
                                               stop=(ic == NDC - 1 and e != 0))
                            if e == 0:
                                ins = T.matmul(pm[:, :], g_fm[:, tc * 128:(tc + 1) * 128],
                                               b2_sb[:, hlf * 512:(hlf + 1) * 512],
                                               start=False, stop=True)
                            PE.inc(ins)
                            W.wait(V, PE)
                            DV.inc(V.tensor_tensor(xv[:, tc, hlf * 512:(hlf + 1) * 512],
                                                   pm[:, :],
                                                   xv[:, tc, hlf * 512:(hlf + 1) * 512],
                                                   ALU.add))
                            dv_m2[tc % 2] = DV.n
                        w2_guard[wb] = (PE, PE.n)

        def attention():
            with ExitStack() as ph:
                phase_barrier()
                mt, at, _ = stats_norm("attn", "a0", ph)
                utm = [ph.enter_context(nc.sbuf_tensor(f"utm{i}", [128, D], BF16))
                       for i in range(2)]
                astage = [ph.enter_context(nc.sbuf_tensor(f"astage{i}", [128, NDC * 128], BF16))
                          for i in range(2)]
                p_tr = [ph.enter_context(nc.psum_tensor(f"patr{i}", [128, 128], BF16))
                        for i in range(2)]
                W.wait(Sc, AC)
                sc_ad = [0, 0]
                dv_tr = [0, 0]
                dv_u = {}

                def emit_u(tc):
                    W.wait(V, AC)
                    W.wait(V, DV, dv_tr[0])
                    DV.inc(V.tensor_scalar(utm[tc % 2][:, :], xv[:, tc, :],
                                           mt[:, tc:tc + 1], at[:, tc:tc + 1],
                                           ALU.subtract, ALU.mult))
                    dv_u[tc] = DV.n

                emit_u(0)
                for tc in range(NTC):
                    if tc + 1 < NTC:
                        emit_u(tc + 1)
                    ast = astage[tc % 2]
                    W.wait(T, DV, dv_u[tc])
                    W.wait(T, WGC, wg0)
                    for dc in range(NDC):
                        pt = p_tr[dc % 2]
                        if dc % 2 == 0:
                            W.wait(T, DV, dv_tr[0])
                        else:
                            W.wait(T, AC, dv_tr[1])
                        PE.inc(T.transpose(pt[:, :], utm[tc % 2][:, dc * 128:(dc + 1) * 128],
                                           idb_sb[:, :]))
                        if dc % 2 == 0:
                            W.wait(V, PE)
                            W.wait(V, SAD, sc_ad[tc % 2])
                            DV.inc(V.tensor_copy(ast[:, dc * 128:(dc + 1) * 128], pt[:, :]))
                            dv_tr[0] = DV.n
                        else:
                            W.wait(Sc, PE)
                            W.wait(Sc, SAD, sc_ad[tc % 2])
                            AC.inc(Sc.copy(ast[:, dc * 128:(dc + 1) * 128], pt[:, :]))
                            dv_tr[1] = AC.n
                    W.wait(Sy, DV)
                    W.wait(Sy, AC)
                    sc_ad[tc % 2] = sdma(
                        SAD, Sy,
                        attin_d[:, :, tc * 128:(tc + 1) * 128].rearrange("dc p f -> p dc f"),
                        ast[:, :].rearrange("p (dc f) -> p dc f", f=128))

            wqe2 = ar[:, 16384:20480]
            wqo2 = ar[:, 20480:24576]
            load_w3(SATW, wqe_b, wkeT, guard=(PE, PE.n))
            load_w3(SATW, wqo_b, wkoT)
            load_w3(SATW, wv_b, wvT)
            load_w3(SATW, wqe2, wqeT)
            wk_mark = load_w3(SATW, wqo2, wqoT)

            with ExitStack() as ph:
                phase_barrier()
                abuf = [ph.enter_context(nc.sbuf_tensor(f"aab{i}", [128, NDC * 512], BF16))
                        for i in range(2)]
                cosb = ph.enter_context(nc.sbuf_tensor("cosbA", [128, 4 * 512], BF16))
                sinb = ph.enter_context(nc.sbuf_tensor("sinbA", [128, 4 * 512], BF16))
                ke_sb = [ph.enter_context(nc.sbuf_tensor(f"keS{i}", [128, 512], BF16))
                         for i in range(2)]
                ko_sb = [ph.enter_context(nc.sbuf_tensor(f"koS{i}", [128, 512], BF16))
                         for i in range(2)]
                t1 = ph.enter_context(nc.sbuf_tensor("rt1A", [128, 512], BF16))
                t2 = ph.enter_context(nc.sbuf_tensor("rt2A", [128, 512], BF16))
                t3 = ph.enter_context(nc.sbuf_tensor("rt3A", [128, 512], BF16))
                t4 = ph.enter_context(nc.sbuf_tensor("rt4A", [128, 512], BF16))
                kst = [ph.enter_context(nc.sbuf_tensor(f"kst{i}", [128, 512], BF16))
                       for i in range(4)]
                sq2 = [ph.enter_context(nc.sbuf_tensor(f"sqA{i}", [128, 512], BF16))
                       for i in range(2)]
                knacc = ph.enter_context(nc.sbuf_tensor("knacc", [128, 512], F32))
                qnacc = ph.enter_context(nc.sbuf_tensor("qnaccA", [128, 512], F32))
                kn1 = ph.enter_context(nc.sbuf_tensor("kn1", [128, 1], F32))
                vsb = [ph.enter_context(nc.sbuf_tensor(f"vsb{i}", [128, 1040], BF16))
                       for i in range(2)]
                p_ke = [ph.enter_context(nc.psum_tensor(f"pke{i}", [128, 512], F32))
                        for i in range(2)]
                p_ko = [ph.enter_context(nc.psum_tensor(f"pko{i}", [128, 512], F32))
                        for i in range(2)]
                p_kn = [ph.enter_context(nc.psum_tensor(f"pkn{i}", [128, 512], F32))
                        for i in range(2)]
                p_v = [ph.enter_context(nc.psum_tensor(f"pv2{i}", [128, 512], F32))
                       for i in range(2)]
                p_knt = p_kn[0][0:1, 0:128]

                DV.inc(V.memset(knacc[:, :], 0.0))
                DV.inc(V.memset(qnacc[:, :], 0.0))
                for i in range(2):
                    DV.inc(V.memset(
                        vsb[i][:, :].rearrange("p (h f) -> p h f", f=65)[:, :, 64:65], 1.0))

                adma = {}
                a_guard = [(PE, 0), (PE, 0)]

                def fetch_attin(st, bi):
                    W.wait(Sy, *a_guard[bi])
                    W.wait(Sy, SAD)
                    adma[st] = (sdma(SA[bi], Sy,
                                     abuf[bi][:, :].rearrange("p (dc f) -> p dc f", f=512),
                                     attin_d[:, :, st * 512:(st + 1) * 512].rearrange("dc p f -> p dc f")),
                                bi)

                tmark = [0]

                def fetch_tables(col, dv_guard):
                    W.wait(G, DV, dv_guard)
                    W.wait(G, STB)
                    STB.dma(G.dma_start(
                        cosb[:, :].rearrange("p (j f) -> p j f", f=512),
                        cosT[:, col * 512:(col + 1) * 512].rearrange("(j p) f -> p j f", p=128)))
                    STB.dma(G.dma_start(
                        sinb[:, :].rearrange("p (j f) -> p j f", f=512),
                        sinT[:, col * 512:(col + 1) * 512].rearrange("(j p) f -> p j f", p=128)))
                    tmark[0] = STB.n

                fetch_attin(0, 0)
                fetch_tables(0, 0)
                kst_guard = [0, 0, 0, 0]
                sc_vdma = [0, 0]
                dv_keko = [0, 0]
                dv_keko_rd = [0, 0]
                dv_kn = [0, 0]
                dv_v = [0, 0]
                pl_sq = [0, 0]
                for st in range(4):
                    scn, ab = adma[st]
                    if st + 1 < 4:
                        fetch_attin(st + 1, (st + 1) % 2)
                    dv_rope_st = 0
                    tasks = [(side, jc) for side in range(2) for jc in range(4)]
                    pe_proj = {}

                    def emit_proj(i):
                        side, jc = tasks[i]
                        web = wqe_b if side == 0 else wqe2
                        wob2 = wqo_b if side == 0 else wqo2
                        kbk = i % 2
                        W.wait(T, SA[ab], scn)
                        W.wait(T, SATW, wk_mark)
                        W.wait(T, AC, dv_keko[kbk])
                        for dc in range(NDC):
                            T.matmul(p_ke[kbk][:, :],
                                     web[:, dc * 512 + jc * 128: dc * 512 + (jc + 1) * 128],
                                     abuf[ab][:, dc * 512:(dc + 1) * 512],
                                     start=(dc == 0), stop=(dc == NDC - 1))
                        for dc in range(NDC):
                            ins = T.matmul(p_ko[kbk][:, :],
                                           wob2[:, dc * 512 + jc * 128: dc * 512 + (jc + 1) * 128],
                                           abuf[ab][:, dc * 512:(dc + 1) * 512],
                                           start=(dc == 0), stop=(dc == NDC - 1))
                        pe_proj[i] = PE.inc(ins)

                    emit_proj(0)
                    for i, (side, jc) in enumerate(tasks):
                        if i + 1 < len(tasks):
                            emit_proj(i + 1)
                        dest = kro_d if side == 0 else qro_d
                        nacc = knacc if side == 0 else qnacc
                        kbk = i % 2
                        co = cosb[:, jc * 512:(jc + 1) * 512]
                        si = sinb[:, jc * 512:(jc + 1) * 512]
                        klo = kst[2 * kbk]
                        khi = kst[2 * kbk + 1]
                        W.wait(Sc, PE, pe_proj[i])
                        W.wait(Sc, DV, dv_keko_rd[kbk])
                        AC.inc(Sc.copy(ke_sb[kbk][:, :], p_ke[kbk][:, :]))
                        AC.inc(Sc.copy(ko_sb[kbk][:, :], p_ko[kbk][:, :]))
                        dv_keko[kbk] = AC.n
                        W.wait(V, AC, dv_keko[kbk])
                        W.wait(V, STB, tmark[0])
                        W.wait(V, SKST[2 * kbk], kst_guard[2 * kbk])
                        W.wait(V, SKST[2 * kbk + 1], kst_guard[2 * kbk + 1])
                        V.tensor_tensor(t1[:, :], ke_sb[kbk][:, :], co, ALU.mult)
                        V.tensor_tensor(t2[:, :], ko_sb[kbk][:, :], si, ALU.mult)
                        V.tensor_tensor(t3[:, :], ke_sb[kbk][:, :], si, ALU.mult)
                        DV.inc(V.tensor_tensor(t4[:, :], ko_sb[kbk][:, :], co, ALU.mult))
                        W.wait(V, DV)
                        DV.inc(V.tensor_tensor(klo[:, :], t1[:, :], t2[:, :], ALU.subtract))
                        DV.inc(V.tensor_tensor(khi[:, :], t3[:, :], t4[:, :], ALU.add))
                        dv_rope_st = DV.n
                        dv_keko_rd[kbk] = DV.n
                        W.wait(Sy, DV, dv_rope_st)
                        kst_guard[2 * kbk] = sdma(
                            SKST[2 * kbk], Sy,
                            dest[jc, :, st * 512:(st + 1) * 512], klo[:, :])
                        kst_guard[2 * kbk + 1] = sdma(
                            SKST[2 * kbk + 1], Sy,
                            dest[4 + jc, :, st * 512:(st + 1) * 512], khi[:, :])
                        for ki, kt in ((0, klo), (1, khi)):
                            W.wait(G, DV, dv_rope_st)
                            W.wait(G, PE, pl_sq[ki])
                            PL.inc(G.tensor_tensor(sq2[ki][:, :], kt[:, :], kt[:, :],
                                                   ALU.mult))
                            pl_sq2 = PL.n
                            W.wait(T, PL, pl_sq2)
                            W.wait(T, WGC, wg0)
                            W.wait(T, DV, dv_kn[ki])
                            PE.inc(T.matmul(p_kn[ki][:, :], oAB_sb[:, :], sq2[ki][:, :],
                                            start=True, stop=True))
                            pl_sq[ki] = PE.n
                            W.wait(V, PE)
                            W.wait(V, DV, max(dv_kn))
                            DV.inc(V.tensor_tensor(nacc[:, :], p_kn[ki][:, :],
                                                   nacc[:, :], ALU.max))
                            dv_kn[ki] = DV.n
                    W.wait(V, DV)
                    DV.inc(V.reduce_max(kn1[:, :], qnacc[:, :], axis=AX.X))
                    W.wait(T, DV)
                    W.wait(T, DV, dv_kn[0])
                    PE.inc(T.transpose(p_knt, kn1[:, :], idf_sb[:, :]))
                    W.wait(V, PE)
                    DV.inc(V.reduce_max(knmax[:, 4 + st:5 + st], p_knt, axis=AX.X))
                    dv_kn[0] = DV.n
                    DV.inc(V.memset(qnacc[:, :], 0.0))
                    for sub in range(4):
                        stk128 = st * 4 + sub
                        vs = vsb[sub % 2]
                        for hlf in range(2):
                            vbk = hlf
                            W.wait(T, SA[ab], scn)
                            W.wait(T, DV, dv_v[vbk])
                            for dc in range(NDC):
                                ins = T.matmul(p_v[vbk][:, :],
                                               abuf[ab][:, dc * 512 + sub * 128: dc * 512 + (sub + 1) * 128],
                                               wv_b[:, dc * D + hlf * 512: dc * D + (hlf + 1) * 512],
                                               start=(dc == 0), stop=(dc == NDC - 1))
                            PE.inc(ins)
                            W.wait(V, PE)
                            W.wait(V, SVS[sub % 2], sc_vdma[sub % 2])
                            dst = vs[:, :].rearrange("p (h f) -> p h f", f=65)[
                                :, hlf * 8:(hlf + 1) * 8, 0:64]
                            src = p_v[vbk][:, :].rearrange("p (h f) -> p h f", f=64)
                            DV.inc(V.tensor_copy(dst, src))
                            dv_v[vbk] = DV.n
                        W.wait(Sy, DV)
                        sc_vdma[sub % 2] = sdma(
                            SVS[sub % 2], Sy,
                            v_d[:, stk128, :, :].rearrange("h p f -> p h f"),
                            vs[:, :].rearrange("p (h f) -> p h f", f=65))
                    if st + 1 < 4:
                        fetch_tables(st + 1, dv_rope_st)
                    a_guard[ab] = (PE, PE.n)
                W.wait(V, DV)
                DV.inc(V.reduce_max(kn1[:, :], knacc[:, :], axis=AX.X))
                W.wait(T, DV)
                W.wait(T, DV, dv_kn[0])
                PE.inc(T.transpose(p_knt, kn1[:, :], idf_sb[:, :]))
                W.wait(V, PE)
                DV.inc(V.reduce_max(knmax[:, 0:1], p_knt, axis=AX.X))

            wq_mark = load_w3(SATW, wo_t, woT, guard=(PE, PE.n))
            qbuf = [ar[:, 0:4096], ar[:, 4096:8192]]

            with ExitStack() as ph:
                phase_barrier()
                csc = ph.enter_context(nc.sbuf_tensor("csc", [1, 4], F32))
                bias_sb = ph.enter_context(nc.sbuf_tensor("bias_sb", [128, 1], F32))
                vbuf = [ph.enter_context(nc.sbuf_tensor(f"vbq{i}", [128, 2 * 16 * 65], BF16))
                        for i in range(2)]
                kbuf = kbuf_v
                Ebuf = Ebuf_v
                rz = [ph.enter_context(nc.sbuf_tensor(f"rz{i}", [1, 512], BF16))
                      for i in range(2)]
                zb = [ph.enter_context(nc.sbuf_tensor(f"zb{i}", [64, 512], F32))
                      for i in range(2)]
                O_sb = ph.enter_context(nc.sbuf_tensor("O_sb", [128, NDC * 512], BF16))
                p_scw = [ph.enter_context(nc.psum_tensor(f"pscw{i}", [128, 1024], F32))
                         for i in range(2)]
                p_pv = [ph.enter_context(nc.psum_tensor(f"ppv{i}", [65, 512], F32))
                        for i in range(2)]
                p_z = ph.enter_context(nc.psum_tensor("pz", [64, 512], F32))
                p_wo = ph.enter_context(nc.psum_tensor("pwo", [128, 512], F32))
                p_b = p_wo[:, 0:1]
                dv_pv = {}
                ac_zb = {}
                dv_zb_used = {}

                def heads_of(c):
                    hA = 2 * c if c < 4 else 8 + 2 * (c - 4)
                    return hA, hA + 1

                kdma = {}
                k_guard = [(PE, 0), (PE, 0)]

                def fetch_pair(qt, c, bi):
                    W.wait(Sy, *k_guard[bi])
                    for s4 in SKST:
                        W.wait(Sy, s4)
                    for s2 in SVS:
                        W.wait(Sy, s2)
                    W.wait(Sy, SKV[bi])
                    hA, hB = heads_of(c)
                    SKV[bi].dma(Sy.dma_start(kbuf[bi][:, :], kro_d[c, :, :]))
                    SKV[bi].dma(Sy.dma_start(
                        vbuf[bi][:, :].rearrange("p (hh k f) -> p hh k f", hh=2, f=65),
                        v_d[hA:hA + 2, :, :, :].rearrange("hh k p f -> p hh k f")))
                    kdma[(qt, c)] = (SKV[bi].n, bi)

                qdma = {}
                q_guard = [(PE, 0), (PE, 0)]

                def fetch_q(qt, bi):
                    W.wait(Sy, *q_guard[bi])
                    for s4 in SKST:
                        W.wait(Sy, s4)
                    qdma[qt] = (sdma(SA[bi], Sy,
                                     qbuf[bi][:, :].rearrange("p (c f) -> p c f", f=512),
                                     qro_d[:, :, qt * 512:(qt + 1) * 512].rearrange("c p f -> p c f")),
                                bi)

                fetch_q(0, 0)
                pe_wo_last = 0
                for qt in range(NQT):
                    scn_q, qb = qdma[qt]
                    if qt + 1 < NQT:
                        fetch_q(qt + 1, (qt + 1) % 2)
                    W.wait(V, AC)
                    DV.inc(V.tensor_tensor(csc[:, 1:2], knmax[:, 4 + qt:5 + qt],
                                           knmax[:, 0:1], ALU.add))
                    W.wait(V, DV)
                    DV.inc(V.tensor_scalar(csc[:, 2:3], csc[:, 1:2], -1.02 / 16.0, None,
                                           ALU.mult))
                    W.wait(T, DV)
                    W.wait(T, DV, dv_zb_used.get(0, 0))
                    PE.inc(T.matmul(p_b, ones_sb[0:1, 0:128], csc[0:1, 2:3],
                                    start=True, stop=True))
                    W.wait(Sc, PE)
                    AC.inc(Sc.copy(bias_sb[:, :], p_b))
                    W.wait(Sc, AC)

                    fetch_pair(qt, 0, 0)
                    prev_exp = {}
                    pending = [None]

                    def emit_normalize():
                        if pending[0] is None:
                            return
                        cN, pvm = pending[0]
                        pending[0] = None
                        for hh in range(2):
                            pvb = p_pv[hh]
                            W.wait(V, PE, pvm[(hh, 15)])
                            W.wait(V, AC, ac_zb.get(hh, 0))
                            with nc.allow_low_precision(
                                    reason="softmax normalizer 1/Z to bf16 — same "
                                           "class as the bf16 attention operands"):
                                DV.inc(V.reciprocal(rz[hh][:, :], pvb[64:65, :]))
                            dv_rz = DV.n
                            W.wait(T, DV, dv_rz)
                            W.wait(T, AC, ac_zb.get(1 - hh, 0))
                            PE.inc(T.matmul(p_z[:, :], ones_bf[0:1, 0:64],
                                            rz[hh][0:1, :], start=True, stop=True))
                            W.wait(Sc, PE)
                            W.wait(Sc, DV, dv_zb_used.get(hh, 0))
                            AC.inc(Sc.copy(zb[hh][:, :], p_z[:, :]))
                            ac_zb[hh] = AC.n
                            W.wait(V, AC, ac_zb[hh])
                            dst = O_sb[64 * hh:64 * hh + 64, cN * 512:(cN + 1) * 512]
                            DV.inc(V.tensor_tensor(dst, pvb[0:64, :], zb[hh][:, :],
                                                   ALU.mult))
                            dv_pv[hh] = DV.n
                            dv_zb_used[hh] = DV.n

                    for c in range(8):
                        if c + 1 < 8:
                            fetch_pair(qt, c + 1, (c + 1) % 2)
                        scn_k, kb = kdma[(qt, c)]
                        hA, hB = heads_of(c)
                        qhh = [qbuf[qb][0:64, c * 512:(c + 1) * 512],
                               qbuf[qb][64:128, c * 512:(c + 1) * 512]]
                        khh = [kbuf[kb][0:64, :], kbuf[kb][64:128, :]]
                        ac_exp = {}
                        pe_sc_mark = {}
                        pe_pv_mark = {}

                        def emit_sc(hh, kp):
                            bank = (2 * kp + hh) % 2
                            W.wait(T, SKV[kb], scn_k)
                            W.wait(T, SA[qb], scn_q)
                            if kp >= 1:
                                W.wait(T, AC, ac_exp.get((hh, kp - 1),
                                                         prev_exp.get((hh, 7), 0)))
                            else:
                                W.wait(T, AC, prev_exp.get((hh, 7), 0))
                            for half in range(2):
                                kc = 2 * kp + half
                                ins = T.matmul(p_scw[bank][:, half * 512:(half + 1) * 512],
                                               khh[hh][:, kc * 128:(kc + 1) * 128],
                                               qhh[hh], start=True, stop=True)
                            PE.inc(ins)
                            pe_sc_mark[(hh, kp)] = PE.n

                        def emit_exp(hh, kp):
                            slot = (2 * kp + hh) % 4
                            W.wait(Sc, PE, pe_sc_mark[(hh, kp)])
                            W.wait(Sc, PE, pe_pv_mark.get((hh, 2 * kp - 4),
                                                          prev_exp.get("pvdone", 0)))
                            AC.inc(Sc.activation(Ebuf[:, slot * 1024:(slot + 1) * 1024],
                                                 p_scw[(2 * kp + hh) % 2][:, :],
                                                 AF.Exp, bias=bias_sb[:, :], scale=0.125))
                            ac_exp[(hh, kp)] = AC.n

                        def emit_pv(hh, kc):
                            kp = kc // 2
                            slot = (2 * kp + hh) % 4
                            if kc == 0:
                                W.wait(T, DV, dv_pv.get(hh, 0))
                            W.wait(T, AC, ac_exp[(hh, kp)])
                            PE.inc(T.matmul(p_pv[hh][:, :],
                                            vbuf[kb][:, (hh * 16 + kc) * 65:
                                                     (hh * 16 + kc + 1) * 65],
                                            Ebuf[:, slot * 1024 + (kc % 2) * 512:
                                                 slot * 1024 + (kc % 2) * 512 + 512],
                                            start=(kc == 0), stop=(kc == 15)))
                            pe_pv_mark[(hh, kc)] = PE.n

                        for hh in range(2):
                            emit_sc(hh, 0)
                            emit_exp(hh, 0)
                        emit_normalize()
                        for kp in range(8):
                            if kp + 1 < 8:
                                for hh in range(2):
                                    emit_sc(hh, kp + 1)
                                    emit_exp(hh, kp + 1)
                            for half in range(2):
                                emit_pv(0, 2 * kp + half)
                                emit_pv(1, 2 * kp + half)
                        prev_exp = dict(ac_exp)
                        prev_exp["pvdone"] = PE.n
                        k_guard[kb] = (PE, PE.n)
                        pending[0] = (c, dict(pe_pv_mark))
                        if c == 7:
                            emit_normalize()
                    q_guard[qb] = (PE, PE.n)
                    dv_O = DV.n
                    W.wait(T, AC)
                    wo_banks = [p_wo, p_scw[0][:, 0:512]]
                    dv_wo = [DV.n, DV.n]
                    for ts in range(4):
                        tc = qt * 4 + ts
                        for hlf in range(2):
                            i2 = (ts * 2 + hlf) % 2
                            pmw = wo_banks[i2]
                            W.wait(T, DV, dv_O)
                            W.wait(T, DV, dv_wo[i2])
                            W.wait(T, SATW, wq_mark)
                            for dc in range(NDC):
                                T.matmul(pmw[:, :],
                                         O_sb[:, dc * 512 + ts * 128: dc * 512 + (ts + 1) * 128],
                                         wo_t[:, dc * D + hlf * 512: dc * D + (hlf + 1) * 512],
                                         start=(dc == 0), stop=False)
                            PE.inc(T.matmul(pmw[:, :], ones_sb[0:1, 0:128],
                                            wob_sb[0:1, hlf * 512:(hlf + 1) * 512],
                                            start=False, stop=True))
                            pe_wo_last = PE.n
                            W.wait(V, PE)
                            DV.inc(V.tensor_tensor(xv[:, tc, hlf * 512:(hlf + 1) * 512],
                                                   pmw[:, :],
                                                   xv[:, tc, hlf * 512:(hlf + 1) * 512],
                                                   ALU.add))
                            dv_wo[i2] = DV.n
                    dv_zb_used[0] = max(dv_zb_used.get(0, 0), dv_wo[0])

        ffn(0)
        if stage >= 2:
            attention()
        if stage >= 3:
            ffn(1)

        W.wait(Sy, DV)
        W.wait(Sy, AC)
        W.wait(Sy, PE)
        ST.dma(Sy.dma_start(out.rearrange("(tc p) d -> p tc d", p=128), xv))
        Sy.wait_ge(ST.h, ST.n)
        assert len(ALL_SEMS) + 2 < 106
    return nc


_CONST_CACHE = {}


def _consts():
    if _CONST_CACHE:
        return _CONST_CACHE
    pos = np.arange(S, dtype=np.float32)[:, None]
    freqs = np.exp(np.arange(0, D, 2, dtype=np.float32) *
                   np.float32(-math.log(10000.0) / D))
    ang = pos * freqs[None, :]
    _CONST_CACHE["cosT"] = np.ascontiguousarray(np.cos(ang).astype(np.float32).T)
    _CONST_CACHE["sinT"] = np.ascontiguousarray(np.sin(ang).astype(np.float32).T)
    _CONST_CACHE["idf"] = np.eye(128, dtype=np.float32)
    oab = np.zeros((128, 128), np.float32)
    oab[0:64, 0:64] = 1.0
    oab[64:128, 64:128] = 1.0
    _CONST_CACHE["onesAB"] = oab
    _CONST_CACHE["onesr"] = np.ones((1, 512), np.float32)
    oh = np.zeros((NE, NE, 128), np.float32)
    for e in range(NE):
        oh[e, e, :] = 1.0 / ALPHA
    _CONST_CACHE["ohdiv"] = oh
    return _CONST_CACHE


def _prep(inputs):
    c = _consts()
    wq = np.asarray(inputs["L0_attn_wq"], np.float32)
    wk = np.asarray(inputs["L0_attn_wk"], np.float32)
    wv = np.asarray(inputs["L0_attn_wv"], np.float32)
    wo = np.asarray(inputs["L0_attn_wo_w"], np.float32)
    m1w = np.asarray(inputs["L0_ffn_mlp1_w"], np.float32)
    m1b = np.asarray(inputs["L0_ffn_mlp1_b"], np.float32)
    m2w = np.asarray(inputs["L0_ffn_mlp2_w"], np.float32)
    shared = {
        "wqeT": np.ascontiguousarray(wq[0::2, :].T),
        "wqoT": np.ascontiguousarray(wq[1::2, :].T),
        "wkeT": np.ascontiguousarray(wk[0::2, :].T),
        "wkoT": np.ascontiguousarray(wk[1::2, :].T),
        "wvT": np.ascontiguousarray(wv.T),
        "woT": np.ascontiguousarray(wo.T),
        "wob": np.asarray(inputs["L0_attn_wo_b"], np.float32).reshape(1, D),
        "cosT": c["cosT"], "sinT": c["sinT"],
        "nwr": np.asarray(inputs["L0_ffn_norm_w"], np.float32).reshape(1, D),
        "gwT": np.ascontiguousarray(np.asarray(inputs["L0_ffn_gate_w"], np.float32).T),
        "gbr": np.asarray(inputs["L0_ffn_gate_b"], np.float32).reshape(1, NE),
        "w1eT": np.ascontiguousarray(m1w[:, 0::2, :].transpose(0, 2, 1)),
        "w1oT": np.ascontiguousarray(m1w[:, 1::2, :].transpose(0, 2, 1)),
        "w2T": np.ascontiguousarray(m2w.transpose(0, 2, 1)),
        "b1e3": np.ascontiguousarray(m1b[:, 0::2].reshape(NE, 8, 128).transpose(1, 2, 0)),
        "b1o3": np.ascontiguousarray(m1b[:, 1::2].reshape(NE, 8, 128).transpose(1, 2, 0)),
        "b2n": np.asarray(inputs["L0_ffn_mlp2_b"], np.float32),
        "idf": c["idf"], "onesAB": c["onesAB"], "onesr": c["onesr"],
        "ohdiv": c["ohdiv"],
    }
    x = np.asarray(inputs["L0_attn_x"], np.float32)
    maps = []
    for b in range(8):
        m = dict(shared)
        m["x"] = np.ascontiguousarray(x[b])
        maps.append(m)
    return maps


def kernel(**inputs):
    maps = _prep(inputs)
    nc = build_nc(stage=3)
    res = run_bass_kernel_spmd(nc, maps, core_ids=list(range(8)))
    return np.stack([np.asarray(r["out"], np.float32) for r in res.results], axis=0)


# version 12
# speedup vs baseline: 1.1964x; 1.0010x over previous
import math
from contextlib import ExitStack

import numpy as np

import concourse.bass as bass
import concourse.mybir as mybir
from concourse.bass_utils import run_bass_kernel_spmd

F32 = mybir.dt.float32
BF16 = mybir.dt.bfloat16
AF = mybir.ActivationFunctionType
ALU = mybir.AluOpType
AX = mybir.AxisListType

S = 2048
D = 1024
H = 16
NE = 8
NTC = 16
NQT = 4
NDC = 8
EPS_LN = 1e-5
EPS_RMS = float(np.finfo(np.float32).eps)
ALPHA = 1.702
LIMIT = 7.0


class Sem:
    def __init__(self, stack, nc, name):
        self.h = stack.enter_context(nc.semaphore(name))
        self.n = 0
        self.name = name

    def inc(self, instr, k=1):
        instr.then_inc(self.h, k)
        self.n += k
        return self.n

    def dma(self, instr):
        return self.inc(instr, 16)


class Waiter:
    def __init__(self):
        self.seen = {}

    def wait(self, eng, sem, at=None):
        v = sem.n if at is None else at
        if v <= 0:
            return
        key = (id(eng), sem.name)
        if self.seen.get(key, -1) >= v:
            return
        eng.wait_ge(sem.h, v)
        self.seen[key] = v


def build_nc(stage=3):
    nc = bass.Bass("TRN2", target_bir_lowering=False)

    xin = nc.dram_tensor("x", [S, D], F32, kind="ExternalInput").ap()
    wqeT = nc.dram_tensor("wqeT", [D, 512], F32, kind="ExternalInput").ap()
    wqoT = nc.dram_tensor("wqoT", [D, 512], F32, kind="ExternalInput").ap()
    wkeT = nc.dram_tensor("wkeT", [D, 512], F32, kind="ExternalInput").ap()
    wkoT = nc.dram_tensor("wkoT", [D, 512], F32, kind="ExternalInput").ap()
    wvT = nc.dram_tensor("wvT", [D, D], F32, kind="ExternalInput").ap()
    woT = nc.dram_tensor("woT", [D, D], F32, kind="ExternalInput").ap()
    wob = nc.dram_tensor("wob", [1, D], F32, kind="ExternalInput").ap()
    cosT = nc.dram_tensor("cosT", [512, S], F32, kind="ExternalInput").ap()
    sinT = nc.dram_tensor("sinT", [512, S], F32, kind="ExternalInput").ap()
    nwr = nc.dram_tensor("nwr", [1, D], F32, kind="ExternalInput").ap()
    gwT = nc.dram_tensor("gwT", [D, NE], F32, kind="ExternalInput").ap()
    gbr = nc.dram_tensor("gbr", [1, NE], F32, kind="ExternalInput").ap()
    w1eT = nc.dram_tensor("w1eT", [NE, D, D], F32, kind="ExternalInput").ap()
    w1oT = nc.dram_tensor("w1oT", [NE, D, D], F32, kind="ExternalInput").ap()
    w2T = nc.dram_tensor("w2T", [NE, D, D], F32, kind="ExternalInput").ap()
    b1e3 = nc.dram_tensor("b1e3", [8, 128, NE], F32, kind="ExternalInput").ap()
    b1o3 = nc.dram_tensor("b1o3", [8, 128, NE], F32, kind="ExternalInput").ap()
    b2n = nc.dram_tensor("b2n", [NE, D], F32, kind="ExternalInput").ap()
    idf = nc.dram_tensor("idf", [128, 128], F32, kind="ExternalInput").ap()
    onesAB = nc.dram_tensor("onesAB", [128, 128], F32, kind="ExternalInput").ap()
    onesr = nc.dram_tensor("onesr", [1, 512], F32, kind="ExternalInput").ap()
    ohdiv = nc.dram_tensor("ohdiv", [NE, NE, 128], F32, kind="ExternalInput").ap()

    out = nc.dram_tensor("out", [S, D], F32, kind="ExternalOutput").ap()

    attin_d = nc.dram_tensor("attin_d", [NDC, 128, S], BF16, kind="Internal").ap()
    h_d = nc.dram_tensor("h_d", [NDC, 128, S], BF16, kind="Internal").ap()
    kro_d = nc.dram_tensor("kro_d", [NDC, 128, S], BF16, kind="Internal").ap()
    qro_d = nc.dram_tensor("qro_d", [NDC, 128, S], BF16, kind="Internal").ap()
    v_d = nc.dram_tensor("v_d", [H, NTC, 128, 65], BF16, kind="Internal").ap()

    T = nc.tensor
    V = nc.vector
    Sc = nc.scalar
    G = nc.gpsimd
    Sy = nc.sync
    W = Waiter()

    stk = ExitStack()
    with stk:
        LD = Sem(stk, nc, "LD")
        WGC = Sem(stk, nc, "WGC")
        PE = Sem(stk, nc, "PE")
        DV = Sem(stk, nc, "DV")
        AC = Sem(stk, nc, "AC")
        PL = Sem(stk, nc, "PL")
        ST = Sem(stk, nc, "ST")
        SH = [Sem(stk, nc, f"SH{i}") for i in range(2)]
        SA = [Sem(stk, nc, f"SA{i}") for i in range(2)]
        SKV = [Sem(stk, nc, f"SKV{i}") for i in range(2)]
        SW1 = Sem(stk, nc, "SW1")
        SW2 = [Sem(stk, nc, f"SW2{i}") for i in range(2)]
        STB = Sem(stk, nc, "STB")
        SATW = Sem(stk, nc, "SATW")
        SKST = [Sem(stk, nc, f"SKST{i}") for i in range(4)]
        SVS = [Sem(stk, nc, f"SVS{i}") for i in range(2)]
        SHD = Sem(stk, nc, "SHD")
        SAD = Sem(stk, nc, "SAD")
        ALL_SEMS = ([LD, WGC, PE, DV, AC, PL, SW1, STB, SATW, SHD, SAD]
                    + SH + SA + SKV + SW2 + SKST + SVS)

        def sdma(sem, q, out_ap, in_ap):
            W.wait(q, sem)
            sem.dma(q.dma_start(out_ap, in_ap))
            return sem.n

        x_sb = stk.enter_context(nc.sbuf_tensor("x_sb", [128, NTC * D], F32))
        xv = x_sb[:, :].rearrange("p (tc d) -> p tc d", tc=NTC)
        g_fm = stk.enter_context(nc.sbuf_tensor("g_fm", [NE, S], F32))
        nw_bc = stk.enter_context(nc.sbuf_tensor("nw_bc", [128, D], F32))
        b2_sb = stk.enter_context(nc.sbuf_tensor("b2_sb", [NE, D], F32))
        b1e_sb = stk.enter_context(nc.sbuf_tensor("b1e_sb", [128, 64], F32))
        b1o_sb = stk.enter_context(nc.sbuf_tensor("b1o_sb", [128, 64], F32))
        gb_sb = stk.enter_context(nc.sbuf_tensor("gb_sb", [1, NE], F32))
        wob_sb = stk.enter_context(nc.sbuf_tensor("wob_sb", [1, D], F32))
        ones_sb = stk.enter_context(nc.sbuf_tensor("ones_sb", [1, 128], F32))
        oAB_sb = stk.enter_context(nc.sbuf_tensor("oAB_sb", [128, 128], BF16))
        idf_sb = stk.enter_context(nc.sbuf_tensor("idf_sb", [128, 128], F32))
        idb_sb = stk.enter_context(nc.sbuf_tensor("idb_sb", [128, 128], BF16))
        oh_sb = stk.enter_context(nc.sbuf_tensor("oh_sb", [NE, NE * 128], F32))
        gwT_sb = stk.enter_context(nc.sbuf_tensor("gwT_sb", [128, NDC * NE], BF16))
        knmax = stk.enter_context(nc.sbuf_tensor("knmax", [1, 8], F32))
        arena = stk.enter_context(nc.sbuf_tensor("arena", [128, 24576], BF16))
        ar = arena[:, :]
        w1e_b = ar[:, 0:8192]
        w1o_b = ar[:, 8192:16384]
        w2_b = [ar[:, 16384:20480], ar[:, 20480:24576]]
        wqe_b = ar[:, 0:4096]
        wqo_b = ar[:, 4096:8192]
        wv_b = ar[:, 8192:16384]
        wo_t = ar[:, 16384:24576]
        Ebuf_v = ar[:, 8192:12288]
        kbuf_v = [ar[:, 12288:14336], ar[:, 14336:16384]]

        LD.dma(Sy.dma_start(x_sb[:, :].rearrange("p (tc d) -> p tc d", tc=NTC),
                            xin.rearrange("(tc p) d -> p tc d", p=128)))
        LD.dma(Sy.dma_start(b2_sb[:, :], b2n))
        LD.dma(Sy.dma_start(b1e_sb[:, :].rearrange("p (i e) -> p i e", e=NE),
                            b1e3.rearrange("i p e -> p i e")))
        LD.dma(Sy.dma_start(b1o_sb[:, :].rearrange("p (i e) -> p i e", e=NE),
                            b1o3.rearrange("i p e -> p i e")))
        LD.dma(Sy.dma_start(gb_sb[:, :], gbr))
        LD.dma(Sy.dma_start(wob_sb[:, :], wob))
        LD.dma(Sy.dma_start(ones_sb[:, :], onesr[0:1, 0:128]))
        LD.dma(Sy.dma_start(idf_sb[:, :], idf))
        LD.dma(Sy.dma_start(oh_sb[:, :].rearrange("p (e c) -> p e c", e=NE),
                            ohdiv.rearrange("e p c -> p e c")))
        ones_bf = stk.enter_context(nc.sbuf_tensor("ones_bf", [1, 64], BF16))
        WGC.dma(G.dma_start(ones_bf[:, :], onesr[0:1, 0:64]))
        WGC.dma(G.dma_start(oAB_sb[:, :], onesAB))
        WGC.dma(G.dma_start(idb_sb[:, :], idf))
        WGC.dma(G.dma_start(gwT_sb[:, :].rearrange("p (dc e) -> p dc e", e=NE),
                            gwT.rearrange("(dc p) e -> p dc e", p=128)))
        ld0, wg0 = LD.n, WGC.n

        with nc.psum_tensor("p_nw", [128, 512], F32) as p_nw, \
                nc.sbuf_tensor("nwr_sb", [1, D], F32) as nwr_sb:
            LD.dma(Sy.dma_start(nwr_sb[:, :], nwr))
            ld0 = LD.n
            for hlf in range(2):
                W.wait(T, LD, ld0)
                W.wait(T, AC)
                PE.inc(T.matmul(p_nw[:, :], ones_sb[0:1, 0:128],
                                nwr_sb[0:1, hlf * 512:(hlf + 1) * 512],
                                start=True, stop=True))
                W.wait(Sc, PE)
                AC.inc(Sc.copy(nw_bc[:, hlf * 512:(hlf + 1) * 512], p_nw[:, :]))

        def phase_barrier():
            for eng in (T, V, Sc, G, Sy):
                for sem in ALL_SEMS:
                    W.wait(eng, sem)

        def load_w3(sem, dst, src, guard=None):
            if guard is not None:
                W.wait(G, *guard)
            n = src.shape[-1]
            return sdma(sem, G, dst[:, :].rearrange("p (dc n) -> p dc n", n=n),
                        src.rearrange("(dc p) n -> p dc n", p=128))

        def stats_norm(kind, tag, ph):
            raw = ph.enter_context(nc.sbuf_tensor(f"raw{tag}", [128, NTC * 12], F32))
            agg = ph.enter_context(nc.sbuf_tensor(f"agg{tag}", [128, NTC * 4], F32))
            mt = ph.enter_context(nc.sbuf_tensor(f"m{tag}", [128, NTC], F32))
            at = ph.enter_context(nc.sbuf_tensor(f"a{tag}", [128, NTC], F32))
            r2 = ph.enter_context(nc.sbuf_tensor(f"r2{tag}", [128, NTC], F32))
            tmp = ph.enter_context(nc.sbuf_tensor(f"tmq{tag}", [128, NTC * 4], F32))
            W.wait(V, LD, ld0)
            W.wait(V, DV)
            W.wait(V, PL)
            for tc in range(NTC):
                V.bn_stats(raw[:, tc * 12:tc * 12 + 6], xv[:, tc, 0:512])
                ins = V.bn_stats(raw[:, tc * 12 + 6:tc * 12 + 12], xv[:, tc, 512:1024])
                if tc == NTC - 1:
                    DV.inc(ins)
            W.wait(V, DV)
            for tc in range(NTC):
                V.bn_aggr(agg[:, tc * 4:tc * 4 + 2], raw[:, tc * 12:tc * 12 + 6])
                ins = V.bn_aggr(agg[:, tc * 4 + 2:tc * 4 + 4],
                                raw[:, tc * 12 + 6:tc * 12 + 12])
                if tc == NTC - 1:
                    DV.inc(ins)
            W.wait(V, DV)
            av = agg[:, :].rearrange("p (tc k) -> p tc k", k=4)
            m1, v1, m2, v2 = av[:, :, 0], av[:, :, 1], av[:, :, 2], av[:, :, 3]
            tv = tmp[:, :].rearrange("p (k tc) -> p k tc", k=4)
            t0, t1, t2, t3 = tv[:, 0, :], tv[:, 1, :], tv[:, 2, :], tv[:, 3, :]
            V.tensor_tensor(t0, m1, m2, ALU.add)
            V.tensor_tensor(t1, m1, m2, ALU.subtract)
            DV.inc(V.tensor_tensor(t2, v1, v2, ALU.add))
            W.wait(V, DV)
            V.tensor_scalar(mt[:, :], t0, 0.5, None, ALU.mult)
            DV.inc(V.tensor_tensor(t3, t1, t1, ALU.mult))
            W.wait(V, DV)
            DV.inc(V.scalar_tensor_tensor(t0, t3, 0.5, t2, ALU.mult, ALU.add))
            W.wait(V, DV)
            DV.inc(V.tensor_scalar(t1, t0, 0.5, EPS_LN, ALU.mult, ALU.add))
            W.wait(Sc, DV)
            AC.inc(Sc.activation(t2, t1, AF.Ln))
            W.wait(Sc, AC)
            AC.inc(Sc.activation(at[:, :], t2, AF.Exp, scale=-0.5))
            if kind == "ffn":
                W.wait(V, AC)
                V.tensor_scalar(t0, t1, -EPS_LN, None, ALU.add)
                DV.inc(V.tensor_tensor(t2, at[:, :], at[:, :], ALU.mult))
                W.wait(V, DV)
                DV.inc(V.tensor_tensor(t3, t0, t2, ALU.mult))
                W.wait(V, DV)
                DV.inc(V.tensor_scalar(t3, t3, EPS_RMS, None, ALU.add))
                W.wait(Sc, DV)
                AC.inc(Sc.activation(t0, t3, AF.Ln))
                W.wait(Sc, AC)
                AC.inc(Sc.activation(r2[:, :], t0, AF.Exp, scale=-0.5))
            return mt, at, r2

        def ffn(inv):
            tag = f"f{inv}"
            load_w3(SW1, w1e_b, w1eT[0], guard=(PE, PE.n))
            w1mark0 = load_w3(SW1, w1o_b, w1oT[0])
            W.wait(G, SW2[0])
            mk = sdma(SW2[0], G,
                      w2_b[0][:, :].rearrange("p (i n) -> p i n", n=512),
                      w2T[0, :, 0:512].rearrange("(i p) n -> p i n", p=128))
            w2mark0 = (mk, 0)
            with ExitStack() as ph:
                phase_barrier()
                mt, at, r2 = stats_norm("ffn", tag, ph)
                u_sb = [ph.enter_context(nc.sbuf_tensor(f"u{tag}{i}", [128, D], F32))
                        for i in range(2)]
                h1_sb = [ph.enter_context(nc.sbuf_tensor(f"h1{tag}{i}", [128, D], F32))
                         for i in range(2)]
                htm = [ph.enter_context(nc.sbuf_tensor(f"htm{tag}{i}", [128, D], BF16))
                       for i in range(2)]
                gbuf = [ph.enter_context(nc.sbuf_tensor(f"gb2{tag}{i}", [128, D], BF16))
                        for i in range(2)]
                hstage = [ph.enter_context(nc.sbuf_tensor(f"hs{tag}{i}", [128, NDC * 128], BF16))
                          for i in range(2)]
                lg_sb = ph.enter_context(nc.sbuf_tensor(f"lg{tag}", [128, 8], F32))
                top8 = ph.enter_context(nc.sbuf_tensor(f"top{tag}", [128, 8], F32))
                gtmp = ph.enter_context(nc.sbuf_tensor(f"gq{tag}", [128, 24], F32))
                g_tm = ph.enter_context(nc.sbuf_tensor(f"gtm{tag}", [128, 8], F32))
                p_tr = [ph.enter_context(nc.psum_tensor(f"ptr{tag}{i}", [128, 128], BF16))
                        for i in range(2)]
                p_lg = ph.enter_context(nc.psum_tensor(f"plg{tag}", [128, 8], F32))
                p_gt = ph.enter_context(nc.psum_tensor(f"pgt{tag}", [8, 128], F32))

                W.wait(Sc, AC)
                sc_hd = [0, 0]
                dv_tr = [0, 0]
                shd_mark = {}
                pl_h = {}

                def emit_norm(tc):
                    i = tc % 2
                    W.wait(V, AC)
                    W.wait(V, DV, dv_tr[0])
                    W.wait(V, PL)
                    DV.inc(V.tensor_scalar(u_sb[i][:, :], xv[:, tc, :], mt[:, tc:tc + 1],
                                           at[:, tc:tc + 1], ALU.subtract, ALU.mult))
                    W.wait(V, DV)
                    DV.inc(V.tensor_scalar(h1_sb[i][:, :], u_sb[i][:, :],
                                           r2[:, tc:tc + 1], None, ALU.mult))
                    dv_h1 = DV.n
                    W.wait(G, DV, dv_h1)
                    PL.inc(G.tensor_tensor(htm[i][:, :], h1_sb[i][:, :], nw_bc[:, :],
                                           ALU.mult))
                    pl_h[tc] = PL.n
                    DV.inc(V.scalar_tensor_tensor(xv[:, tc, :], u_sb[i][:, :], 0.5,
                                                  xv[:, tc, :], ALU.mult, ALU.add))

                emit_norm(0)
                for tc in range(NTC):
                    if tc + 1 < NTC:
                        emit_norm(tc + 1)
                    i = tc % 2
                    hs = hstage[i]
                    W.wait(T, PL, pl_h[tc])
                    W.wait(T, WGC, wg0)
                    for dc in range(NDC):
                        pt = p_tr[dc % 2]
                        if dc % 2 == 0:
                            W.wait(T, DV, dv_tr[0])
                        else:
                            W.wait(T, AC, dv_tr[1])
                        PE.inc(T.transpose(pt[:, :], htm[i][:, dc * 128:(dc + 1) * 128],
                                           idb_sb[:, :]))
                        if dc % 2 == 0:
                            W.wait(V, PE)
                            W.wait(V, SHD, sc_hd[i])
                            DV.inc(V.tensor_copy(hs[:, dc * 128:(dc + 1) * 128], pt[:, :]))
                            dv_tr[0] = DV.n
                        else:
                            W.wait(Sc, PE)
                            W.wait(Sc, SHD, sc_hd[i])
                            AC.inc(Sc.copy(hs[:, dc * 128:(dc + 1) * 128], pt[:, :]))
                            dv_tr[1] = AC.n
                    dv_h = DV.n
                    ac_h = AC.n
                    W.wait(Sc, DV, dv_h)
                    W.wait(Sc, AC, ac_h)
                    sc_hd[i] = sdma(
                        SHD, Sc,
                        h_d[:, :, tc * 128:(tc + 1) * 128].rearrange("dc p f -> p dc f"),
                        hs[:, :].rearrange("p (dc f) -> p dc f", f=128))
                    shd_mark[tc] = sc_hd[i]

                gdma = {}
                g_guard = [0, 0]

                def fetch_g(tc, bi):
                    W.wait(Sy, PE, g_guard[bi])
                    W.wait(Sy, SHD, shd_mark[tc])
                    gdma[tc] = (sdma(SH[bi], Sy,
                                     gbuf[bi][:, :].rearrange("p (dc f) -> p dc f", f=128),
                                     h_d[:, :, tc * 128:(tc + 1) * 128].rearrange("dc p f -> p dc f")),
                                bi)

                fetch_g(0, 0)
                for tc in range(NTC):
                    if tc + 1 < NTC:
                        fetch_g(tc + 1, (tc + 1) % 2)
                    scn_g, gi = gdma[tc]
                    W.wait(T, SH[gi], scn_g)
                    W.wait(T, AC)
                    W.wait(T, AC)
                    for dc in range(NDC):
                        T.matmul(p_lg[:, :], gbuf[gi][:, dc * 128:(dc + 1) * 128],
                                 gwT_sb[:, dc * NE:(dc + 1) * NE],
                                 start=(dc == 0), stop=False)
                    PE.inc(T.matmul(p_lg[:, :], ones_sb[0:1, 0:128], gb_sb[0:1, :],
                                    start=False, stop=True))
                    g_guard[gi] = PE.n
                    W.wait(Sc, PE)
                    W.wait(Sc, DV)
                    AC.inc(Sc.copy(lg_sb[:, :], p_lg[:, :]))
                    W.wait(V, AC)
                    DV.inc(V.max(top8[:, :], lg_sb[:, :]))
                    W.wait(V, DV)
                    DV.inc(V.tensor_tensor(gtmp[:, 0:1], top8[:, 0:1], top8[:, 1:2],
                                           ALU.subtract))
                    W.wait(Sc, DV)
                    AC.inc(Sc.activation(gtmp[:, 1:2], gtmp[:, 0:1], AF.Sigmoid))
                    W.wait(V, AC)
                    V.tensor_scalar(gtmp[:, 2:3], gtmp[:, 1:2], 0.5, None, ALU.mult)
                    DV.inc(V.tensor_scalar(gtmp[:, 3:4], gtmp[:, 1:2], -0.5, 0.5,
                                           ALU.mult, ALU.add))
                    V.tensor_scalar(gtmp[:, 8:16], lg_sb[:, :], top8[:, 0:1], None,
                                    ALU.is_equal)
                    DV.inc(V.tensor_scalar(gtmp[:, 16:24], lg_sb[:, :], top8[:, 1:2],
                                           None, ALU.is_equal))
                    W.wait(V, DV)
                    V.tensor_scalar(gtmp[:, 8:16], gtmp[:, 8:16], gtmp[:, 2:3], None,
                                    ALU.mult)
                    DV.inc(V.tensor_scalar(gtmp[:, 16:24], gtmp[:, 16:24], gtmp[:, 3:4],
                                           None, ALU.mult))
                    W.wait(V, DV)
                    DV.inc(V.tensor_tensor(g_tm[:, :], gtmp[:, 8:16], gtmp[:, 16:24],
                                           ALU.add))
                    W.wait(T, DV)
                    PE.inc(T.transpose(p_gt[:, :], g_tm[:, :], idf_sb[:, :]))
                    W.wait(Sc, PE)
                    AC.inc(Sc.copy(g_fm[:, tc * 128:(tc + 1) * 128], p_gt[:, :]))

            with ExitStack() as ph:
                phase_barrier()
                hbuf = [ph.enter_context(nc.sbuf_tensor(f"hb{tag}{i}", [128, NDC * 512], BF16))
                        for i in range(2)]
                acg = ph.enter_context(nc.sbuf_tensor(f"acg{tag}", [128, NDC * S], BF16))
                acgv = acg[:, :].rearrange("p (i t) -> p i t", i=NDC)
                v0 = [ph.enter_context(nc.sbuf_tensor(f"v0{tag}{i}", [128, 512], F32))
                      for i in range(2)]
                sl = [ph.enter_context(nc.sbuf_tensor(f"sl{tag}{i}", [128, 512], F32))
                      for i in range(2)]
                l1 = [ph.enter_context(nc.sbuf_tensor(f"l1{tag}{i}", [128, 512], F32))
                      for i in range(2)]
                l2s = ph.enter_context(nc.sbuf_tensor(f"l2{tag}", [128, 512], F32))
                l2 = [l2s, l2s]
                pp = [ph.enter_context(nc.sbuf_tensor(f"pq{tag}{i}", [128, 512], F32))
                      for i in range(2)]
                gbc = ph.enter_context(nc.sbuf_tensor(f"gbc{tag}", [128, 512], F32))
                p_t1e = [ph.enter_context(nc.psum_tensor(f"pt1e{tag}{i}", [128, 512], F32))
                         for i in range(2)]
                p_t1o = [ph.enter_context(nc.psum_tensor(f"pt1o{tag}{i}", [128, 512], F32))
                         for i in range(2)]
                p_gb = ph.enter_context(nc.psum_tensor(f"pgb{tag}", [128, 512], F32))
                p_m2 = [ph.enter_context(nc.psum_tensor(f"pm2{tag}{i}", [128, 512], F32))
                        for i in range(2)]

                w1mark = {}
                w2mark = {}
                w1_guard = [(PE, PE.n)]
                w2_guard = [(PE, PE.n), (PE, PE.n)]
                h_guard = [(PE, PE.n), (PE, PE.n)]

                def fetch_w1(e):
                    load_w3(SW1, w1e_b, w1eT[e], guard=w1_guard[0])
                    w1mark[e] = load_w3(SW1, w1o_b, w1oT[e])

                def fetch_w2(e, hlf, bi):
                    W.wait(G, *w2_guard[bi])
                    mk = sdma(SW2[bi], G,
                              w2_b[bi][:, :].rearrange("p (i n) -> p i n", n=512),
                              w2T[e, :, hlf * 512:(hlf + 1) * 512].rearrange(
                                  "(i p) n -> p i n", p=128))
                    w2mark[(e, hlf)] = (mk, bi)

                hdma = {}

                def fetch_h(e, tt, bi):
                    W.wait(Sy, *h_guard[bi])
                    W.wait(Sy, SHD)
                    mk = sdma(SH[bi], Sy,
                              hbuf[bi][:, :].rearrange("p (dc f) -> p dc f", f=512),
                              h_d[:, :, tt * 512:(tt + 1) * 512].rearrange("dc p f -> p dc f"))
                    hdma[(e, tt)] = (mk, bi)

                w1mark[0] = w1mark0
                w2mark[(0, 0)] = w2mark0
                fetch_h(0, 0, 0)
                dv_t1e = [0, 0]
                dv_t1o = [0, 0]
                ac_sl = [0, 0]
                dv_pp = [0, 0]
                pl_l2 = [0, 0]
                pl_acg = [0, 0]
                dv_gb_used = 0
                ac_gfm = AC.n
                for e in range(NE):
                    for tt in range(NQT):
                        if tt + 1 < NQT:
                            fetch_h(e, tt + 1, (tt + 1) % 2)
                        scn, hb = hdma[(e, tt)]
                        W.wait(T, AC, ac_gfm)
                        W.wait(T, AC)
                        PE.inc(T.matmul(p_gb[:, :], oh_sb[:, e * 128:(e + 1) * 128],
                                        g_fm[:, tt * 512:(tt + 1) * 512],
                                        start=True, stop=True))
                        pe_gb = PE.n
                        W.wait(Sc, PE, pe_gb)
                        W.wait(Sc, PL)
                        AC.inc(Sc.copy(gbc[:, :], p_gb[:, :]))
                        ac_gbc = AC.n
                        for ic in range(NDC):
                            bk = ic % 2
                            W.wait(T, SH[hb], scn)
                            W.wait(T, SW1, w1mark[e])
                            W.wait(T, DV, dv_t1e[bk])
                            for dc in range(NDC):
                                ins = T.matmul(p_t1e[bk][:, :],
                                               w1e_b[:, dc * D + ic * 128: dc * D + (ic + 1) * 128],
                                               hbuf[hb][:, dc * 512:(dc + 1) * 512],
                                               start=(dc == 0), stop=(dc == NDC - 1))
                            pe_e = PE.inc(ins)
                            W.wait(T, DV, dv_t1o[bk])
                            for dc in range(NDC):
                                ins = T.matmul(p_t1o[bk][:, :],
                                               w1o_b[:, dc * D + ic * 128: dc * D + (ic + 1) * 128],
                                               hbuf[hb][:, dc * 512:(dc + 1) * 512],
                                               start=(dc == 0), stop=(dc == NDC - 1))
                            pe_o = PE.inc(ins)
                            b1ec = b1e_sb[:, ic * 8 + e: ic * 8 + e + 1]
                            b1oc = b1o_sb[:, ic * 8 + e: ic * 8 + e + 1]
                            W.wait(V, PE, pe_e)
                            W.wait(V, AC, ac_sl[bk])
                            DV.inc(V.tensor_scalar(v0[bk][:, :], p_t1e[bk][:, :],
                                                   b1ec, LIMIT, ALU.add, ALU.min))
                            dv_v0 = dv_t1e[bk] = DV.n
                            W.wait(V, PE, pe_o)
                            DV.inc(V.tensor_scalar(l1[bk][:, :], p_t1o[bk][:, :],
                                                   b1oc, LIMIT, ALU.add, ALU.min))
                            dv_l1 = dv_t1o[bk] = DV.n
                            W.wait(Sc, DV, dv_v0)
                            W.wait(Sc, DV, dv_pp[bk])
                            AC.inc(Sc.activation(sl[bk][:, :], v0[bk][:, :],
                                                 AF.Silu, scale=ALPHA))
                            ac_sl[bk] = AC.n
                            W.wait(V, DV, dv_l1)
                            l2w = V.tensor_scalar(l2[bk][:, :], l1[bk][:, :],
                                                  -LIMIT, None, ALU.max)
                            pl_l2[bk] = DV.inc(l2w)
                            W.wait(V, DV, pl_l2[bk])
                            W.wait(V, AC, ac_sl[bk])
                            W.wait(V, PL, pl_acg[bk])
                            DV.inc(V.scalar_tensor_tensor(pp[bk][:, :], l2[bk][:, :],
                                                          1.0, sl[bk][:, :],
                                                          ALU.add, ALU.mult))
                            dv_pp[bk] = DV.n
                            W.wait(G, DV, dv_pp[bk])
                            W.wait(G, AC, ac_gbc)
                            PL.inc(G.tensor_tensor(acgv[:, ic, tt * 512:(tt + 1) * 512],
                                                   pp[bk][:, :], gbc[:, :], ALU.mult))
                            pl_acg[bk] = PL.n
                        h_guard[hb] = (PE, PE.n)
                    w1_guard[0] = (PE, PE.n)
                    fetch_w2(e, 1, 1 - w2mark[(e, 0)][1])
                    for hlf in range(2):
                        wgn, wb = w2mark[(e, hlf)]
                        if hlf == 1 and e + 1 < NE:
                            fetch_w1(e + 1)
                            fetch_w2(e + 1, 0, 1 - wb)
                            fetch_h(e + 1, 0, 0)
                        dv_m2 = [DV.n, DV.n]
                        W.wait(T, PL)
                        for tc in range(NTC):
                            pm = p_m2[tc % 2]
                            W.wait(T, SW2[wb], wgn)
                            W.wait(T, DV, dv_m2[tc % 2])
                            for ic in range(NDC):
                                ins = T.matmul(pm[:, :],
                                               acgv[:, ic, tc * 128:(tc + 1) * 128],
                                               w2_b[wb][:, ic * 512:(ic + 1) * 512],
                                               start=(ic == 0),
                                               stop=(ic == NDC - 1 and e != 0))
                            if e == 0:
                                ins = T.matmul(pm[:, :], g_fm[:, tc * 128:(tc + 1) * 128],
                                               b2_sb[:, hlf * 512:(hlf + 1) * 512],
                                               start=False, stop=True)
                            PE.inc(ins)
                            W.wait(V, PE)
                            DV.inc(V.tensor_tensor(xv[:, tc, hlf * 512:(hlf + 1) * 512],
                                                   pm[:, :],
                                                   xv[:, tc, hlf * 512:(hlf + 1) * 512],
                                                   ALU.add))
                            dv_m2[tc % 2] = DV.n
                        w2_guard[wb] = (PE, PE.n)

        def attention():
            with ExitStack() as ph:
                phase_barrier()
                mt, at, _ = stats_norm("attn", "a0", ph)
                utm = [ph.enter_context(nc.sbuf_tensor(f"utm{i}", [128, D], BF16))
                       for i in range(2)]
                astage = [ph.enter_context(nc.sbuf_tensor(f"astage{i}", [128, NDC * 128], BF16))
                          for i in range(2)]
                p_tr = [ph.enter_context(nc.psum_tensor(f"patr{i}", [128, 128], BF16))
                        for i in range(2)]
                W.wait(Sc, AC)
                sc_ad = [0, 0]
                dv_tr = [0, 0]
                dv_u = {}

                def emit_u(tc):
                    W.wait(V, AC)
                    W.wait(V, DV, dv_tr[0])
                    DV.inc(V.tensor_scalar(utm[tc % 2][:, :], xv[:, tc, :],
                                           mt[:, tc:tc + 1], at[:, tc:tc + 1],
                                           ALU.subtract, ALU.mult))
                    dv_u[tc] = DV.n

                emit_u(0)
                for tc in range(NTC):
                    if tc + 1 < NTC:
                        emit_u(tc + 1)
                    ast = astage[tc % 2]
                    W.wait(T, DV, dv_u[tc])
                    W.wait(T, WGC, wg0)
                    for dc in range(NDC):
                        pt = p_tr[dc % 2]
                        if dc % 2 == 0:
                            W.wait(T, DV, dv_tr[0])
                        else:
                            W.wait(T, AC, dv_tr[1])
                        PE.inc(T.transpose(pt[:, :], utm[tc % 2][:, dc * 128:(dc + 1) * 128],
                                           idb_sb[:, :]))
                        if dc % 2 == 0:
                            W.wait(V, PE)
                            W.wait(V, SAD, sc_ad[tc % 2])
                            DV.inc(V.tensor_copy(ast[:, dc * 128:(dc + 1) * 128], pt[:, :]))
                            dv_tr[0] = DV.n
                        else:
                            W.wait(Sc, PE)
                            W.wait(Sc, SAD, sc_ad[tc % 2])
                            AC.inc(Sc.copy(ast[:, dc * 128:(dc + 1) * 128], pt[:, :]))
                            dv_tr[1] = AC.n
                    W.wait(Sy, DV)
                    W.wait(Sy, AC)
                    sc_ad[tc % 2] = sdma(
                        SAD, Sy,
                        attin_d[:, :, tc * 128:(tc + 1) * 128].rearrange("dc p f -> p dc f"),
                        ast[:, :].rearrange("p (dc f) -> p dc f", f=128))

            wqe2 = ar[:, 16384:20480]
            wqo2 = ar[:, 20480:24576]
            load_w3(SATW, wqe_b, wkeT, guard=(PE, PE.n))
            load_w3(SATW, wqo_b, wkoT)
            load_w3(SATW, wv_b, wvT)
            load_w3(SATW, wqe2, wqeT)
            wk_mark = load_w3(SATW, wqo2, wqoT)

            with ExitStack() as ph:
                phase_barrier()
                abuf = [ph.enter_context(nc.sbuf_tensor(f"aab{i}", [128, NDC * 512], BF16))
                        for i in range(2)]
                cosb = ph.enter_context(nc.sbuf_tensor("cosbA", [128, 4 * 512], BF16))
                sinb = ph.enter_context(nc.sbuf_tensor("sinbA", [128, 4 * 512], BF16))
                ke_sb = [ph.enter_context(nc.sbuf_tensor(f"keS{i}", [128, 512], BF16))
                         for i in range(2)]
                ko_sb = [ph.enter_context(nc.sbuf_tensor(f"koS{i}", [128, 512], BF16))
                         for i in range(2)]
                t1 = ph.enter_context(nc.sbuf_tensor("rt1A", [128, 512], BF16))
                t2 = ph.enter_context(nc.sbuf_tensor("rt2A", [128, 512], BF16))
                t3 = ph.enter_context(nc.sbuf_tensor("rt3A", [128, 512], BF16))
                t4 = ph.enter_context(nc.sbuf_tensor("rt4A", [128, 512], BF16))
                kst = [ph.enter_context(nc.sbuf_tensor(f"kst{i}", [128, 512], BF16))
                       for i in range(4)]
                sq2 = [ph.enter_context(nc.sbuf_tensor(f"sqA{i}", [128, 512], BF16))
                       for i in range(2)]
                knacc = ph.enter_context(nc.sbuf_tensor("knacc", [128, 512], F32))
                qnacc = ph.enter_context(nc.sbuf_tensor("qnaccA", [128, 512], F32))
                kn1 = ph.enter_context(nc.sbuf_tensor("kn1", [128, 1], F32))
                vsb = [ph.enter_context(nc.sbuf_tensor(f"vsb{i}", [128, 1040], BF16))
                       for i in range(2)]
                p_ke = [ph.enter_context(nc.psum_tensor(f"pke{i}", [128, 512], F32))
                        for i in range(2)]
                p_ko = [ph.enter_context(nc.psum_tensor(f"pko{i}", [128, 512], F32))
                        for i in range(2)]
                p_kn = [ph.enter_context(nc.psum_tensor(f"pkn{i}", [128, 512], F32))
                        for i in range(2)]
                p_v = [ph.enter_context(nc.psum_tensor(f"pv2{i}", [128, 512], F32))
                       for i in range(2)]
                p_knt = p_kn[0][0:1, 0:128]

                DV.inc(V.memset(knacc[:, :], 0.0))
                DV.inc(V.memset(qnacc[:, :], 0.0))
                for i in range(2):
                    DV.inc(V.memset(
                        vsb[i][:, :].rearrange("p (h f) -> p h f", f=65)[:, :, 64:65], 1.0))

                adma = {}
                a_guard = [(PE, 0), (PE, 0)]

                def fetch_attin(st, bi):
                    W.wait(Sy, *a_guard[bi])
                    W.wait(Sy, SAD)
                    adma[st] = (sdma(SA[bi], Sy,
                                     abuf[bi][:, :].rearrange("p (dc f) -> p dc f", f=512),
                                     attin_d[:, :, st * 512:(st + 1) * 512].rearrange("dc p f -> p dc f")),
                                bi)

                tmark = [0]

                def fetch_tables(col, dv_guard):
                    W.wait(G, DV, dv_guard)
                    W.wait(G, STB)
                    STB.dma(G.dma_start(
                        cosb[:, :].rearrange("p (j f) -> p j f", f=512),
                        cosT[:, col * 512:(col + 1) * 512].rearrange("(j p) f -> p j f", p=128)))
                    STB.dma(G.dma_start(
                        sinb[:, :].rearrange("p (j f) -> p j f", f=512),
                        sinT[:, col * 512:(col + 1) * 512].rearrange("(j p) f -> p j f", p=128)))
                    tmark[0] = STB.n

                fetch_attin(0, 0)
                fetch_tables(0, 0)
                kst_guard = [0, 0, 0, 0]
                sc_vdma = [0, 0]
                dv_keko = [0, 0]
                dv_keko_rd = [0, 0]
                dv_kn = [0, 0]
                dv_v = [0, 0]
                pl_sq = [0, 0]
                for st in range(4):
                    scn, ab = adma[st]
                    if st + 1 < 4:
                        fetch_attin(st + 1, (st + 1) % 2)
                    dv_rope_st = 0
                    tasks = [(side, jc) for side in range(2) for jc in range(4)]
                    pe_proj = {}

                    def emit_proj(i):
                        side, jc = tasks[i]
                        web = wqe_b if side == 0 else wqe2
                        wob2 = wqo_b if side == 0 else wqo2
                        kbk = i % 2
                        W.wait(T, SA[ab], scn)
                        W.wait(T, SATW, wk_mark)
                        W.wait(T, AC, dv_keko[kbk])
                        for dc in range(NDC):
                            T.matmul(p_ke[kbk][:, :],
                                     web[:, dc * 512 + jc * 128: dc * 512 + (jc + 1) * 128],
                                     abuf[ab][:, dc * 512:(dc + 1) * 512],
                                     start=(dc == 0), stop=(dc == NDC - 1))
                        for dc in range(NDC):
                            ins = T.matmul(p_ko[kbk][:, :],
                                           wob2[:, dc * 512 + jc * 128: dc * 512 + (jc + 1) * 128],
                                           abuf[ab][:, dc * 512:(dc + 1) * 512],
                                           start=(dc == 0), stop=(dc == NDC - 1))
                        pe_proj[i] = PE.inc(ins)

                    emit_proj(0)
                    for i, (side, jc) in enumerate(tasks):
                        if i + 1 < len(tasks):
                            emit_proj(i + 1)
                        dest = kro_d if side == 0 else qro_d
                        nacc = knacc if side == 0 else qnacc
                        kbk = i % 2
                        co = cosb[:, jc * 512:(jc + 1) * 512]
                        si = sinb[:, jc * 512:(jc + 1) * 512]
                        klo = kst[2 * kbk]
                        khi = kst[2 * kbk + 1]
                        W.wait(Sc, PE, pe_proj[i])
                        W.wait(Sc, DV, dv_keko_rd[kbk])
                        AC.inc(Sc.copy(ke_sb[kbk][:, :], p_ke[kbk][:, :]))
                        AC.inc(Sc.copy(ko_sb[kbk][:, :], p_ko[kbk][:, :]))
                        dv_keko[kbk] = AC.n
                        W.wait(V, AC, dv_keko[kbk])
                        W.wait(V, STB, tmark[0])
                        W.wait(V, SKST[2 * kbk], kst_guard[2 * kbk])
                        W.wait(V, SKST[2 * kbk + 1], kst_guard[2 * kbk + 1])
                        V.tensor_tensor(t1[:, :], ke_sb[kbk][:, :], co, ALU.mult)
                        V.tensor_tensor(t2[:, :], ko_sb[kbk][:, :], si, ALU.mult)
                        V.tensor_tensor(t3[:, :], ke_sb[kbk][:, :], si, ALU.mult)
                        DV.inc(V.tensor_tensor(t4[:, :], ko_sb[kbk][:, :], co, ALU.mult))
                        W.wait(V, DV)
                        DV.inc(V.tensor_tensor(klo[:, :], t1[:, :], t2[:, :], ALU.subtract))
                        DV.inc(V.tensor_tensor(khi[:, :], t3[:, :], t4[:, :], ALU.add))
                        dv_rope_st = DV.n
                        dv_keko_rd[kbk] = DV.n
                        W.wait(Sc, DV, dv_rope_st)
                        kst_guard[2 * kbk] = sdma(
                            SKST[2 * kbk], Sc,
                            dest[jc, :, st * 512:(st + 1) * 512], klo[:, :])
                        kst_guard[2 * kbk + 1] = sdma(
                            SKST[2 * kbk + 1], Sc,
                            dest[4 + jc, :, st * 512:(st + 1) * 512], khi[:, :])
                        for ki, kt in ((0, klo), (1, khi)):
                            W.wait(G, DV, dv_rope_st)
                            W.wait(G, PE, pl_sq[ki])
                            PL.inc(G.tensor_tensor(sq2[ki][:, :], kt[:, :], kt[:, :],
                                                   ALU.mult))
                            pl_sq2 = PL.n
                            W.wait(T, PL, pl_sq2)
                            W.wait(T, WGC, wg0)
                            W.wait(T, DV, dv_kn[ki])
                            PE.inc(T.matmul(p_kn[ki][:, :], oAB_sb[:, :], sq2[ki][:, :],
                                            start=True, stop=True))
                            pl_sq[ki] = PE.n
                            W.wait(V, PE)
                            W.wait(V, DV, max(dv_kn))
                            DV.inc(V.tensor_tensor(nacc[:, :], p_kn[ki][:, :],
                                                   nacc[:, :], ALU.max))
                            dv_kn[ki] = DV.n
                    W.wait(V, DV)
                    DV.inc(V.reduce_max(kn1[:, :], qnacc[:, :], axis=AX.X))
                    W.wait(T, DV)
                    W.wait(T, DV, dv_kn[0])
                    PE.inc(T.transpose(p_knt, kn1[:, :], idf_sb[:, :]))
                    W.wait(V, PE)
                    DV.inc(V.reduce_max(knmax[:, 4 + st:5 + st], p_knt, axis=AX.X))
                    dv_kn[0] = DV.n
                    DV.inc(V.memset(qnacc[:, :], 0.0))
                    for sub in range(4):
                        stk128 = st * 4 + sub
                        vs = vsb[sub % 2]
                        for hlf in range(2):
                            vbk = hlf
                            W.wait(T, SA[ab], scn)
                            W.wait(T, DV, dv_v[vbk])
                            for dc in range(NDC):
                                ins = T.matmul(p_v[vbk][:, :],
                                               abuf[ab][:, dc * 512 + sub * 128: dc * 512 + (sub + 1) * 128],
                                               wv_b[:, dc * D + hlf * 512: dc * D + (hlf + 1) * 512],
                                               start=(dc == 0), stop=(dc == NDC - 1))
                            PE.inc(ins)
                            W.wait(V, PE)
                            W.wait(V, SVS[sub % 2], sc_vdma[sub % 2])
                            dst = vs[:, :].rearrange("p (h f) -> p h f", f=65)[
                                :, hlf * 8:(hlf + 1) * 8, 0:64]
                            src = p_v[vbk][:, :].rearrange("p (h f) -> p h f", f=64)
                            DV.inc(V.tensor_copy(dst, src))
                            dv_v[vbk] = DV.n
                        W.wait(Sc, DV)
                        sc_vdma[sub % 2] = sdma(
                            SVS[sub % 2], Sc,
                            v_d[:, stk128, :, :].rearrange("h p f -> p h f"),
                            vs[:, :].rearrange("p (h f) -> p h f", f=65))
                    if st + 1 < 4:
                        fetch_tables(st + 1, dv_rope_st)
                    a_guard[ab] = (PE, PE.n)
                W.wait(V, DV)
                DV.inc(V.reduce_max(kn1[:, :], knacc[:, :], axis=AX.X))
                W.wait(T, DV)
                W.wait(T, DV, dv_kn[0])
                PE.inc(T.transpose(p_knt, kn1[:, :], idf_sb[:, :]))
                W.wait(V, PE)
                DV.inc(V.reduce_max(knmax[:, 0:1], p_knt, axis=AX.X))

            wq_mark = load_w3(SATW, wo_t, woT, guard=(PE, PE.n))
            qbuf = [ar[:, 0:4096], ar[:, 4096:8192]]

            with ExitStack() as ph:
                phase_barrier()
                csc = ph.enter_context(nc.sbuf_tensor("csc", [1, 4], F32))
                bias_sb = ph.enter_context(nc.sbuf_tensor("bias_sb", [128, 1], F32))
                vbuf = [ph.enter_context(nc.sbuf_tensor(f"vbq{i}", [128, 2 * 16 * 65], BF16))
                        for i in range(2)]
                kbuf = kbuf_v
                Ebuf = Ebuf_v
                rz = [ph.enter_context(nc.sbuf_tensor(f"rz{i}", [1, 512], BF16))
                      for i in range(2)]
                zb = [ph.enter_context(nc.sbuf_tensor(f"zb{i}", [64, 512], F32))
                      for i in range(2)]
                O_sb = ph.enter_context(nc.sbuf_tensor("O_sb", [128, NDC * 512], BF16))
                p_scw = [ph.enter_context(nc.psum_tensor(f"pscw{i}", [128, 1024], F32))
                         for i in range(2)]
                p_pv = [ph.enter_context(nc.psum_tensor(f"ppv{i}", [65, 512], F32))
                        for i in range(2)]
                p_z = ph.enter_context(nc.psum_tensor("pz", [64, 512], F32))
                p_wo = ph.enter_context(nc.psum_tensor("pwo", [128, 512], F32))
                p_b = p_wo[:, 0:1]
                dv_pv = {}
                ac_zb = {}
                dv_zb_used = {}

                def heads_of(c):
                    hA = 2 * c if c < 4 else 8 + 2 * (c - 4)
                    return hA, hA + 1

                kdma = {}
                k_guard = [(PE, 0), (PE, 0)]

                def fetch_pair(qt, c, bi):
                    W.wait(Sy, *k_guard[bi])
                    for s4 in SKST:
                        W.wait(Sy, s4)
                    for s2 in SVS:
                        W.wait(Sy, s2)
                    W.wait(Sy, SKV[bi])
                    hA, hB = heads_of(c)
                    SKV[bi].dma(Sy.dma_start(kbuf[bi][:, :], kro_d[c, :, :]))
                    SKV[bi].dma(Sy.dma_start(
                        vbuf[bi][:, :].rearrange("p (hh k f) -> p hh k f", hh=2, f=65),
                        v_d[hA:hA + 2, :, :, :].rearrange("hh k p f -> p hh k f")))
                    kdma[(qt, c)] = (SKV[bi].n, bi)

                qdma = {}
                q_guard = [(PE, 0), (PE, 0)]

                def fetch_q(qt, bi):
                    W.wait(Sy, *q_guard[bi])
                    for s4 in SKST:
                        W.wait(Sy, s4)
                    qdma[qt] = (sdma(SA[bi], Sy,
                                     qbuf[bi][:, :].rearrange("p (c f) -> p c f", f=512),
                                     qro_d[:, :, qt * 512:(qt + 1) * 512].rearrange("c p f -> p c f")),
                                bi)

                fetch_q(0, 0)
                pe_wo_last = 0
                for qt in range(NQT):
                    scn_q, qb = qdma[qt]
                    if qt + 1 < NQT:
                        fetch_q(qt + 1, (qt + 1) % 2)
                    W.wait(V, AC)
                    DV.inc(V.tensor_tensor(csc[:, 1:2], knmax[:, 4 + qt:5 + qt],
                                           knmax[:, 0:1], ALU.add))
                    W.wait(V, DV)
                    DV.inc(V.tensor_scalar(csc[:, 2:3], csc[:, 1:2], -1.02 / 16.0, None,
                                           ALU.mult))
                    W.wait(T, DV)
                    W.wait(T, DV, dv_zb_used.get(0, 0))
                    PE.inc(T.matmul(p_b, ones_sb[0:1, 0:128], csc[0:1, 2:3],
                                    start=True, stop=True))
                    W.wait(Sc, PE)
                    AC.inc(Sc.copy(bias_sb[:, :], p_b))
                    W.wait(Sc, AC)

                    fetch_pair(qt, 0, 0)
                    prev_exp = {}
                    pending = [None]

                    def emit_normalize():
                        if pending[0] is None:
                            return
                        cN, pvm = pending[0]
                        pending[0] = None
                        for hh in range(2):
                            pvb = p_pv[hh]
                            W.wait(V, PE, pvm[(hh, 15)])
                            W.wait(V, AC, ac_zb.get(hh, 0))
                            with nc.allow_low_precision(
                                    reason="softmax normalizer 1/Z to bf16 — same "
                                           "class as the bf16 attention operands"):
                                DV.inc(V.reciprocal(rz[hh][:, :], pvb[64:65, :]))
                            dv_rz = DV.n
                            W.wait(T, DV, dv_rz)
                            W.wait(T, AC, ac_zb.get(1 - hh, 0))
                            PE.inc(T.matmul(p_z[:, :], ones_bf[0:1, 0:64],
                                            rz[hh][0:1, :], start=True, stop=True))
                            W.wait(Sc, PE)
                            W.wait(Sc, DV, dv_zb_used.get(hh, 0))
                            AC.inc(Sc.copy(zb[hh][:, :], p_z[:, :]))
                            ac_zb[hh] = AC.n
                            W.wait(V, AC, ac_zb[hh])
                            dst = O_sb[64 * hh:64 * hh + 64, cN * 512:(cN + 1) * 512]
                            DV.inc(V.tensor_tensor(dst, pvb[0:64, :], zb[hh][:, :],
                                                   ALU.mult))
                            dv_pv[hh] = DV.n
                            dv_zb_used[hh] = DV.n

                    for c in range(8):
                        if c + 1 < 8:
                            fetch_pair(qt, c + 1, (c + 1) % 2)
                        scn_k, kb = kdma[(qt, c)]
                        hA, hB = heads_of(c)
                        qhh = [qbuf[qb][0:64, c * 512:(c + 1) * 512],
                               qbuf[qb][64:128, c * 512:(c + 1) * 512]]
                        khh = [kbuf[kb][0:64, :], kbuf[kb][64:128, :]]
                        ac_exp = {}
                        pe_sc_mark = {}
                        pe_pv_mark = {}

                        def emit_sc(hh, kp):
                            bank = (2 * kp + hh) % 2
                            W.wait(T, SKV[kb], scn_k)
                            W.wait(T, SA[qb], scn_q)
                            if kp >= 1:
                                W.wait(T, AC, ac_exp.get((hh, kp - 1),
                                                         prev_exp.get((hh, 7), 0)))
                            else:
                                W.wait(T, AC, prev_exp.get((hh, 7), 0))
                            for half in range(2):
                                kc = 2 * kp + half
                                ins = T.matmul(p_scw[bank][:, half * 512:(half + 1) * 512],
                                               khh[hh][:, kc * 128:(kc + 1) * 128],
                                               qhh[hh], start=True, stop=True)
                            PE.inc(ins)
                            pe_sc_mark[(hh, kp)] = PE.n

                        def emit_exp(hh, kp):
                            slot = (2 * kp + hh) % 4
                            W.wait(Sc, PE, pe_sc_mark[(hh, kp)])
                            W.wait(Sc, PE, pe_pv_mark.get((hh, 2 * kp - 4),
                                                          prev_exp.get("pvdone", 0)))
                            AC.inc(Sc.activation(Ebuf[:, slot * 1024:(slot + 1) * 1024],
                                                 p_scw[(2 * kp + hh) % 2][:, :],
                                                 AF.Exp, bias=bias_sb[:, :], scale=0.125))
                            ac_exp[(hh, kp)] = AC.n

                        def emit_pv(hh, kc):
                            kp = kc // 2
                            slot = (2 * kp + hh) % 4
                            if kc == 0:
                                W.wait(T, DV, dv_pv.get(hh, 0))
                            W.wait(T, AC, ac_exp[(hh, kp)])
                            PE.inc(T.matmul(p_pv[hh][:, :],
                                            vbuf[kb][:, (hh * 16 + kc) * 65:
                                                     (hh * 16 + kc + 1) * 65],
                                            Ebuf[:, slot * 1024 + (kc % 2) * 512:
                                                 slot * 1024 + (kc % 2) * 512 + 512],
                                            start=(kc == 0), stop=(kc == 15)))
                            pe_pv_mark[(hh, kc)] = PE.n

                        for hh in range(2):
                            emit_sc(hh, 0)
                            emit_exp(hh, 0)
                        emit_normalize()
                        for kp in range(8):
                            if kp + 1 < 8:
                                for hh in range(2):
                                    emit_sc(hh, kp + 1)
                                    emit_exp(hh, kp + 1)
                            for half in range(2):
                                emit_pv(0, 2 * kp + half)
                                emit_pv(1, 2 * kp + half)
                        prev_exp = dict(ac_exp)
                        prev_exp["pvdone"] = PE.n
                        k_guard[kb] = (PE, PE.n)
                        pending[0] = (c, dict(pe_pv_mark))
                        if c == 7:
                            emit_normalize()
                    q_guard[qb] = (PE, PE.n)
                    dv_O = DV.n
                    W.wait(T, AC)
                    wo_banks = [p_wo, p_scw[0][:, 0:512]]
                    dv_wo = [DV.n, DV.n]
                    for ts in range(4):
                        tc = qt * 4 + ts
                        for hlf in range(2):
                            i2 = (ts * 2 + hlf) % 2
                            pmw = wo_banks[i2]
                            W.wait(T, DV, dv_O)
                            W.wait(T, DV, dv_wo[i2])
                            W.wait(T, SATW, wq_mark)
                            for dc in range(NDC):
                                T.matmul(pmw[:, :],
                                         O_sb[:, dc * 512 + ts * 128: dc * 512 + (ts + 1) * 128],
                                         wo_t[:, dc * D + hlf * 512: dc * D + (hlf + 1) * 512],
                                         start=(dc == 0), stop=False)
                            PE.inc(T.matmul(pmw[:, :], ones_sb[0:1, 0:128],
                                            wob_sb[0:1, hlf * 512:(hlf + 1) * 512],
                                            start=False, stop=True))
                            pe_wo_last = PE.n
                            W.wait(V, PE)
                            DV.inc(V.tensor_tensor(xv[:, tc, hlf * 512:(hlf + 1) * 512],
                                                   pmw[:, :],
                                                   xv[:, tc, hlf * 512:(hlf + 1) * 512],
                                                   ALU.add))
                            dv_wo[i2] = DV.n
                    dv_zb_used[0] = max(dv_zb_used.get(0, 0), dv_wo[0])

        ffn(0)
        if stage >= 2:
            attention()
        if stage >= 3:
            ffn(1)

        W.wait(Sy, DV)
        W.wait(Sy, AC)
        W.wait(Sy, PE)
        ST.dma(Sy.dma_start(out.rearrange("(tc p) d -> p tc d", p=128), xv))
        Sy.wait_ge(ST.h, ST.n)
        assert len(ALL_SEMS) + 2 < 106
    return nc


_CONST_CACHE = {}


def _consts():
    if _CONST_CACHE:
        return _CONST_CACHE
    pos = np.arange(S, dtype=np.float32)[:, None]
    freqs = np.exp(np.arange(0, D, 2, dtype=np.float32) *
                   np.float32(-math.log(10000.0) / D))
    ang = pos * freqs[None, :]
    _CONST_CACHE["cosT"] = np.ascontiguousarray(np.cos(ang).astype(np.float32).T)
    _CONST_CACHE["sinT"] = np.ascontiguousarray(np.sin(ang).astype(np.float32).T)
    _CONST_CACHE["idf"] = np.eye(128, dtype=np.float32)
    oab = np.zeros((128, 128), np.float32)
    oab[0:64, 0:64] = 1.0
    oab[64:128, 64:128] = 1.0
    _CONST_CACHE["onesAB"] = oab
    _CONST_CACHE["onesr"] = np.ones((1, 512), np.float32)
    oh = np.zeros((NE, NE, 128), np.float32)
    for e in range(NE):
        oh[e, e, :] = 1.0 / ALPHA
    _CONST_CACHE["ohdiv"] = oh
    return _CONST_CACHE


def _prep(inputs):
    c = _consts()
    wq = np.asarray(inputs["L0_attn_wq"], np.float32)
    wk = np.asarray(inputs["L0_attn_wk"], np.float32)
    wv = np.asarray(inputs["L0_attn_wv"], np.float32)
    wo = np.asarray(inputs["L0_attn_wo_w"], np.float32)
    m1w = np.asarray(inputs["L0_ffn_mlp1_w"], np.float32)
    m1b = np.asarray(inputs["L0_ffn_mlp1_b"], np.float32)
    m2w = np.asarray(inputs["L0_ffn_mlp2_w"], np.float32)
    shared = {
        "wqeT": np.ascontiguousarray(wq[0::2, :].T),
        "wqoT": np.ascontiguousarray(wq[1::2, :].T),
        "wkeT": np.ascontiguousarray(wk[0::2, :].T),
        "wkoT": np.ascontiguousarray(wk[1::2, :].T),
        "wvT": np.ascontiguousarray(wv.T),
        "woT": np.ascontiguousarray(wo.T),
        "wob": np.asarray(inputs["L0_attn_wo_b"], np.float32).reshape(1, D),
        "cosT": c["cosT"], "sinT": c["sinT"],
        "nwr": np.asarray(inputs["L0_ffn_norm_w"], np.float32).reshape(1, D),
        "gwT": np.ascontiguousarray(np.asarray(inputs["L0_ffn_gate_w"], np.float32).T),
        "gbr": np.asarray(inputs["L0_ffn_gate_b"], np.float32).reshape(1, NE),
        "w1eT": np.ascontiguousarray(m1w[:, 0::2, :].transpose(0, 2, 1)),
        "w1oT": np.ascontiguousarray(m1w[:, 1::2, :].transpose(0, 2, 1)),
        "w2T": np.ascontiguousarray(m2w.transpose(0, 2, 1)),
        "b1e3": np.ascontiguousarray(m1b[:, 0::2].reshape(NE, 8, 128).transpose(1, 2, 0)),
        "b1o3": np.ascontiguousarray(m1b[:, 1::2].reshape(NE, 8, 128).transpose(1, 2, 0)),
        "b2n": np.asarray(inputs["L0_ffn_mlp2_b"], np.float32),
        "idf": c["idf"], "onesAB": c["onesAB"], "onesr": c["onesr"],
        "ohdiv": c["ohdiv"],
    }
    x = np.asarray(inputs["L0_attn_x"], np.float32)
    maps = []
    for b in range(8):
        m = dict(shared)
        m["x"] = np.ascontiguousarray(x[b])
        maps.append(m)
    return maps


def kernel(**inputs):
    maps = _prep(inputs)
    nc = build_nc(stage=3)
    res = run_bass_kernel_spmd(nc, maps, core_ids=list(range(8)))
    return np.stack([np.asarray(r["out"], np.float32) for r in res.results], axis=0)


# version 13
# speedup vs baseline: 1.2037x; 1.0061x over previous
import math
from contextlib import ExitStack

import numpy as np

import concourse.bass as bass
import concourse.mybir as mybir
from concourse.bass_utils import run_bass_kernel_spmd

F32 = mybir.dt.float32
BF16 = mybir.dt.bfloat16
AF = mybir.ActivationFunctionType
ALU = mybir.AluOpType
AX = mybir.AxisListType

S = 2048
D = 1024
H = 16
NE = 8
NTC = 16
NQT = 4
NDC = 8
EPS_LN = 1e-5
EPS_RMS = float(np.finfo(np.float32).eps)
ALPHA = 1.702
LIMIT = 7.0


class Sem:
    def __init__(self, stack, nc, name):
        self.h = stack.enter_context(nc.semaphore(name))
        self.n = 0
        self.name = name

    def inc(self, instr, k=1):
        instr.then_inc(self.h, k)
        self.n += k
        return self.n

    def dma(self, instr):
        return self.inc(instr, 16)


class Waiter:
    def __init__(self):
        self.seen = {}

    def wait(self, eng, sem, at=None):
        v = sem.n if at is None else at
        if v <= 0:
            return
        key = (id(eng), sem.name)
        if self.seen.get(key, -1) >= v:
            return
        eng.wait_ge(sem.h, v)
        self.seen[key] = v


def build_nc(stage=3):
    nc = bass.Bass("TRN2", target_bir_lowering=False)

    xin = nc.dram_tensor("x", [S, D], F32, kind="ExternalInput").ap()
    wqeT = nc.dram_tensor("wqeT", [D, 512], F32, kind="ExternalInput").ap()
    wqoT = nc.dram_tensor("wqoT", [D, 512], F32, kind="ExternalInput").ap()
    wkeT = nc.dram_tensor("wkeT", [D, 512], F32, kind="ExternalInput").ap()
    wkoT = nc.dram_tensor("wkoT", [D, 512], F32, kind="ExternalInput").ap()
    wvT = nc.dram_tensor("wvT", [D, D], F32, kind="ExternalInput").ap()
    woT = nc.dram_tensor("woT", [D, D], F32, kind="ExternalInput").ap()
    wob = nc.dram_tensor("wob", [1, D], F32, kind="ExternalInput").ap()
    cosT = nc.dram_tensor("cosT", [512, S], F32, kind="ExternalInput").ap()
    sinT = nc.dram_tensor("sinT", [512, S], F32, kind="ExternalInput").ap()
    nwr = nc.dram_tensor("nwr", [1, D], F32, kind="ExternalInput").ap()
    gwT = nc.dram_tensor("gwT", [D, NE], F32, kind="ExternalInput").ap()
    gbr = nc.dram_tensor("gbr", [1, NE], F32, kind="ExternalInput").ap()
    w1eT = nc.dram_tensor("w1eT", [NE, D, D], F32, kind="ExternalInput").ap()
    w1oT = nc.dram_tensor("w1oT", [NE, D, D], F32, kind="ExternalInput").ap()
    w2T = nc.dram_tensor("w2T", [NE, D, D], F32, kind="ExternalInput").ap()
    b1e3 = nc.dram_tensor("b1e3", [8, 128, NE], F32, kind="ExternalInput").ap()
    b1o3 = nc.dram_tensor("b1o3", [8, 128, NE], F32, kind="ExternalInput").ap()
    b2n = nc.dram_tensor("b2n", [NE, D], F32, kind="ExternalInput").ap()
    idf = nc.dram_tensor("idf", [128, 128], F32, kind="ExternalInput").ap()
    onesAB = nc.dram_tensor("onesAB", [128, 128], F32, kind="ExternalInput").ap()
    onesr = nc.dram_tensor("onesr", [1, 512], F32, kind="ExternalInput").ap()
    ohdiv = nc.dram_tensor("ohdiv", [NE, NE, 128], F32, kind="ExternalInput").ap()

    out = nc.dram_tensor("out", [S, D], F32, kind="ExternalOutput").ap()

    attin_d = nc.dram_tensor("attin_d", [NDC, 128, S], BF16, kind="Internal").ap()
    h_d = nc.dram_tensor("h_d", [NDC, 128, S], BF16, kind="Internal").ap()
    kro_d = nc.dram_tensor("kro_d", [NDC, 128, S], BF16, kind="Internal").ap()
    qro_d = nc.dram_tensor("qro_d", [NDC, 128, S], BF16, kind="Internal").ap()
    v_d = nc.dram_tensor("v_d", [H, NTC, 128, 65], BF16, kind="Internal").ap()

    T = nc.tensor
    V = nc.vector
    Sc = nc.scalar
    G = nc.gpsimd
    Sy = nc.sync
    W = Waiter()

    stk = ExitStack()
    with stk:
        LD = Sem(stk, nc, "LD")
        WGC = Sem(stk, nc, "WGC")
        PE = Sem(stk, nc, "PE")
        DV = Sem(stk, nc, "DV")
        AC = Sem(stk, nc, "AC")
        PL = Sem(stk, nc, "PL")
        ST = Sem(stk, nc, "ST")
        SH = [Sem(stk, nc, f"SH{i}") for i in range(2)]
        SA = [Sem(stk, nc, f"SA{i}") for i in range(2)]
        SKV = [Sem(stk, nc, f"SKV{i}") for i in range(2)]
        SW1 = Sem(stk, nc, "SW1")
        SW2 = [Sem(stk, nc, f"SW2{i}") for i in range(2)]
        STB = Sem(stk, nc, "STB")
        SATW = Sem(stk, nc, "SATW")
        SKST = [Sem(stk, nc, f"SKST{i}") for i in range(4)]
        SVS = [Sem(stk, nc, f"SVS{i}") for i in range(2)]
        SHD = Sem(stk, nc, "SHD")
        SAD = Sem(stk, nc, "SAD")
        ALL_SEMS = ([LD, WGC, PE, DV, AC, PL, SW1, STB, SATW, SHD, SAD]
                    + SH + SA + SKV + SW2 + SKST + SVS)

        def sdma(sem, q, out_ap, in_ap):
            W.wait(q, sem)
            sem.dma(q.dma_start(out_ap, in_ap))
            return sem.n

        x_sb = stk.enter_context(nc.sbuf_tensor("x_sb", [128, NTC * D], F32))
        xv = x_sb[:, :].rearrange("p (tc d) -> p tc d", tc=NTC)
        g_fm = stk.enter_context(nc.sbuf_tensor("g_fm", [NE, S], F32))
        nw_bc = stk.enter_context(nc.sbuf_tensor("nw_bc", [128, D], F32))
        b2_sb = stk.enter_context(nc.sbuf_tensor("b2_sb", [NE, D], F32))
        b1e_sb = stk.enter_context(nc.sbuf_tensor("b1e_sb", [128, 64], F32))
        b1o_sb = stk.enter_context(nc.sbuf_tensor("b1o_sb", [128, 64], F32))
        gb_sb = stk.enter_context(nc.sbuf_tensor("gb_sb", [1, NE], F32))
        wob_sb = stk.enter_context(nc.sbuf_tensor("wob_sb", [1, D], F32))
        ones_sb = stk.enter_context(nc.sbuf_tensor("ones_sb", [1, 128], F32))
        oAB_sb = stk.enter_context(nc.sbuf_tensor("oAB_sb", [128, 128], BF16))
        idf_sb = stk.enter_context(nc.sbuf_tensor("idf_sb", [128, 128], F32))
        idb_sb = stk.enter_context(nc.sbuf_tensor("idb_sb", [128, 128], BF16))
        oh_sb = stk.enter_context(nc.sbuf_tensor("oh_sb", [NE, NE * 128], F32))
        gwT_sb = stk.enter_context(nc.sbuf_tensor("gwT_sb", [128, NDC * NE], BF16))
        knmax = stk.enter_context(nc.sbuf_tensor("knmax", [1, 8], F32))
        arena = stk.enter_context(nc.sbuf_tensor("arena", [128, 24576], BF16))
        ar = arena[:, :]
        w1e_b = ar[:, 0:8192]
        w1o_b = ar[:, 8192:16384]
        w2_b = [ar[:, 16384:20480], ar[:, 20480:24576]]
        wqe_b = ar[:, 0:4096]
        wqo_b = ar[:, 4096:8192]
        wv_b = ar[:, 8192:16384]
        wo_t = ar[:, 16384:24576]
        Ebuf_v = ar[:, 8192:12288]
        kbuf_v = [ar[:, 12288:14336], ar[:, 14336:16384]]

        LD.dma(Sy.dma_start(x_sb[:, :].rearrange("p (tc d) -> p tc d", tc=NTC),
                            xin.rearrange("(tc p) d -> p tc d", p=128)))
        LD.dma(Sy.dma_start(b2_sb[:, :], b2n))
        LD.dma(Sy.dma_start(b1e_sb[:, :].rearrange("p (i e) -> p i e", e=NE),
                            b1e3.rearrange("i p e -> p i e")))
        LD.dma(Sy.dma_start(b1o_sb[:, :].rearrange("p (i e) -> p i e", e=NE),
                            b1o3.rearrange("i p e -> p i e")))
        LD.dma(Sy.dma_start(gb_sb[:, :], gbr))
        LD.dma(Sy.dma_start(wob_sb[:, :], wob))
        LD.dma(Sy.dma_start(ones_sb[:, :], onesr[0:1, 0:128]))
        LD.dma(Sy.dma_start(idf_sb[:, :], idf))
        LD.dma(Sy.dma_start(oh_sb[:, :].rearrange("p (e c) -> p e c", e=NE),
                            ohdiv.rearrange("e p c -> p e c")))
        ones_bf = stk.enter_context(nc.sbuf_tensor("ones_bf", [1, 64], BF16))
        WGC.dma(G.dma_start(ones_bf[:, :], onesr[0:1, 0:64]))
        WGC.dma(G.dma_start(oAB_sb[:, :], onesAB))
        WGC.dma(G.dma_start(idb_sb[:, :], idf))
        WGC.dma(G.dma_start(gwT_sb[:, :].rearrange("p (dc e) -> p dc e", e=NE),
                            gwT.rearrange("(dc p) e -> p dc e", p=128)))
        ld0, wg0 = LD.n, WGC.n

        with nc.psum_tensor("p_nw", [128, 512], F32) as p_nw, \
                nc.sbuf_tensor("nwr_sb", [1, D], F32) as nwr_sb:
            LD.dma(Sy.dma_start(nwr_sb[:, :], nwr))
            ld0 = LD.n
            for hlf in range(2):
                W.wait(T, LD, ld0)
                W.wait(T, AC)
                PE.inc(T.matmul(p_nw[:, :], ones_sb[0:1, 0:128],
                                nwr_sb[0:1, hlf * 512:(hlf + 1) * 512],
                                start=True, stop=True))
                W.wait(Sc, PE)
                AC.inc(Sc.copy(nw_bc[:, hlf * 512:(hlf + 1) * 512], p_nw[:, :]))

        def phase_barrier():
            for eng in (T, V, Sc, G, Sy):
                for sem in ALL_SEMS:
                    W.wait(eng, sem)

        def load_w3(sem, dst, src, guard=None):
            if guard is not None:
                W.wait(G, *guard)
            n = src.shape[-1]
            return sdma(sem, G, dst[:, :].rearrange("p (dc n) -> p dc n", n=n),
                        src.rearrange("(dc p) n -> p dc n", p=128))

        def stats_norm(kind, tag, ph):
            raw = ph.enter_context(nc.sbuf_tensor(f"raw{tag}", [128, NTC * 12], F32))
            agg = ph.enter_context(nc.sbuf_tensor(f"agg{tag}", [128, NTC * 4], F32))
            mt = ph.enter_context(nc.sbuf_tensor(f"m{tag}", [128, NTC], F32))
            at = ph.enter_context(nc.sbuf_tensor(f"a{tag}", [128, NTC], F32))
            r2 = ph.enter_context(nc.sbuf_tensor(f"r2{tag}", [128, NTC], F32))
            tmp = ph.enter_context(nc.sbuf_tensor(f"tmq{tag}", [128, NTC * 4], F32))
            W.wait(V, LD, ld0)
            W.wait(V, DV)
            W.wait(V, PL)
            for tc in range(NTC):
                V.bn_stats(raw[:, tc * 12:tc * 12 + 6], xv[:, tc, 0:512])
                ins = V.bn_stats(raw[:, tc * 12 + 6:tc * 12 + 12], xv[:, tc, 512:1024])
                if tc == NTC - 1:
                    DV.inc(ins)
            W.wait(V, DV)
            for tc in range(NTC):
                V.bn_aggr(agg[:, tc * 4:tc * 4 + 2], raw[:, tc * 12:tc * 12 + 6])
                ins = V.bn_aggr(agg[:, tc * 4 + 2:tc * 4 + 4],
                                raw[:, tc * 12 + 6:tc * 12 + 12])
                if tc == NTC - 1:
                    DV.inc(ins)
            W.wait(V, DV)
            av = agg[:, :].rearrange("p (tc k) -> p tc k", k=4)
            m1, v1, m2, v2 = av[:, :, 0], av[:, :, 1], av[:, :, 2], av[:, :, 3]
            tv = tmp[:, :].rearrange("p (k tc) -> p k tc", k=4)
            t0, t1, t2, t3 = tv[:, 0, :], tv[:, 1, :], tv[:, 2, :], tv[:, 3, :]
            V.tensor_tensor(t0, m1, m2, ALU.add)
            V.tensor_tensor(t1, m1, m2, ALU.subtract)
            DV.inc(V.tensor_tensor(t2, v1, v2, ALU.add))
            W.wait(V, DV)
            V.tensor_scalar(mt[:, :], t0, 0.5, None, ALU.mult)
            DV.inc(V.tensor_tensor(t3, t1, t1, ALU.mult))
            W.wait(V, DV)
            DV.inc(V.scalar_tensor_tensor(t0, t3, 0.5, t2, ALU.mult, ALU.add))
            W.wait(V, DV)
            DV.inc(V.tensor_scalar(t1, t0, 0.5, EPS_LN, ALU.mult, ALU.add))
            W.wait(Sc, DV)
            AC.inc(Sc.activation(t2, t1, AF.Ln))
            W.wait(Sc, AC)
            AC.inc(Sc.activation(at[:, :], t2, AF.Exp, scale=-0.5))
            if kind == "ffn":
                W.wait(V, AC)
                V.tensor_scalar(t0, t1, -EPS_LN, None, ALU.add)
                DV.inc(V.tensor_tensor(t2, at[:, :], at[:, :], ALU.mult))
                W.wait(V, DV)
                DV.inc(V.tensor_tensor(t3, t0, t2, ALU.mult))
                W.wait(V, DV)
                DV.inc(V.tensor_scalar(t3, t3, EPS_RMS, None, ALU.add))
                W.wait(Sc, DV)
                AC.inc(Sc.activation(t0, t3, AF.Ln))
                W.wait(Sc, AC)
                AC.inc(Sc.activation(r2[:, :], t0, AF.Exp, scale=-0.5))
            return mt, at, r2

        def ffn(inv):
            tag = f"f{inv}"
            load_w3(SW1, w1e_b, w1eT[0], guard=(PE, PE.n))
            w1mark0 = load_w3(SW1, w1o_b, w1oT[0])
            W.wait(G, SW2[0])
            mk = sdma(SW2[0], G,
                      w2_b[0][:, :].rearrange("p (i n) -> p i n", n=512),
                      w2T[0, :, 0:512].rearrange("(i p) n -> p i n", p=128))
            w2mark0 = (mk, 0)
            with ExitStack() as ph:
                phase_barrier()
                mt, at, r2 = stats_norm("ffn", tag, ph)
                u_sb = [ph.enter_context(nc.sbuf_tensor(f"u{tag}{i}", [128, D], F32))
                        for i in range(2)]
                h1_sb = [ph.enter_context(nc.sbuf_tensor(f"h1{tag}{i}", [128, D], F32))
                         for i in range(2)]
                htm = [ph.enter_context(nc.sbuf_tensor(f"htm{tag}{i}", [128, D], BF16))
                       for i in range(2)]
                gbuf = [ph.enter_context(nc.sbuf_tensor(f"gb2{tag}{i}", [128, D], BF16))
                        for i in range(2)]
                hstage = [ph.enter_context(nc.sbuf_tensor(f"hs{tag}{i}", [128, NDC * 128], BF16))
                          for i in range(2)]
                lg_sb = ph.enter_context(nc.sbuf_tensor(f"lg{tag}", [128, 8], F32))
                top8 = ph.enter_context(nc.sbuf_tensor(f"top{tag}", [128, 8], F32))
                gtmp = ph.enter_context(nc.sbuf_tensor(f"gq{tag}", [128, 24], F32))
                g_tm = ph.enter_context(nc.sbuf_tensor(f"gtm{tag}", [128, 8], F32))
                p_tr = [ph.enter_context(nc.psum_tensor(f"ptr{tag}{i}", [128, 128], BF16))
                        for i in range(2)]
                p_lg = ph.enter_context(nc.psum_tensor(f"plg{tag}", [128, 8], F32))
                p_gt = ph.enter_context(nc.psum_tensor(f"pgt{tag}", [8, 128], F32))

                W.wait(Sc, AC)
                sc_hd = [0, 0]
                dv_tr = [0, 0]
                shd_mark = {}
                pl_h = {}

                def emit_norm(tc):
                    i = tc % 2
                    W.wait(V, AC)
                    W.wait(V, DV, dv_tr[0])
                    W.wait(V, PL)
                    DV.inc(V.tensor_scalar(u_sb[i][:, :], xv[:, tc, :], mt[:, tc:tc + 1],
                                           at[:, tc:tc + 1], ALU.subtract, ALU.mult))
                    W.wait(V, DV)
                    DV.inc(V.tensor_scalar(h1_sb[i][:, :], u_sb[i][:, :],
                                           r2[:, tc:tc + 1], None, ALU.mult))
                    dv_h1 = DV.n
                    W.wait(G, DV, dv_h1)
                    PL.inc(G.tensor_tensor(htm[i][:, :], h1_sb[i][:, :], nw_bc[:, :],
                                           ALU.mult))
                    pl_h[tc] = PL.n
                    DV.inc(V.scalar_tensor_tensor(xv[:, tc, :], u_sb[i][:, :], 0.5,
                                                  xv[:, tc, :], ALU.mult, ALU.add))

                emit_norm(0)
                for tc in range(NTC):
                    if tc + 1 < NTC:
                        emit_norm(tc + 1)
                    i = tc % 2
                    hs = hstage[i]
                    W.wait(T, PL, pl_h[tc])
                    W.wait(T, WGC, wg0)
                    for dc in range(NDC):
                        pt = p_tr[dc % 2]
                        if dc % 2 == 0:
                            W.wait(T, DV, dv_tr[0])
                        else:
                            W.wait(T, AC, dv_tr[1])
                        PE.inc(T.transpose(pt[:, :], htm[i][:, dc * 128:(dc + 1) * 128],
                                           idb_sb[:, :]))
                        if dc % 2 == 0:
                            W.wait(V, PE)
                            W.wait(V, SHD, sc_hd[i])
                            DV.inc(V.tensor_copy(hs[:, dc * 128:(dc + 1) * 128], pt[:, :]))
                            dv_tr[0] = DV.n
                        else:
                            W.wait(Sc, PE)
                            W.wait(Sc, SHD, sc_hd[i])
                            AC.inc(Sc.copy(hs[:, dc * 128:(dc + 1) * 128], pt[:, :]))
                            dv_tr[1] = AC.n
                    dv_h = DV.n
                    ac_h = AC.n
                    W.wait(Sc, DV, dv_h)
                    W.wait(Sc, AC, ac_h)
                    sc_hd[i] = sdma(
                        SHD, Sc,
                        h_d[:, :, tc * 128:(tc + 1) * 128].rearrange("dc p f -> p dc f"),
                        hs[:, :].rearrange("p (dc f) -> p dc f", f=128))
                    shd_mark[tc] = sc_hd[i]

                gdma = {}
                g_guard = [0, 0]

                def fetch_g(tc, bi):
                    W.wait(Sy, PE, g_guard[bi])
                    W.wait(Sy, SHD, shd_mark[tc])
                    gdma[tc] = (sdma(SH[bi], Sy,
                                     gbuf[bi][:, :].rearrange("p (dc f) -> p dc f", f=128),
                                     h_d[:, :, tc * 128:(tc + 1) * 128].rearrange("dc p f -> p dc f")),
                                bi)

                fetch_g(0, 0)
                for tc in range(NTC):
                    if tc + 1 < NTC:
                        fetch_g(tc + 1, (tc + 1) % 2)
                    scn_g, gi = gdma[tc]
                    W.wait(T, SH[gi], scn_g)
                    W.wait(T, AC)
                    W.wait(T, AC)
                    for dc in range(NDC):
                        T.matmul(p_lg[:, :], gbuf[gi][:, dc * 128:(dc + 1) * 128],
                                 gwT_sb[:, dc * NE:(dc + 1) * NE],
                                 start=(dc == 0), stop=False)
                    PE.inc(T.matmul(p_lg[:, :], ones_sb[0:1, 0:128], gb_sb[0:1, :],
                                    start=False, stop=True))
                    g_guard[gi] = PE.n
                    W.wait(Sc, PE)
                    W.wait(Sc, DV)
                    AC.inc(Sc.copy(lg_sb[:, :], p_lg[:, :]))
                    W.wait(V, AC)
                    DV.inc(V.max(top8[:, :], lg_sb[:, :]))
                    W.wait(V, DV)
                    DV.inc(V.tensor_tensor(gtmp[:, 0:1], top8[:, 0:1], top8[:, 1:2],
                                           ALU.subtract))
                    W.wait(Sc, DV)
                    AC.inc(Sc.activation(gtmp[:, 1:2], gtmp[:, 0:1], AF.Sigmoid))
                    W.wait(V, AC)
                    V.tensor_scalar(gtmp[:, 2:3], gtmp[:, 1:2], 0.5, None, ALU.mult)
                    DV.inc(V.tensor_scalar(gtmp[:, 3:4], gtmp[:, 1:2], -0.5, 0.5,
                                           ALU.mult, ALU.add))
                    V.tensor_scalar(gtmp[:, 8:16], lg_sb[:, :], top8[:, 0:1], None,
                                    ALU.is_equal)
                    DV.inc(V.tensor_scalar(gtmp[:, 16:24], lg_sb[:, :], top8[:, 1:2],
                                           None, ALU.is_equal))
                    W.wait(V, DV)
                    V.tensor_scalar(gtmp[:, 8:16], gtmp[:, 8:16], gtmp[:, 2:3], None,
                                    ALU.mult)
                    DV.inc(V.tensor_scalar(gtmp[:, 16:24], gtmp[:, 16:24], gtmp[:, 3:4],
                                           None, ALU.mult))
                    W.wait(V, DV)
                    DV.inc(V.tensor_tensor(g_tm[:, :], gtmp[:, 8:16], gtmp[:, 16:24],
                                           ALU.add))
                    W.wait(T, DV)
                    PE.inc(T.transpose(p_gt[:, :], g_tm[:, :], idf_sb[:, :]))
                    W.wait(Sc, PE)
                    AC.inc(Sc.copy(g_fm[:, tc * 128:(tc + 1) * 128], p_gt[:, :]))

            with ExitStack() as ph:
                phase_barrier()
                hbuf = [ph.enter_context(nc.sbuf_tensor(f"hb{tag}{i}", [128, NDC * 512], BF16))
                        for i in range(2)]
                acg = ph.enter_context(nc.sbuf_tensor(f"acg{tag}", [128, NDC * S], BF16))
                acgv = acg[:, :].rearrange("p (i t) -> p i t", i=NDC)
                v0 = [ph.enter_context(nc.sbuf_tensor(f"v0{tag}{i}", [128, 512], F32))
                      for i in range(2)]
                sl = [ph.enter_context(nc.sbuf_tensor(f"sl{tag}{i}", [128, 512], F32))
                      for i in range(2)]
                l1 = [ph.enter_context(nc.sbuf_tensor(f"l1{tag}{i}", [128, 512], F32))
                      for i in range(2)]
                l2s = ph.enter_context(nc.sbuf_tensor(f"l2{tag}", [128, 512], F32))
                l2 = [l2s, l2s]
                pp = [ph.enter_context(nc.sbuf_tensor(f"pq{tag}{i}", [128, 512], F32))
                      for i in range(2)]
                gbc = ph.enter_context(nc.sbuf_tensor(f"gbc{tag}", [128, 512], F32))
                p_t1e = [ph.enter_context(nc.psum_tensor(f"pt1e{tag}{i}", [128, 512], F32))
                         for i in range(2)]
                p_t1o = [ph.enter_context(nc.psum_tensor(f"pt1o{tag}{i}", [128, 512], F32))
                         for i in range(2)]
                p_gb = ph.enter_context(nc.psum_tensor(f"pgb{tag}", [128, 512], F32))
                p_m2 = [ph.enter_context(nc.psum_tensor(f"pm2{tag}{i}", [128, 512], F32))
                        for i in range(2)]

                w1mark = {}
                w2mark = {}
                w1_guard = [(PE, PE.n)]
                w2_guard = [(PE, PE.n), (PE, PE.n)]
                h_guard = [(PE, PE.n), (PE, PE.n)]

                def fetch_w1(e):
                    load_w3(SW1, w1e_b, w1eT[e], guard=w1_guard[0])
                    w1mark[e] = load_w3(SW1, w1o_b, w1oT[e])

                def fetch_w2(e, hlf, bi):
                    W.wait(G, *w2_guard[bi])
                    mk = sdma(SW2[bi], G,
                              w2_b[bi][:, :].rearrange("p (i n) -> p i n", n=512),
                              w2T[e, :, hlf * 512:(hlf + 1) * 512].rearrange(
                                  "(i p) n -> p i n", p=128))
                    w2mark[(e, hlf)] = (mk, bi)

                hdma = {}

                def fetch_h(e, tt, bi):
                    W.wait(Sy, *h_guard[bi])
                    W.wait(Sy, SHD)
                    mk = sdma(SH[bi], Sy,
                              hbuf[bi][:, :].rearrange("p (dc f) -> p dc f", f=512),
                              h_d[:, :, tt * 512:(tt + 1) * 512].rearrange("dc p f -> p dc f"))
                    hdma[(e, tt)] = (mk, bi)

                w1mark[0] = w1mark0
                w2mark[(0, 0)] = w2mark0
                fetch_h(0, 0, 0)
                dv_t1e = [0, 0]
                dv_t1o = [0, 0]
                ac_sl = [0, 0]
                dv_pp = [0, 0]
                pl_l2 = [0, 0]
                pl_acg = [0, 0]
                dv_gb_used = 0
                ac_gfm = AC.n
                for e in range(NE):
                    for tt in range(NQT):
                        if tt + 1 < NQT:
                            fetch_h(e, tt + 1, (tt + 1) % 2)
                        scn, hb = hdma[(e, tt)]
                        W.wait(T, AC, ac_gfm)
                        W.wait(T, AC)
                        PE.inc(T.matmul(p_gb[:, :], oh_sb[:, e * 128:(e + 1) * 128],
                                        g_fm[:, tt * 512:(tt + 1) * 512],
                                        start=True, stop=True))
                        pe_gb = PE.n
                        W.wait(Sc, PE, pe_gb)
                        W.wait(Sc, PL)
                        AC.inc(Sc.copy(gbc[:, :], p_gb[:, :]))
                        ac_gbc = AC.n
                        for ic in range(NDC):
                            bk = ic % 2
                            W.wait(T, SH[hb], scn)
                            W.wait(T, SW1, w1mark[e])
                            W.wait(T, DV, dv_t1e[bk])
                            for dc in range(NDC):
                                ins = T.matmul(p_t1e[bk][:, :],
                                               w1e_b[:, dc * D + ic * 128: dc * D + (ic + 1) * 128],
                                               hbuf[hb][:, dc * 512:(dc + 1) * 512],
                                               start=(dc == 0), stop=(dc == NDC - 1))
                            pe_e = PE.inc(ins)
                            W.wait(T, DV, dv_t1o[bk])
                            for dc in range(NDC):
                                ins = T.matmul(p_t1o[bk][:, :],
                                               w1o_b[:, dc * D + ic * 128: dc * D + (ic + 1) * 128],
                                               hbuf[hb][:, dc * 512:(dc + 1) * 512],
                                               start=(dc == 0), stop=(dc == NDC - 1))
                            pe_o = PE.inc(ins)
                            b1ec = b1e_sb[:, ic * 8 + e: ic * 8 + e + 1]
                            b1oc = b1o_sb[:, ic * 8 + e: ic * 8 + e + 1]
                            W.wait(V, PE, pe_e)
                            W.wait(V, AC, ac_sl[bk])
                            DV.inc(V.tensor_scalar(v0[bk][:, :], p_t1e[bk][:, :],
                                                   b1ec, LIMIT, ALU.add, ALU.min))
                            dv_v0 = dv_t1e[bk] = DV.n
                            W.wait(V, PE, pe_o)
                            DV.inc(V.tensor_scalar(l1[bk][:, :], p_t1o[bk][:, :],
                                                   b1oc, LIMIT, ALU.add, ALU.min))
                            dv_l1 = dv_t1o[bk] = DV.n
                            W.wait(Sc, DV, dv_v0)
                            W.wait(Sc, DV, dv_pp[bk])
                            AC.inc(Sc.activation(sl[bk][:, :], v0[bk][:, :],
                                                 AF.Silu, scale=ALPHA))
                            ac_sl[bk] = AC.n
                            W.wait(V, DV, dv_l1)
                            l2w = V.tensor_scalar(l2[bk][:, :], l1[bk][:, :],
                                                  -LIMIT, None, ALU.max)
                            pl_l2[bk] = DV.inc(l2w)
                            W.wait(V, DV, pl_l2[bk])
                            W.wait(V, AC, ac_sl[bk])
                            W.wait(V, PL, pl_acg[bk])
                            DV.inc(V.scalar_tensor_tensor(pp[bk][:, :], l2[bk][:, :],
                                                          1.0, sl[bk][:, :],
                                                          ALU.add, ALU.mult))
                            dv_pp[bk] = DV.n
                            W.wait(G, DV, dv_pp[bk])
                            W.wait(G, AC, ac_gbc)
                            PL.inc(G.tensor_tensor(acgv[:, ic, tt * 512:(tt + 1) * 512],
                                                   pp[bk][:, :], gbc[:, :], ALU.mult))
                            pl_acg[bk] = PL.n
                        h_guard[hb] = (PE, PE.n)
                    w1_guard[0] = (PE, PE.n)
                    fetch_w2(e, 1, 1 - w2mark[(e, 0)][1])
                    for hlf in range(2):
                        wgn, wb = w2mark[(e, hlf)]
                        if hlf == 1 and e + 1 < NE:
                            fetch_w1(e + 1)
                            fetch_w2(e + 1, 0, 1 - wb)
                            fetch_h(e + 1, 0, 0)
                        dv_m2 = [DV.n, DV.n]
                        W.wait(T, PL)
                        for tc in range(NTC):
                            pm = p_m2[tc % 2]
                            W.wait(T, SW2[wb], wgn)
                            W.wait(T, DV, dv_m2[tc % 2])
                            for ic in range(NDC):
                                ins = T.matmul(pm[:, :],
                                               acgv[:, ic, tc * 128:(tc + 1) * 128],
                                               w2_b[wb][:, ic * 512:(ic + 1) * 512],
                                               start=(ic == 0),
                                               stop=(ic == NDC - 1 and e != 0))
                            if e == 0:
                                ins = T.matmul(pm[:, :], g_fm[:, tc * 128:(tc + 1) * 128],
                                               b2_sb[:, hlf * 512:(hlf + 1) * 512],
                                               start=False, stop=True)
                            PE.inc(ins)
                            W.wait(V, PE)
                            DV.inc(V.tensor_tensor(xv[:, tc, hlf * 512:(hlf + 1) * 512],
                                                   pm[:, :],
                                                   xv[:, tc, hlf * 512:(hlf + 1) * 512],
                                                   ALU.add))
                            dv_m2[tc % 2] = DV.n
                            if inv == 1 and e == NE - 1 and hlf == 1:
                                W.wait(Sy, DV, dv_m2[tc % 2])
                                ST.dma(Sy.dma_start(
                                    out.rearrange("(tc p) d -> p tc d", p=128)[:, tc, :],
                                    xv[:, tc, :]))
                        w2_guard[wb] = (PE, PE.n)

        def attention():
            with ExitStack() as ph:
                phase_barrier()
                mt, at, _ = stats_norm("attn", "a0", ph)
                utm = [ph.enter_context(nc.sbuf_tensor(f"utm{i}", [128, D], BF16))
                       for i in range(2)]
                astage = [ph.enter_context(nc.sbuf_tensor(f"astage{i}", [128, NDC * 128], BF16))
                          for i in range(2)]
                p_tr = [ph.enter_context(nc.psum_tensor(f"patr{i}", [128, 128], BF16))
                        for i in range(2)]
                W.wait(Sc, AC)
                sc_ad = [0, 0]
                dv_tr = [0, 0]
                dv_u = {}

                def emit_u(tc):
                    W.wait(V, AC)
                    W.wait(V, DV, dv_tr[0])
                    DV.inc(V.tensor_scalar(utm[tc % 2][:, :], xv[:, tc, :],
                                           mt[:, tc:tc + 1], at[:, tc:tc + 1],
                                           ALU.subtract, ALU.mult))
                    dv_u[tc] = DV.n

                emit_u(0)
                for tc in range(NTC):
                    if tc + 1 < NTC:
                        emit_u(tc + 1)
                    ast = astage[tc % 2]
                    W.wait(T, DV, dv_u[tc])
                    W.wait(T, WGC, wg0)
                    for dc in range(NDC):
                        pt = p_tr[dc % 2]
                        if dc % 2 == 0:
                            W.wait(T, DV, dv_tr[0])
                        else:
                            W.wait(T, AC, dv_tr[1])
                        PE.inc(T.transpose(pt[:, :], utm[tc % 2][:, dc * 128:(dc + 1) * 128],
                                           idb_sb[:, :]))
                        if dc % 2 == 0:
                            W.wait(V, PE)
                            W.wait(V, SAD, sc_ad[tc % 2])
                            DV.inc(V.tensor_copy(ast[:, dc * 128:(dc + 1) * 128], pt[:, :]))
                            dv_tr[0] = DV.n
                        else:
                            W.wait(Sc, PE)
                            W.wait(Sc, SAD, sc_ad[tc % 2])
                            AC.inc(Sc.copy(ast[:, dc * 128:(dc + 1) * 128], pt[:, :]))
                            dv_tr[1] = AC.n
                    W.wait(Sy, DV)
                    W.wait(Sy, AC)
                    sc_ad[tc % 2] = sdma(
                        SAD, Sy,
                        attin_d[:, :, tc * 128:(tc + 1) * 128].rearrange("dc p f -> p dc f"),
                        ast[:, :].rearrange("p (dc f) -> p dc f", f=128))

            wqe2 = ar[:, 16384:20480]
            wqo2 = ar[:, 20480:24576]
            load_w3(SATW, wqe_b, wkeT, guard=(PE, PE.n))
            load_w3(SATW, wqo_b, wkoT)
            load_w3(SATW, wv_b, wvT)
            load_w3(SATW, wqe2, wqeT)
            wk_mark = load_w3(SATW, wqo2, wqoT)

            with ExitStack() as ph:
                phase_barrier()
                abuf = [ph.enter_context(nc.sbuf_tensor(f"aab{i}", [128, NDC * 512], BF16))
                        for i in range(2)]
                cosb = ph.enter_context(nc.sbuf_tensor("cosbA", [128, 4 * 512], BF16))
                sinb = ph.enter_context(nc.sbuf_tensor("sinbA", [128, 4 * 512], BF16))
                ke_sb = [ph.enter_context(nc.sbuf_tensor(f"keS{i}", [128, 512], BF16))
                         for i in range(2)]
                ko_sb = [ph.enter_context(nc.sbuf_tensor(f"koS{i}", [128, 512], BF16))
                         for i in range(2)]
                t1 = ph.enter_context(nc.sbuf_tensor("rt1A", [128, 512], BF16))
                t2 = ph.enter_context(nc.sbuf_tensor("rt2A", [128, 512], BF16))
                t3 = ph.enter_context(nc.sbuf_tensor("rt3A", [128, 512], BF16))
                t4 = ph.enter_context(nc.sbuf_tensor("rt4A", [128, 512], BF16))
                kst = [ph.enter_context(nc.sbuf_tensor(f"kst{i}", [128, 512], BF16))
                       for i in range(4)]
                sq2 = [ph.enter_context(nc.sbuf_tensor(f"sqA{i}", [128, 512], BF16))
                       for i in range(2)]
                knacc = ph.enter_context(nc.sbuf_tensor("knacc", [128, 512], F32))
                qnacc = ph.enter_context(nc.sbuf_tensor("qnaccA", [128, 512], F32))
                kn1 = ph.enter_context(nc.sbuf_tensor("kn1", [128, 1], F32))
                vsb = [ph.enter_context(nc.sbuf_tensor(f"vsb{i}", [128, 1040], BF16))
                       for i in range(2)]
                p_ke = [ph.enter_context(nc.psum_tensor(f"pke{i}", [128, 512], F32))
                        for i in range(2)]
                p_ko = [ph.enter_context(nc.psum_tensor(f"pko{i}", [128, 512], F32))
                        for i in range(2)]
                p_kn = [ph.enter_context(nc.psum_tensor(f"pkn{i}", [128, 512], F32))
                        for i in range(2)]
                p_v = [ph.enter_context(nc.psum_tensor(f"pv2{i}", [128, 512], F32))
                       for i in range(2)]
                p_knt = p_kn[0][0:1, 0:128]

                DV.inc(V.memset(knacc[:, :], 0.0))
                DV.inc(V.memset(qnacc[:, :], 0.0))
                for i in range(2):
                    DV.inc(V.memset(
                        vsb[i][:, :].rearrange("p (h f) -> p h f", f=65)[:, :, 64:65], 1.0))

                adma = {}
                a_guard = [(PE, 0), (PE, 0)]

                def fetch_attin(st, bi):
                    W.wait(Sy, *a_guard[bi])
                    W.wait(Sy, SAD)
                    adma[st] = (sdma(SA[bi], Sy,
                                     abuf[bi][:, :].rearrange("p (dc f) -> p dc f", f=512),
                                     attin_d[:, :, st * 512:(st + 1) * 512].rearrange("dc p f -> p dc f")),
                                bi)

                tmark = [0]

                def fetch_tables(col, dv_guard):
                    W.wait(G, DV, dv_guard)
                    W.wait(G, STB)
                    STB.dma(G.dma_start(
                        cosb[:, :].rearrange("p (j f) -> p j f", f=512),
                        cosT[:, col * 512:(col + 1) * 512].rearrange("(j p) f -> p j f", p=128)))
                    STB.dma(G.dma_start(
                        sinb[:, :].rearrange("p (j f) -> p j f", f=512),
                        sinT[:, col * 512:(col + 1) * 512].rearrange("(j p) f -> p j f", p=128)))
                    tmark[0] = STB.n

                fetch_attin(0, 0)
                fetch_tables(0, 0)
                kst_guard = [0, 0, 0, 0]
                sc_vdma = [0, 0]
                dv_keko = [0, 0]
                dv_keko_rd = [0, 0]
                dv_kn = [0, 0]
                dv_v = [0, 0]
                pl_sq = [0, 0]
                for st in range(4):
                    scn, ab = adma[st]
                    if st + 1 < 4:
                        fetch_attin(st + 1, (st + 1) % 2)
                    dv_rope_st = 0
                    tasks = [(side, jc) for side in range(2) for jc in range(4)]
                    pe_proj = {}

                    def emit_proj(i):
                        side, jc = tasks[i]
                        web = wqe_b if side == 0 else wqe2
                        wob2 = wqo_b if side == 0 else wqo2
                        kbk = i % 2
                        W.wait(T, SA[ab], scn)
                        W.wait(T, SATW, wk_mark)
                        W.wait(T, AC, dv_keko[kbk])
                        for dc in range(NDC):
                            T.matmul(p_ke[kbk][:, :],
                                     web[:, dc * 512 + jc * 128: dc * 512 + (jc + 1) * 128],
                                     abuf[ab][:, dc * 512:(dc + 1) * 512],
                                     start=(dc == 0), stop=(dc == NDC - 1))
                        for dc in range(NDC):
                            ins = T.matmul(p_ko[kbk][:, :],
                                           wob2[:, dc * 512 + jc * 128: dc * 512 + (jc + 1) * 128],
                                           abuf[ab][:, dc * 512:(dc + 1) * 512],
                                           start=(dc == 0), stop=(dc == NDC - 1))
                        pe_proj[i] = PE.inc(ins)

                    emit_proj(0)
                    for i, (side, jc) in enumerate(tasks):
                        if i + 1 < len(tasks):
                            emit_proj(i + 1)
                        dest = kro_d if side == 0 else qro_d
                        nacc = knacc if side == 0 else qnacc
                        kbk = i % 2
                        co = cosb[:, jc * 512:(jc + 1) * 512]
                        si = sinb[:, jc * 512:(jc + 1) * 512]
                        klo = kst[2 * kbk]
                        khi = kst[2 * kbk + 1]
                        W.wait(Sc, PE, pe_proj[i])
                        W.wait(Sc, DV, dv_keko_rd[kbk])
                        AC.inc(Sc.copy(ke_sb[kbk][:, :], p_ke[kbk][:, :]))
                        AC.inc(Sc.copy(ko_sb[kbk][:, :], p_ko[kbk][:, :]))
                        dv_keko[kbk] = AC.n
                        W.wait(V, AC, dv_keko[kbk])
                        W.wait(V, STB, tmark[0])
                        W.wait(V, SKST[2 * kbk], kst_guard[2 * kbk])
                        W.wait(V, SKST[2 * kbk + 1], kst_guard[2 * kbk + 1])
                        V.tensor_tensor(t1[:, :], ke_sb[kbk][:, :], co, ALU.mult)
                        V.tensor_tensor(t2[:, :], ko_sb[kbk][:, :], si, ALU.mult)
                        V.tensor_tensor(t3[:, :], ke_sb[kbk][:, :], si, ALU.mult)
                        DV.inc(V.tensor_tensor(t4[:, :], ko_sb[kbk][:, :], co, ALU.mult))
                        W.wait(V, DV)
                        DV.inc(V.tensor_tensor(klo[:, :], t1[:, :], t2[:, :], ALU.subtract))
                        DV.inc(V.tensor_tensor(khi[:, :], t3[:, :], t4[:, :], ALU.add))
                        dv_rope_st = DV.n
                        dv_keko_rd[kbk] = DV.n
                        W.wait(Sc, DV, dv_rope_st)
                        kst_guard[2 * kbk] = sdma(
                            SKST[2 * kbk], Sc,
                            dest[jc, :, st * 512:(st + 1) * 512], klo[:, :])
                        kst_guard[2 * kbk + 1] = sdma(
                            SKST[2 * kbk + 1], Sc,
                            dest[4 + jc, :, st * 512:(st + 1) * 512], khi[:, :])
                        for ki, kt in ((0, klo), (1, khi)):
                            W.wait(G, DV, dv_rope_st)
                            W.wait(G, PE, pl_sq[ki])
                            PL.inc(G.tensor_tensor(sq2[ki][:, :], kt[:, :], kt[:, :],
                                                   ALU.mult))
                            pl_sq2 = PL.n
                            W.wait(T, PL, pl_sq2)
                            W.wait(T, WGC, wg0)
                            W.wait(T, DV, dv_kn[ki])
                            PE.inc(T.matmul(p_kn[ki][:, :], oAB_sb[:, :], sq2[ki][:, :],
                                            start=True, stop=True))
                            pl_sq[ki] = PE.n
                            W.wait(V, PE)
                            W.wait(V, DV, max(dv_kn))
                            DV.inc(V.tensor_tensor(nacc[:, :], p_kn[ki][:, :],
                                                   nacc[:, :], ALU.max))
                            dv_kn[ki] = DV.n
                    W.wait(V, DV)
                    DV.inc(V.reduce_max(kn1[:, :], qnacc[:, :], axis=AX.X))
                    W.wait(T, DV)
                    W.wait(T, DV, dv_kn[0])
                    PE.inc(T.transpose(p_knt, kn1[:, :], idf_sb[:, :]))
                    W.wait(V, PE)
                    DV.inc(V.reduce_max(knmax[:, 4 + st:5 + st], p_knt, axis=AX.X))
                    dv_kn[0] = DV.n
                    DV.inc(V.memset(qnacc[:, :], 0.0))
                    for sub in range(4):
                        stk128 = st * 4 + sub
                        vs = vsb[sub % 2]
                        for hlf in range(2):
                            vbk = hlf
                            W.wait(T, SA[ab], scn)
                            W.wait(T, DV, dv_v[vbk])
                            for dc in range(NDC):
                                ins = T.matmul(p_v[vbk][:, :],
                                               abuf[ab][:, dc * 512 + sub * 128: dc * 512 + (sub + 1) * 128],
                                               wv_b[:, dc * D + hlf * 512: dc * D + (hlf + 1) * 512],
                                               start=(dc == 0), stop=(dc == NDC - 1))
                            PE.inc(ins)
                            W.wait(V, PE)
                            W.wait(V, SVS[sub % 2], sc_vdma[sub % 2])
                            dst = vs[:, :].rearrange("p (h f) -> p h f", f=65)[
                                :, hlf * 8:(hlf + 1) * 8, 0:64]
                            src = p_v[vbk][:, :].rearrange("p (h f) -> p h f", f=64)
                            DV.inc(V.tensor_copy(dst, src))
                            dv_v[vbk] = DV.n
                        W.wait(Sc, DV)
                        sc_vdma[sub % 2] = sdma(
                            SVS[sub % 2], Sc,
                            v_d[:, stk128, :, :].rearrange("h p f -> p h f"),
                            vs[:, :].rearrange("p (h f) -> p h f", f=65))
                    if st + 1 < 4:
                        fetch_tables(st + 1, dv_rope_st)
                    a_guard[ab] = (PE, PE.n)
                W.wait(V, DV)
                DV.inc(V.reduce_max(kn1[:, :], knacc[:, :], axis=AX.X))
                W.wait(T, DV)
                W.wait(T, DV, dv_kn[0])
                PE.inc(T.transpose(p_knt, kn1[:, :], idf_sb[:, :]))
                W.wait(V, PE)
                DV.inc(V.reduce_max(knmax[:, 0:1], p_knt, axis=AX.X))

            wq_mark = load_w3(SATW, wo_t, woT, guard=(PE, PE.n))
            qbuf = [ar[:, 0:4096], ar[:, 4096:8192]]

            with ExitStack() as ph:
                phase_barrier()
                csc = ph.enter_context(nc.sbuf_tensor("csc", [1, 4], F32))
                bias_sb = ph.enter_context(nc.sbuf_tensor("bias_sb", [128, 1], F32))
                vbuf = [ph.enter_context(nc.sbuf_tensor(f"vbq{i}", [128, 2 * 16 * 65], BF16))
                        for i in range(2)]
                kbuf = kbuf_v
                Ebuf = Ebuf_v
                rz = [ph.enter_context(nc.sbuf_tensor(f"rz{i}", [1, 512], BF16))
                      for i in range(2)]
                zb = [ph.enter_context(nc.sbuf_tensor(f"zb{i}", [64, 512], F32))
                      for i in range(2)]
                O_sb = ph.enter_context(nc.sbuf_tensor("O_sb", [128, NDC * 512], BF16))
                p_scw = [ph.enter_context(nc.psum_tensor(f"pscw{i}", [128, 1024], F32))
                         for i in range(2)]
                p_pv = [ph.enter_context(nc.psum_tensor(f"ppv{i}", [65, 512], F32))
                        for i in range(2)]
                p_z = ph.enter_context(nc.psum_tensor("pz", [64, 512], F32))
                p_wo = ph.enter_context(nc.psum_tensor("pwo", [128, 512], F32))
                p_b = p_wo[:, 0:1]
                dv_pv = {}
                ac_zb = {}
                dv_zb_used = {}

                def heads_of(c):
                    hA = 2 * c if c < 4 else 8 + 2 * (c - 4)
                    return hA, hA + 1

                kdma = {}
                k_guard = [(PE, 0), (PE, 0)]

                def fetch_pair(qt, c, bi):
                    W.wait(Sy, *k_guard[bi])
                    for s4 in SKST:
                        W.wait(Sy, s4)
                    for s2 in SVS:
                        W.wait(Sy, s2)
                    W.wait(Sy, SKV[bi])
                    hA, hB = heads_of(c)
                    SKV[bi].dma(Sy.dma_start(kbuf[bi][:, :], kro_d[c, :, :]))
                    SKV[bi].dma(Sy.dma_start(
                        vbuf[bi][:, :].rearrange("p (hh k f) -> p hh k f", hh=2, f=65),
                        v_d[hA:hA + 2, :, :, :].rearrange("hh k p f -> p hh k f")))
                    kdma[(qt, c)] = (SKV[bi].n, bi)

                qdma = {}
                q_guard = [(PE, 0), (PE, 0)]

                def fetch_q(qt, bi):
                    W.wait(Sy, *q_guard[bi])
                    for s4 in SKST:
                        W.wait(Sy, s4)
                    qdma[qt] = (sdma(SA[bi], Sy,
                                     qbuf[bi][:, :].rearrange("p (c f) -> p c f", f=512),
                                     qro_d[:, :, qt * 512:(qt + 1) * 512].rearrange("c p f -> p c f")),
                                bi)

                fetch_q(0, 0)
                pe_wo_last = 0
                for qt in range(NQT):
                    scn_q, qb = qdma[qt]
                    if qt + 1 < NQT:
                        fetch_q(qt + 1, (qt + 1) % 2)
                    W.wait(V, AC)
                    DV.inc(V.tensor_tensor(csc[:, 1:2], knmax[:, 4 + qt:5 + qt],
                                           knmax[:, 0:1], ALU.add))
                    W.wait(V, DV)
                    DV.inc(V.tensor_scalar(csc[:, 2:3], csc[:, 1:2], -1.02 / 16.0, None,
                                           ALU.mult))
                    W.wait(T, DV)
                    W.wait(T, DV, dv_zb_used.get(0, 0))
                    PE.inc(T.matmul(p_b, ones_sb[0:1, 0:128], csc[0:1, 2:3],
                                    start=True, stop=True))
                    W.wait(Sc, PE)
                    AC.inc(Sc.copy(bias_sb[:, :], p_b))
                    W.wait(Sc, AC)

                    fetch_pair(qt, 0, 0)
                    prev_exp = {}
                    pending = [None]

                    def emit_normalize():
                        if pending[0] is None:
                            return
                        cN, pvm = pending[0]
                        pending[0] = None
                        for hh in range(2):
                            pvb = p_pv[hh]
                            W.wait(V, PE, pvm[(hh, 15)])
                            W.wait(V, AC, ac_zb.get(hh, 0))
                            with nc.allow_low_precision(
                                    reason="softmax normalizer 1/Z to bf16 — same "
                                           "class as the bf16 attention operands"):
                                DV.inc(V.reciprocal(rz[hh][:, :], pvb[64:65, :]))
                            dv_rz = DV.n
                            W.wait(T, DV, dv_rz)
                            W.wait(T, AC, ac_zb.get(1 - hh, 0))
                            PE.inc(T.matmul(p_z[:, :], ones_bf[0:1, 0:64],
                                            rz[hh][0:1, :], start=True, stop=True))
                            W.wait(Sc, PE)
                            W.wait(Sc, DV, dv_zb_used.get(hh, 0))
                            AC.inc(Sc.copy(zb[hh][:, :], p_z[:, :]))
                            ac_zb[hh] = AC.n
                            W.wait(V, AC, ac_zb[hh])
                            dst = O_sb[64 * hh:64 * hh + 64, cN * 512:(cN + 1) * 512]
                            DV.inc(V.tensor_tensor(dst, pvb[0:64, :], zb[hh][:, :],
                                                   ALU.mult))
                            dv_pv[hh] = DV.n
                            dv_zb_used[hh] = DV.n

                    for c in range(8):
                        if c + 1 < 8:
                            fetch_pair(qt, c + 1, (c + 1) % 2)
                        scn_k, kb = kdma[(qt, c)]
                        hA, hB = heads_of(c)
                        qhh = [qbuf[qb][0:64, c * 512:(c + 1) * 512],
                               qbuf[qb][64:128, c * 512:(c + 1) * 512]]
                        khh = [kbuf[kb][0:64, :], kbuf[kb][64:128, :]]
                        ac_exp = {}
                        pe_sc_mark = {}
                        pe_pv_mark = {}

                        def emit_sc(hh, kp):
                            bank = (2 * kp + hh) % 2
                            W.wait(T, SKV[kb], scn_k)
                            W.wait(T, SA[qb], scn_q)
                            if kp >= 1:
                                W.wait(T, AC, ac_exp.get((hh, kp - 1),
                                                         prev_exp.get((hh, 7), 0)))
                            else:
                                W.wait(T, AC, prev_exp.get((hh, 7), 0))
                            for half in range(2):
                                kc = 2 * kp + half
                                ins = T.matmul(p_scw[bank][:, half * 512:(half + 1) * 512],
                                               khh[hh][:, kc * 128:(kc + 1) * 128],
                                               qhh[hh], start=True, stop=True)
                            PE.inc(ins)
                            pe_sc_mark[(hh, kp)] = PE.n

                        def emit_exp(hh, kp):
                            slot = (2 * kp + hh) % 4
                            W.wait(Sc, PE, pe_sc_mark[(hh, kp)])
                            W.wait(Sc, PE, pe_pv_mark.get((hh, 2 * kp - 4),
                                                          prev_exp.get("pvdone", 0)))
                            AC.inc(Sc.activation(Ebuf[:, slot * 1024:(slot + 1) * 1024],
                                                 p_scw[(2 * kp + hh) % 2][:, :],
                                                 AF.Exp, bias=bias_sb[:, :], scale=0.125))
                            ac_exp[(hh, kp)] = AC.n

                        def emit_pv(hh, kc):
                            kp = kc // 2
                            slot = (2 * kp + hh) % 4
                            if kc == 0:
                                W.wait(T, DV, dv_pv.get(hh, 0))
                            W.wait(T, AC, ac_exp[(hh, kp)])
                            PE.inc(T.matmul(p_pv[hh][:, :],
                                            vbuf[kb][:, (hh * 16 + kc) * 65:
                                                     (hh * 16 + kc + 1) * 65],
                                            Ebuf[:, slot * 1024 + (kc % 2) * 512:
                                                 slot * 1024 + (kc % 2) * 512 + 512],
                                            start=(kc == 0), stop=(kc == 15)))
                            pe_pv_mark[(hh, kc)] = PE.n

                        for hh in range(2):
                            emit_sc(hh, 0)
                            emit_exp(hh, 0)
                        emit_normalize()
                        for kp in range(8):
                            if kp + 1 < 8:
                                for hh in range(2):
                                    emit_sc(hh, kp + 1)
                                    emit_exp(hh, kp + 1)
                            for half in range(2):
                                emit_pv(0, 2 * kp + half)
                                emit_pv(1, 2 * kp + half)
                        prev_exp = dict(ac_exp)
                        prev_exp["pvdone"] = PE.n
                        k_guard[kb] = (PE, PE.n)
                        pending[0] = (c, dict(pe_pv_mark))
                        if c == 7:
                            emit_normalize()
                    q_guard[qb] = (PE, PE.n)
                    dv_O = DV.n
                    W.wait(T, AC)
                    wo_banks = [p_wo, p_scw[0][:, 0:512]]
                    dv_wo = [DV.n, DV.n]
                    for ts in range(4):
                        tc = qt * 4 + ts
                        for hlf in range(2):
                            i2 = (ts * 2 + hlf) % 2
                            pmw = wo_banks[i2]
                            W.wait(T, DV, dv_O)
                            W.wait(T, DV, dv_wo[i2])
                            W.wait(T, SATW, wq_mark)
                            for dc in range(NDC):
                                T.matmul(pmw[:, :],
                                         O_sb[:, dc * 512 + ts * 128: dc * 512 + (ts + 1) * 128],
                                         wo_t[:, dc * D + hlf * 512: dc * D + (hlf + 1) * 512],
                                         start=(dc == 0), stop=False)
                            PE.inc(T.matmul(pmw[:, :], ones_sb[0:1, 0:128],
                                            wob_sb[0:1, hlf * 512:(hlf + 1) * 512],
                                            start=False, stop=True))
                            pe_wo_last = PE.n
                            W.wait(V, PE)
                            DV.inc(V.tensor_tensor(xv[:, tc, hlf * 512:(hlf + 1) * 512],
                                                   pmw[:, :],
                                                   xv[:, tc, hlf * 512:(hlf + 1) * 512],
                                                   ALU.add))
                            dv_wo[i2] = DV.n
                    dv_zb_used[0] = max(dv_zb_used.get(0, 0), dv_wo[0])

        ffn(0)
        if stage >= 2:
            attention()
        if stage >= 3:
            ffn(1)

        W.wait(Sy, DV)
        W.wait(Sy, AC)
        W.wait(Sy, PE)
        Sy.wait_ge(ST.h, ST.n)
        assert len(ALL_SEMS) + 2 < 106
    return nc


_CONST_CACHE = {}


def _consts():
    if _CONST_CACHE:
        return _CONST_CACHE
    pos = np.arange(S, dtype=np.float32)[:, None]
    freqs = np.exp(np.arange(0, D, 2, dtype=np.float32) *
                   np.float32(-math.log(10000.0) / D))
    ang = pos * freqs[None, :]
    _CONST_CACHE["cosT"] = np.ascontiguousarray(np.cos(ang).astype(np.float32).T)
    _CONST_CACHE["sinT"] = np.ascontiguousarray(np.sin(ang).astype(np.float32).T)
    _CONST_CACHE["idf"] = np.eye(128, dtype=np.float32)
    oab = np.zeros((128, 128), np.float32)
    oab[0:64, 0:64] = 1.0
    oab[64:128, 64:128] = 1.0
    _CONST_CACHE["onesAB"] = oab
    _CONST_CACHE["onesr"] = np.ones((1, 512), np.float32)
    oh = np.zeros((NE, NE, 128), np.float32)
    for e in range(NE):
        oh[e, e, :] = 1.0 / ALPHA
    _CONST_CACHE["ohdiv"] = oh
    return _CONST_CACHE


def _prep(inputs):
    c = _consts()
    wq = np.asarray(inputs["L0_attn_wq"], np.float32)
    wk = np.asarray(inputs["L0_attn_wk"], np.float32)
    wv = np.asarray(inputs["L0_attn_wv"], np.float32)
    wo = np.asarray(inputs["L0_attn_wo_w"], np.float32)
    m1w = np.asarray(inputs["L0_ffn_mlp1_w"], np.float32)
    m1b = np.asarray(inputs["L0_ffn_mlp1_b"], np.float32)
    m2w = np.asarray(inputs["L0_ffn_mlp2_w"], np.float32)
    shared = {
        "wqeT": np.ascontiguousarray(wq[0::2, :].T),
        "wqoT": np.ascontiguousarray(wq[1::2, :].T),
        "wkeT": np.ascontiguousarray(wk[0::2, :].T),
        "wkoT": np.ascontiguousarray(wk[1::2, :].T),
        "wvT": np.ascontiguousarray(wv.T),
        "woT": np.ascontiguousarray(wo.T),
        "wob": np.asarray(inputs["L0_attn_wo_b"], np.float32).reshape(1, D),
        "cosT": c["cosT"], "sinT": c["sinT"],
        "nwr": np.asarray(inputs["L0_ffn_norm_w"], np.float32).reshape(1, D),
        "gwT": np.ascontiguousarray(np.asarray(inputs["L0_ffn_gate_w"], np.float32).T),
        "gbr": np.asarray(inputs["L0_ffn_gate_b"], np.float32).reshape(1, NE),
        "w1eT": np.ascontiguousarray(m1w[:, 0::2, :].transpose(0, 2, 1)),
        "w1oT": np.ascontiguousarray(m1w[:, 1::2, :].transpose(0, 2, 1)),
        "w2T": np.ascontiguousarray(m2w.transpose(0, 2, 1)),
        "b1e3": np.ascontiguousarray(m1b[:, 0::2].reshape(NE, 8, 128).transpose(1, 2, 0)),
        "b1o3": np.ascontiguousarray(m1b[:, 1::2].reshape(NE, 8, 128).transpose(1, 2, 0)),
        "b2n": np.asarray(inputs["L0_ffn_mlp2_b"], np.float32),
        "idf": c["idf"], "onesAB": c["onesAB"], "onesr": c["onesr"],
        "ohdiv": c["ohdiv"],
    }
    x = np.asarray(inputs["L0_attn_x"], np.float32)
    maps = []
    for b in range(8):
        m = dict(shared)
        m["x"] = np.ascontiguousarray(x[b])
        maps.append(m)
    return maps


def kernel(**inputs):
    maps = _prep(inputs)
    nc = build_nc(stage=3)
    res = run_bass_kernel_spmd(nc, maps, core_ids=list(range(8)))
    return np.stack([np.asarray(r["out"], np.float32) for r in res.results], axis=0)
